# version 08
# baseline (speedup 1.0000x reference)
import numpy as np
import ml_dtypes
from contextlib import ExitStack
import concourse.bacc as bacc
import concourse.mybir as mybir
from concourse.bass_utils import run_bass_kernel_spmd

F32 = mybir.dt.float32
BF16 = mybir.dt.bfloat16
AF = mybir.ActivationFunctionType
ALU = mybir.AluOpType
AX = mybir.AxisListType

NCORES = 8
B, S, D = 16, 2048, 1024
SPC = B // NCORES
T = SPC * S
NH, DH = 16, 64
EPS_RMS = 1e-6
EPS_LN = 1e-5

BF16NP = ml_dtypes.bfloat16


def _prep_weights(inp):
    w = {}
    f32 = lambda a: np.ascontiguousarray(np.asarray(a, dtype=np.float32))
    w["l0_wq"] = f32(inp["L0_attn_wq"].reshape(D, D))
    w["l0_wk"] = f32(inp["L0_attn_wk"].reshape(D, D))
    w["l0_wv"] = f32(inp["L0_attn_wv"].reshape(D, D))
    w["l0_wo"] = f32(inp["L0_attn_w_out"].reshape(D, D))
    w["l0_bo"] = f32(inp["L0_attn_b_out"].reshape(1, D))
    w["l0_gam"] = f32(np.asarray(inp["L0_attn_norm_scale"]).reshape(8, 128).T)
    w["l0_rw"] = f32(inp["L0_ffn_router_w"])
    w["l0_rb"] = f32(inp["L0_ffn_router_b"].reshape(1, 8))
    w["l0_sw1"] = f32(inp["L0_ffn_sw1"])
    w["l0_sb1"] = f32(inp["L0_ffn_sb1"].reshape(16, 128).T)
    w["l0_sw2"] = f32(inp["L0_ffn_sw2"])
    w["l0_sb2"] = f32(inp["L0_ffn_sb2"].reshape(1, D))
    w["l0_ew1"] = f32(inp["L0_ffn_rw1"])
    w["l0_eb1"] = f32(np.asarray(inp["L0_ffn_rb1"]).reshape(8, 16, 128).transpose(0, 2, 1))
    w["l0_ew2"] = f32(inp["L0_ffn_rw2"])
    w["l0_eb2"] = f32(np.asarray(inp["L0_ffn_rb2"]).reshape(8, 1, D))
    qkv = np.asarray(inp["L1_attn_Wqkv"]).reshape(NH, 3, DH, D)
    w["l1_wq"] = f32(qkv[:, 0].transpose(2, 0, 1).reshape(D, D))
    w["l1_wk"] = f32(qkv[:, 1].transpose(2, 0, 1).reshape(D, D))
    w["l1_wv"] = f32(qkv[:, 2].transpose(2, 0, 1).reshape(D, D))
    w["l1_wo"] = f32(np.asarray(inp["L1_attn_Wproj"]).T)
    w["l1_bo"] = f32(inp["L1_attn_bproj"].reshape(1, D))
    w["l1_gam"] = f32(np.ones((128, 8), np.float32))
    w["l1_rw"] = f32(np.asarray(inp["L1_ffn_router_w"]).T)
    w["l1_ew1"] = f32(np.asarray(inp["L1_ffn_w1"]).transpose(0, 2, 1))
    w["l1_ew2"] = f32(np.asarray(inp["L1_ffn_w2"]).transpose(0, 2, 1))
    w["ident"] = np.eye(128, dtype=np.float32)
    w["ones1"] = np.ones((1, 512), np.float32)
    w["mask"] = np.ascontiguousarray(np.tile(np.kron(np.eye(8, dtype=np.float32), np.ones((16, 16), np.float32)), (1, 4)))
    return w



_NAME_I = [0]


def nm(base):
    _NAME_I[0] += 1
    return f"{base}_{_NAME_I[0]}"


def cv(ap, width, idx):
    if isinstance(idx, int):
        return ap[:, idx * width:(idx + 1) * width]
    return ap.rearrange("p (i u) -> p i u", u=width)[:, idx, :]


class GSem:

    def __init__(self, h):
        self.h = h
        self.n = 0


class PSem:

    def __init__(self, h, base):
        self.h = h
        self.base = base

    def v(self, x):
        return self.base + x


class Pool:

    def __init__(self, nc, ctx, n=30):
        self.sems = [ctx.enter_context(nc.semaphore(f"pool{i}")) for i in range(n)]
        self.counts = [0] * n
        self.names = {}
        self.nc = nc
        self.ctx = ctx
        self.gsems = {}
        self.seedscr = nc.dram_tensor("seed_scr", [64, 16], mybir.dt.float32,
                                      kind="Internal").ap()
        self._seed_i = 0

    def gsem(self, name):
        if name not in self.gsems:
            self.gsems[name] = GSem(self.ctx.enter_context(nc_sem(self.nc, "g_" + name)))
        return self.gsems[name]

    def seed_dma(self, gp, gs, units=1):
        for _ in range(units):
            i = self._seed_i % 64
            self._seed_i += 1
            gp.dma_start(out=self.seedscr[i:i + 1, :], in_=self.seed_src).then_inc(gs.h, 16)
            gs.n += 16

    def clear_block(self):
        pass

    def take(self, names):
        self._cur = {nm: i for i, nm in enumerate(names)}
        return {nm: PSem(self.sems[i], self.counts[i]) for i, nm in enumerate(names)}

    def finish(self, totals):
        for nm, tot in totals.items():
            self.counts[self._cur[nm]] += tot


def nc_sem(nc, name):
    return nc.semaphore(name)


def emit_qkv(nc, pool, g, cfg):
    sm = pool.take(["ld", "gf", "rm", "hb", "tps", "hf", "mq", "qe", "mv", "ve", "gl", "ss"])
    wl = pool.gsem("wl")
    qst = pool.gsem("qst")
    vst = pool.gsem("vst")
    XS, WQ, WK, WV, GAM = cfg["XS"], cfg["WQ"], cfg["WK"], cfg["WV"], cfg["GAM"]
    QS, KS, VS = cfg["QS"], cfg["KS"], cfg["VS"]
    rms = cfg["rms"]
    tm_qk = cfg.get("tm_qk", False)
    qscale = cfg["qscale"]
    NTG = cfg.get("ntg", T // 512)
    wl0, qst0, vst0 = wl.n, qst.n, vst.n

    with (
        nc.sbuf_tensor(nm("wq_sb"), [128, 8 * 1024], BF16) as wq_sb,
        nc.sbuf_tensor(nm("wk_sb"), [128, 8 * 1024], BF16) as wk_sb,
        nc.sbuf_tensor(nm("wv_sb"), [128, 8 * 1024], BF16) as wv_sb,
        nc.sbuf_tensor(nm("gam"), [128, 8], F32) as gam,
        nc.sbuf_tensor(nm("xt"), [128, 4 * 1024], F32) as xt,
        nc.sbuf_tensor(nm("junk"), [128, 4 * 1024], BF16) as junk,
        nc.sbuf_tensor(nm("ssq"), [128, 4], F32) as ssq,
        nc.sbuf_tensor(nm("lnv"), [128, 4], F32) as lnv,
        nc.sbuf_tensor(nm("rstd"), [128, 4], F32) as rstd,
        nc.sbuf_tensor(nm("hbf"), [128, 4 * 1024], BF16) as hbf,
        nc.sbuf_tensor(nm("hfm"), [128, 8 * 512], BF16) as hfm,
        nc.sbuf_tensor(nm("qev"), [128, 2 * 512], BF16) as qev,
        nc.sbuf_tensor(nm("vev"), [128, 2 * 1024], BF16) as vev,
        nc.psum_tensor(nm("tp0"), [128, 1024], BF16) as tp0,
        nc.psum_tensor(nm("tp1"), [128, 1024], BF16) as tp1,
        nc.psum_tensor(nm("pqk0"), [128, 512], F32) as pqk0,
        nc.psum_tensor(nm("pqk1"), [128, 512], F32) as pqk1,
        nc.psum_tensor(nm("pv0"), [128, 1024], F32) as pv0,
        nc.psum_tensor(nm("pv1"), [128, 1024], F32) as pv1,
        nc.Block() as blk,
    ):
        tp = (tp0, tp1)
        pqk = (pqk0, pqk1)
        pv = (pv0, pv1)
        XS4 = XS.rearrange("(g q p) d -> g p q d", q=4, p=128)
        if not tm_qk:
            QS8 = QS.rearrange("f (g n) -> f g n", g=T // 512)
            KS8 = KS.rearrange("f (g n) -> f g n", g=T // 512)
        else:
            QS_tm4 = QS.rearrange("(g q p) (nd f) -> g q p nd f", q=4, p=128, nd=2)
            KS_tm4 = KS.rearrange("(g q p) (nd f) -> g q p nd f", q=4, p=128, nd=2)
        VS4 = VS.rearrange("(g q p) d -> g q p d", q=4, p=128)

        @blk.gpsimd
        def _(gp):
            for w_sb, W in ((wq_sb, WQ), (wk_sb, WK), (wv_sb, WV)):
                gp.dma_start(out=w_sb.rearrange("p (c f) -> p c f", c=8),
                             in_=W.rearrange("(c p) f -> p c f", p=128)).then_inc(wl.h, 16)
            wl.n += 48
            with gp.Fori(0, NTG) as tg:
                if not tm_qk:
                    for w_i, QKS in ((0, QS8), (1, KS8)):
                        for fc in range(8):
                            E = 16 * tg + w_i * 8 + fc
                            gp.wait_ge(sm["qe"].h, sm["qe"].v(E + 2))
                            gp.dma_start(out=QKS[fc * 128:(fc + 1) * 128, tg, :],
                                         in_=cv(qev, 512, fc % 2)).then_inc(qst.h, 16)
                else:
                    for w_i, QKT in ((0, QS_tm4), (1, KS_tm4)):
                        for tt in range(4):
                            for nd in range(2):
                                E = 16 * tg + w_i * 8 + tt * 2 + nd
                                gp.wait_ge(sm["qe"].h, sm["qe"].v(E + 2))
                                gp.dma_start(out=QKT[tg, tt, :, nd],
                                             in_=cv(qev, 512, E % 2)).then_inc(qst.h, 16)
                for tt in range(4):
                    Tv = 4 * tg + tt
                    gp.wait_ge(sm["ve"].h, sm["ve"].v(Tv + 2))
                    gp.dma_start(out=VS4[tg, tt, :, :],
                                 in_=cv(vev, 1024, tt % 2)).then_inc(vst.h, 16)
            qst.n += 16 * 16 * NTG
            vst.n += 16 * 4 * NTG
            gp.wait_ge(qst.h, qst.n)
            gp.wait_ge(vst.h, vst.n)

        @blk.sync
        def _(sy):
            sy.dma_start(out=gam[:, :], in_=GAM).then_inc(sm["gl"].h, 16)
            with sy.Fori(0, NTG) as tg:
                if rms:
                    sy.wait_ge(sm["rm"].h, sm["rm"].v(tg))
                sy.wait_ge(sm["hb"].h, sm["hb"].v(4 * tg))
                sy.dma_start(out=xt.rearrange("p (q d) -> p q d", q=4),
                             in_=XS4[tg, :, :, :]).then_inc(sm["ld"].h, 16)

        @blk.scalar
        def _(s):
            with s.Fori(0, NTG) as tg:
                if rms:
                    s.wait_ge(sm["ld"].h, sm["ld"].v(16 * (tg + 1)))
                    s.wait_ge(sm["ss"].h, sm["ss"].v(2 * tg))
                    for tt in range(4):
                        ins = s.activation(cv(junk, 1024, tt), cv(xt, 1024, tt), AF.Square,
                                           accum_out=ssq[:, tt:tt + 1])
                    ins.then_inc(sm["ss"].h, 1)
                    s.wait_ge(sm["ss"].h, sm["ss"].v(2 * tg + 1))
                    s.activation(lnv[:, :], ssq[:, :], AF.Ln, scale=1.0 / D,
                                 bias=g["epsr"]).then_inc(sm["ss"].h, 1)
                    s.wait_ge(sm["ss"].h, sm["ss"].v(2 * tg + 2))
                    s.activation(rstd[:, :], lnv[:, :], AF.Exp, scale=-0.5).then_inc(sm["rm"].h, 1)
                for w_i in range(2):
                    for fc in range(8):
                        E = 16 * tg + w_i * 8 + fc
                        s.wait_ge(sm["mq"].h, sm["mq"].v(E + 1))
                        s.wait_ge(qst.h, qst0 + 16 * E)
                        s.copy(cv(qev, 512, E % 2), pqk[E % 2][:, :]).then_inc(sm["qe"].h, 1)

        @blk.vector
        def _(v):
            v.wait_ge(wl.h, wl0 + 48)
            v.wait_ge(sm["gl"].h, sm["gl"].v(16))
            for w_sb, sc in ((wq_sb, qscale), (wk_sb, 1.0), (wv_sb, 1.0)):
                for c in range(8):
                    ins = v.tensor_scalar(cv(w_sb, 1024, c), cv(w_sb, 1024, c),
                                          gam[:, c:c + 1], float(sc), ALU.mult, ALU.mult)
            ins.then_inc(sm["gf"].h, 1)
            with v.Fori(0, NTG) as tg:
                if rms:
                    v.wait_ge(sm["rm"].h, sm["rm"].v(tg + 1))
                else:
                    v.wait_ge(sm["ld"].h, sm["ld"].v(16 * (tg + 1)))
                for tt in range(4):
                    if tt == 0:
                        v.wait_ge(sm["tps"].h, sm["tps"].v(4 * tg))
                    if rms:
                        v.tensor_scalar(cv(hbf, 1024, tt), cv(xt, 1024, tt),
                                        rstd[:, tt:tt + 1], None, ALU.mult).then_inc(sm["hb"].h, 1)
                    else:
                        v.tensor_copy(cv(hbf, 1024, tt), cv(xt, 1024, tt)).then_inc(sm["hb"].h, 1)
                for tt in range(4):
                    G = 4 * tg + tt
                    if tt == 0:
                        v.wait_ge(sm["mq"].h, sm["mq"].v(16 * tg))
                        v.wait_ge(sm["mv"].h, sm["mv"].v(8 * tg))
                    v.wait_ge(sm["tps"].h, sm["tps"].v(G + 1))
                    v.tensor_copy(
                        hfm.rearrange("p (c f) -> p c f", c=8)[:, :, tt * 128:(tt + 1) * 128],
                        tp[tt % 2].rearrange("p (c f) -> p c f", c=8)).then_inc(sm["hf"].h, 1)
                for tt in range(4):
                    Tv = 4 * tg + tt
                    v.wait_ge(sm["mv"].h, sm["mv"].v(2 * Tv + 2))
                    v.wait_ge(vst.h, vst0 + 16 * Tv)
                    v.tensor_copy(cv(vev, 1024, tt % 2), pv[tt % 2][:, :]).then_inc(sm["ve"].h, 1)

        @blk.tensor
        def _(t):
            t.sem_inc(sm["qe"].h, 1)
            t.sem_inc(sm["ve"].h, 1)
            t.sem_inc(sm["hf"].h, 1)
            t.wait_ge(sm["gf"].h, sm["gf"].v(1))
            with t.Fori(0, NTG) as tg:
                for tt in range(4):
                    G = 4 * tg + tt
                    t.wait_ge(sm["hb"].h, sm["hb"].v(4 * tg + tt + 1))
                    t.wait_ge(sm["hf"].h, sm["hf"].v(G))
                    for c in range(8):
                        ins = t.transpose(cv(tp[tt % 2], 128, c),
                                          hbf[:, tt * 1024 + c * 128:tt * 1024 + (c + 1) * 128],
                                          g["idn"][:, :])
                    ins.then_inc(sm["tps"].h, 1)
                t.wait_ge(sm["hf"].h, sm["hf"].v(4 * (tg + 1) + 1))
                if not tm_qk:
                    for w_i, w_sb in ((0, wq_sb), (1, wk_sb)):
                        for fc in range(8):
                            E = 16 * tg + w_i * 8 + fc
                            t.wait_ge(sm["qe"].h, sm["qe"].v(E))
                            for kc in range(8):
                                ins = t.matmul(pqk[fc % 2][:, :],
                                               w_sb[:, kc * 1024 + fc * 128:kc * 1024 + (fc + 1) * 128],
                                               cv(hfm, 512, kc),
                                               start=(kc == 0), stop=(kc == 7))
                            ins.then_inc(sm["mq"].h, 1)
                else:
                    for w_i, w_sb in ((0, wq_sb), (1, wk_sb)):
                        for tt in range(4):
                            for nd in range(2):
                                E = 16 * tg + w_i * 8 + tt * 2 + nd
                                t.wait_ge(sm["qe"].h, sm["qe"].v(E))
                                for kc in range(8):
                                    ins = t.matmul(
                                        pqk[E % 2][:, :],
                                        hfm[:, kc * 512 + tt * 128:kc * 512 + (tt + 1) * 128],
                                        w_sb[:, kc * 1024 + nd * 512:kc * 1024 + (nd + 1) * 512],
                                        start=(kc == 0), stop=(kc == 7))
                                ins.then_inc(sm["mq"].h, 1)
                for tt in range(4):
                    t.wait_ge(sm["ve"].h, sm["ve"].v(4 * tg + tt))
                    for nd in range(2):
                        for kc in range(8):
                            ins = t.matmul(pv[tt % 2][:, nd * 512:(nd + 1) * 512],
                                           hfm[:, kc * 512 + tt * 128:kc * 512 + (tt + 1) * 128],
                                           wv_sb[:, kc * 1024 + nd * 512:kc * 1024 + (nd + 1) * 512],
                                           start=(kc == 0), stop=(kc == 7))
                        ins.then_inc(sm["mv"].h, 1)
    pool.finish({"ss": (2 * NTG) if rms else 0, "gl": 16, "ld": 16 * NTG, "gf": 1, "rm": NTG if rms else 0,
                 "hb": 4 * NTG, "tps": 4 * NTG, "hf": 4 * NTG + 1,
                 "mq": 16 * NTG, "qe": 16 * NTG + 1, "mv": 8 * NTG,
                 "ve": 4 * NTG + 1})


def emit_attn(nc, pool, g, cfg):
    sm = pool.take(["qld", "m1", "r1", "e1", "sln", "vaug", "ttr", "trd",
                    "augst", "augld", "m2", "e2", "mpv", "ve2", "msd",
                    "radd", "mpj", "xld", "xst", "ssq2", "srs", "vvar",
                    "lnv", "vs2", "vs3", "xrl"])
    wl = pool.gsem("wl")
    QS, KS, VS, XRES = cfg["QS"], cfg["KS"], cfg["VS"], cfg["XRES"]
    WO, BO, XOUT = cfg["WO"], cfg["BO"], cfg["XOUT"]
    AUGS = nc.dram_tensor(cfg["name"] + "_augs", [4, 16, 128], BF16, kind="Internal").ap()
    wl0 = wl.n
    NHP = 8

    es = ExitStack()
    wo_sb = es.enter_context(nc.sbuf_tensor(nm("a_wo_sb"), [128, 8 * 1024], BF16))
    bo_sb = es.enter_context(nc.sbuf_tensor(nm("a_bo_sb"), [1, 1024], F32))
    qstA = es.enter_context(nc.sbuf_tensor(nm("a_qstA"), [128, 2048], BF16))
    qstB = es.enter_context(nc.sbuf_tensor(nm("a_qstB"), [128, 2048], BF16))
    kstA = es.enter_context(nc.sbuf_tensor(nm("a_kstA"), [128, 2048], BF16))
    kstB = es.enter_context(nc.sbuf_tensor(nm("a_kstB"), [128, 2048], BF16))
    vstA = es.enter_context(nc.sbuf_tensor(nm("a_vstA"), [128, 16 * 64], BF16))
    vstB = es.enter_context(nc.sbuf_tensor(nm("a_vstB"), [128, 16 * 64], BF16))
    ET = es.enter_context(nc.sbuf_tensor(nm("a_ET"), [128, 16 * 512], BF16))
    junkA = es.enter_context(nc.sbuf_tensor(nm("a_junkA"), [128, 2048], BF16))
    nmall = es.enter_context(nc.sbuf_tensor(nm("a_nmall"), [128, 32], F32))
    ssall = es.enter_context(nc.sbuf_tensor(nm("a_ssall"), [128, 32], F32))
    lnl = es.enter_context(nc.sbuf_tensor(nm("a_lnl"), [128, 32], F32))
    negb = es.enter_context(nc.sbuf_tensor(nm("a_negb"), [128, 32], F32))
    lo32 = es.enter_context(nc.sbuf_tensor(nm("a_lo32"), [128, 32], F32))
    hilo = es.enter_context(nc.sbuf_tensor(nm("a_hilo"), [128, 64], BF16))
    tr64 = es.enter_context(nc.sbuf_tensor(nm("a_tr64"), [64, 128], BF16))
    attnO = es.enter_context(nc.sbuf_tensor(nm("a_attnO"), [128, 8 * 2048], BF16))
    xrt = es.enter_context(nc.sbuf_tensor(nm("a_xrt"), [128, 2 * 1024], F32))
    rt = es.enter_context(nc.sbuf_tensor(nm("a_rt"), [128, 2 * 1024], F32))
    ob = es.enter_context(nc.sbuf_tensor(nm("a_ob"), [128, 2 * 1024], F32))
    junkP = es.enter_context(nc.sbuf_tensor(nm("a_junkP"), [128, 2 * 1024], BF16))
    pstat = es.enter_context(nc.sbuf_tensor(nm("a_pstat"), [128, 16], F32))
    with (
        es,
        nc.psum_tensor(nm("S2a"), [128, 2048], F32) as S2a,
        nc.psum_tensor(nm("S2b"), [128, 2048], F32) as S2b,
        nc.Block() as blk,
    ):
        S2 = (S2a, S2b)
        qst = (qstA, qstB)
        kst = (kstA, kstB)
        vst = (vstA, vstB)
        ps_s = (S2a[:, 0:512], S2a[:, 512:1024], S2b[:, 0:512], S2b[:, 512:1024])
        pvp = S2a[:, 1024:1536]
        ptr = S2a[:, 1536:2048].bitcast(BF16)
        po = (S2b[:, 0:1024], S2b[:, 1024:2048])
        QSh = QS.rearrange("(h r) t -> h r t", r=64)
        KSh = KS.rearrange("(h r) t -> h r t", r=64)
        AUGSf = AUGS.rearrange("x a b -> x (a b)")

        @blk.gpsimd
        def _(gp):
            gp.dma_start(out=wo_sb.rearrange("p (c f) -> p c f", c=8),
                         in_=WO.rearrange("(c p) f -> p c f", p=128)).then_inc(wl.h, 16)
            for s in range(SPC):
                with gp.Fori(0, NHP) as hp:
                    i = 8 * s + hp
                    gp.wait_ge(sm["trd"].h, sm["trd"].v(i + 1))
                    for j in range(4):
                        gp.dma_start(out=AUGS[j], in_=tr64[j * 16:(j + 1) * 16, :]).then_inc(sm["augst"].h, 16)
                    gp.wait_ge(sm["augst"].h, sm["augst"].v(64 * (i + 1)))
                    gp.dma_start(out=qst[0][64:66, :], in_=AUGSf[0:2, :]).then_inc(sm["augld"].h, 16)
                    gp.dma_start(out=qst[1][64:66, :], in_=AUGSf[2:4, :]).then_inc(sm["augld"].h, 16)
            gp.wait_ge(wl.h, wl0 + 16)
        wl.n += 16

        @blk.sync
        def _(sy):
            sy.dma_start(out=bo_sb[:, :], in_=BO).then_inc(sm["xld"].h, 16)
            for s in range(SPC):
                with sy.Fori(0, NHP) as hp:
                    i = 8 * s + hp
                    sy.wait_ge(sm["m2"].h, sm["m2"].v(128 * i))
                    for X in range(2):
                        h = 2 * hp + X
                        sy.dma_start(out=qst[X][0:64, :],
                                     in_=QSh[h, :, s * 2048:(s + 1) * 2048]).then_inc(sm["qld"].h, 16)
                        sy.dma_start(out=kst[X][0:64, :],
                                     in_=KSh[h, :, s * 2048:(s + 1) * 2048]).then_inc(sm["qld"].h, 16)
                    sy.wait_ge(sm["mpv"].h, sm["mpv"].v(8 * i))
                    for X in range(2):
                        h = 2 * hp + X
                        sy.dma_start(
                            out=vst[X].rearrange("p (c e) -> p c e", c=16),
                            in_=VS.rearrange("(s c p) (h e) -> s h p c e", s=SPC, p=128, h=16)[s, h],
                        ).then_inc(sm["qld"].h, 16)

                for tt in range(16):
                    PT = 16 * s + tt
                    if PT >= 2:
                        sy.wait_ge(sm["radd"].h, sm["radd"].v(PT - 1))
                    sy.dma_start(out=cv(xrt, 1024, tt % 2),
                                 in_=XRES.rearrange("(t p) d -> t p d", p=128)[PT]).then_inc(sm["xrl"].h, 16)
                    sy.wait_ge(sm["lnv"].h, sm["lnv"].v(PT + 1))
                    sy.dma_start(out=XOUT.rearrange("(t p) d -> t p d", p=128)[PT],
                                 in_=cv(ob, 1024, tt % 2)).then_inc(sm["xst"].h, 16)
            sy.wait_ge(sm["xst"].h, sm["xst"].v(16 * 32))

        @blk.scalar
        def _(s_):
            for s in range(SPC):
                with s_.Fori(0, NHP) as hp:
                    i = 8 * s + hp
                    for u in range(32):
                        s_.wait_ge(sm["r1"].h, sm["r1"].v(32 * i + u + 1))
                        s_.wait_ge(sm["e1"].h, sm["e1"].v(32 * i + max(u - 1, 0)))
                        s_.activation(junkA[:, :], S2[u % 2][:, :], AF.Exp,
                                      bias=nmall[:, u:u + 1],
                                      accum_out=ssall[:, u:u + 1]).then_inc(sm["e1"].h, 1)
                    s_.wait_ge(sm["e1"].h, sm["e1"].v(32 * (i + 1)))
                    s_.activation(lnl[:, :], ssall[:, :], AF.Ln).then_inc(sm["sln"].h, 1)
                    for qg in range(4):
                        for X in range(2):
                            un = 2 * qg + X
                            s_.wait_ge(sm["mpv"].h, sm["mpv"].v(8 * i + un))
                            for kt in range(16):
                                m2j = 128 * i + un * 16 + kt
                                s_.wait_ge(sm["m2"].h, sm["m2"].v(m2j + 1))
                                s_.activation(cv(ET, 512, kt), ps_s[kt % 4],
                                              AF.Exp).then_inc(sm["e2"].h, 1)
                for tt in range(16):
                    PT = 16 * s + tt
                    p2 = tt % 2
                    s_.wait_ge(sm["radd"].h, sm["radd"].v(PT + 1))
                    if PT >= 2:
                        s_.wait_ge(sm["ssq2"].h, sm["ssq2"].v(PT - 1))
                        s_.wait_ge(sm["srs"].h, sm["srs"].v(2 * PT - 2))
                        s_.wait_ge(sm["vvar"].h, sm["vvar"].v(PT - 1))
                    s_.activation(cv(junkP, 1024, p2), cv(rt, 1024, p2), AF.Square,
                                  accum_out=pstat[:, 12 + p2:13 + p2]).then_inc(sm["ssq2"].h, 1)
                    s_.wait_ge(sm["vvar"].h, sm["vvar"].v(PT + 1))
                    s_.activation(pstat[:, 8 + p2:9 + p2],
                                  pstat[:, 6 + p2:7 + p2], AF.Ln,
                                  bias=g["epsl"]).then_inc(sm["srs"].h, 1)
                    s_.wait_ge(sm["srs"].h, sm["srs"].v(2 * PT + 1))
                    s_.activation(pstat[:, 10 + p2:11 + p2],
                                  pstat[:, 8 + p2:9 + p2], AF.Exp,
                                  scale=-0.5).then_inc(sm["srs"].h, 1)

        @blk.vector
        def _(v):
            v.memset(kstA[64:66, :], 1.0)
            v.memset(kstB[64:66, :], 1.0).then_inc(sm["msd"].h, 1)
            for s in range(SPC):
                v.wait_ge(sm["mpj"].h, sm["mpj"].v(32 * s))
                with v.Fori(0, NHP) as hp:
                    i = 8 * s + hp
                    v.wait_ge(sm["e1"].h, sm["e1"].v(32 * i))
                    for u in range(32):
                        v.wait_ge(sm["m1"].h, sm["m1"].v(32 * i + u + 1))
                        v.reduce_max(nmall[:, u:u + 1], S2[u % 2][:, :], AX.X,
                                     negate=True).then_inc(sm["r1"].h, 1)
                    v.wait_ge(sm["sln"].h, sm["sln"].v(i + 1))
                    v.wait_ge(sm["ttr"].h, sm["ttr"].v(i))
                    hiv = hilo.rearrange("p (y z c) -> p y z c", y=2, z=2)[:, :, 0, :]
                    lov = hilo.rearrange("p (y z c) -> p y z c", y=2, z=2)[:, :, 1, :]
                    v.tensor_sub(negb[:, :], nmall[:, :], lnl[:, :]).then_inc(sm["vs2"].h, 1)
                    v.wait_ge(sm["vs2"].h, sm["vs2"].v(3 * i + 1))
                    v.tensor_copy(hiv, negb.rearrange("p (y c) -> p y c", y=2)).then_inc(sm["vs2"].h, 1)
                    v.wait_ge(sm["vs2"].h, sm["vs2"].v(3 * i + 2))
                    v.tensor_sub(lo32.rearrange("p (y c) -> p y c", y=2),
                                 negb.rearrange("p (y c) -> p y c", y=2),
                                 hiv).then_inc(sm["vs2"].h, 1)
                    v.wait_ge(sm["vs2"].h, sm["vs2"].v(3 * i + 3))
                    v.tensor_copy(lov, lo32.rearrange("p (y c) -> p y c", y=2)).then_inc(sm["vaug"].h, 1)
                    v.wait_ge(sm["ttr"].h, sm["ttr"].v(i + 1))
                    v.wait_ge(sm["augst"].h, sm["augst"].v(64 * i))
                    v.tensor_copy(tr64[:, :], ptr[0:64, 0:128]).then_inc(sm["trd"].h, 1)
                    for qg in range(4):
                        v.wait_ge(sm["mpv"].h, sm["mpv"].v(8 * i + 2 * qg + 2))
                        v.tensor_copy(
                            attnO.rearrange("p (c n) -> p c n", n=512)[:, 4 * hp + qg, :],
                            pvp[:, :]).then_inc(sm["ve2"].h, 1)
                for tt in range(16):
                    PT = 16 * s + tt
                    p2 = tt % 2
                    v.wait_ge(sm["mpj"].h, sm["mpj"].v(2 * PT + 2))
                    v.wait_ge(sm["xrl"].h, sm["xrl"].v(16 * (PT + 1)))
                    if PT >= 2:
                        v.wait_ge(sm["xst"].h, sm["xst"].v(16 * (PT - 1)))
                    v.tensor_add(cv(rt, 1024, p2), po[p2][:, :],
                                 cv(xrt, 1024, p2)).then_inc(sm["radd"].h, 1)
                    v.wait_ge(sm["radd"].h, sm["radd"].v(PT + 1))
                    v.reduce_sum(pstat[:, p2:p2 + 1], cv(rt, 1024, p2), AX.X).then_inc(sm["vs3"].h, 1)
                    v.wait_ge(sm["vs3"].h, sm["vs3"].v(3 * PT + 1))
                    v.tensor_scalar(pstat[:, 2 + p2:3 + p2], pstat[:, p2:p2 + 1],
                                    -1.0 / D, None, ALU.mult).then_inc(sm["vs3"].h, 1)
                    v.wait_ge(sm["vs3"].h, sm["vs3"].v(3 * PT + 2))
                    v.tensor_mul(pstat[:, 4 + p2:5 + p2], pstat[:, 2 + p2:3 + p2],
                                 pstat[:, 2 + p2:3 + p2]).then_inc(sm["vs3"].h, 1)
                    v.wait_ge(sm["ssq2"].h, sm["ssq2"].v(PT + 1))
                    v.wait_ge(sm["vs3"].h, sm["vs3"].v(3 * PT + 3))
                    v.tensor_scalar(pstat[:, 6 + p2:7 + p2], pstat[:, 12 + p2:13 + p2],
                                    1.0 / D, pstat[:, 4 + p2:5 + p2],
                                    ALU.mult, ALU.subtract).then_inc(sm["vvar"].h, 1)
                    v.wait_ge(sm["srs"].h, sm["srs"].v(2 * PT + 2))
                    v.tensor_scalar(cv(ob, 1024, p2), cv(rt, 1024, p2),
                                    pstat[:, 2 + p2:3 + p2], pstat[:, 10 + p2:11 + p2],
                                    ALU.add, ALU.mult).then_inc(sm["lnv"].h, 1)

        @blk.tensor
        def _(t):
            t.wait_ge(sm["msd"].h, sm["msd"].v(1))
            t.wait_ge(wl.h, wl0 + 16)
            t.wait_ge(sm["xld"].h, sm["xld"].v(16))
            for s in range(SPC):
                t.wait_ge(sm["radd"].h, sm["radd"].v(16 * s))
                with t.Fori(0, NHP) as hp:
                    i = 8 * s + hp
                    t.wait_ge(sm["qld"].h, sm["qld"].v(96 * (i + 1)))
                    t.wait_ge(sm["trd"].h, sm["trd"].v(i))
                    for u in range(32):
                        X, qt = u // 16, u % 16
                        t.wait_ge(sm["e1"].h, sm["e1"].v(32 * i + max(u - 1, 0)))
                        for kc in range(4):
                            ins = t.matmul(S2[u % 2][:, kc * 512:(kc + 1) * 512],
                                           qst[X][0:64, qt * 128:(qt + 1) * 128],
                                           kst[X][0:64, kc * 512:(kc + 1) * 512],
                                           start=True, stop=True)
                        ins.then_inc(sm["m1"].h, 1)
                    t.wait_ge(sm["vaug"].h, sm["vaug"].v(i + 1))
                    t.transpose(ptr[0:64, 0:128], hilo[:, :], g["idn"][:, :]).then_inc(sm["ttr"].h, 1)
                    t.wait_ge(sm["augld"].h, sm["augld"].v(32 * (i + 1)))
                    for qg in range(4):
                        for X in range(2):
                            un = 2 * qg + X
                            for kt in range(16):
                                m2j = 128 * i + un * 16 + kt
                                t.wait_ge(sm["e2"].h, sm["e2"].v(128 * i + un * 16 + max(kt - 3, 0)))
                                t.matmul(ps_s[kt % 4],
                                         kst[X][0:66, kt * 128:(kt + 1) * 128],
                                         qst[X][0:66, qg * 512:(qg + 1) * 512],
                                         start=True, stop=True).then_inc(sm["m2"].h, 1)
                            t.wait_ge(sm["e2"].h, sm["e2"].v(128 * i + un * 16 + 16))
                            if X == 0:
                                t.wait_ge(sm["ve2"].h, sm["ve2"].v(4 * i + qg))
                            for kt in range(16):
                                ins = t.matmul(pvp[X * 64:X * 64 + 64, :],
                                               vst[X][:, kt * 64:(kt + 1) * 64],
                                               cv(ET, 512, kt),
                                               start=(kt == 0), stop=(kt == 15),
                                               tile_position=(0, 64 * X))
                            ins.then_inc(sm["mpv"].h, 1)
                t.wait_ge(sm["ve2"].h, sm["ve2"].v(32 * (s + 1)))
                t.wait_ge(sm["e1"].h, sm["e1"].v(256 * (s + 1)))
                for tt in range(16):
                    PT = 16 * s + tt
                    if PT >= 2:
                        t.wait_ge(sm["radd"].h, sm["radd"].v(PT - 1))
                    for nd in range(2):
                        for kc in range(8):
                            t.matmul(po[tt % 2][:, nd * 512:(nd + 1) * 512],
                                     attnO[:, kc * 2048 + tt * 128:kc * 2048 + (tt + 1) * 128],
                                     wo_sb[:, kc * 1024 + nd * 512:kc * 1024 + (nd + 1) * 512],
                                     start=(kc == 0), stop=False)
                        t.matmul(po[tt % 2][:, nd * 512:(nd + 1) * 512],
                                 g["ones"][0:1, 0:128], bo_sb[0:1, nd * 512:(nd + 1) * 512],
                                 start=False, stop=True).then_inc(sm["mpj"].h, 1)
    pool.finish({"qld": 96 * 16, "m1": 32 * 16, "r1": 32 * 16, "e1": 32 * 16,
                 "sln": 16, "vaug": 16, "ttr": 16, "trd": 16,
                 "augst": 64 * 16, "augld": 32 * 16, "m2": 128 * 16,
                 "e2": 128 * 16, "mpv": 8 * 16, "ve2": 4 * 16, "msd": 1,
                 "radd": 32, "mpj": 64, "xld": 16, "xst": 16 * 32,
                 "ssq2": 32, "srs": 64, "vvar": 32, "lnv": 32,
                 "vs2": 3 * 16, "vs3": 3 * 32, "xrl": 16 * 32})


def emit_ffn(nc, pool, g, cfg):
    L = cfg["layer"]
    H = cfg["H"]
    shared = cfg["shared"]
    HB = H // 128
    TGW = 512 if shared else 256
    NTGC = T // TGW
    TPC = TGW // 128
    sm = pool.take(["ldx", "cbd", "tpd", "xfd", "mr", "rrm", "sre", "wvd",
                    "vsA", "mh", "gh", "my", "ye", "accv", "dr", "dssq",
                    "dsrs", "dvvar", "dlnv", "ds3", "zf"])
    wle = pool.gsem("wl")
    xfst = pool.gsem("xfst")
    accst = pool.gsem("accst")
    xfld_g = pool.gsem("xfld")
    XIN, XOUT = cfg["XIN"], cfg["XOUT"]
    RW, RB = cfg["RW"], cfg.get("RB")
    XF = nc.dram_tensor(cfg["name"] + "_xf", [D, T], BF16, kind="Internal").ap()
    ACC = nc.dram_tensor(cfg["name"] + "_acc", [T, D], F32, kind="Internal").ap()
    XIN_T = XIN.rearrange("(t p) d -> t p d", p=128)
    XOUT_T = XOUT.rearrange("(t p) d -> t p d", p=128)
    ACC_T = ACC.rearrange("(t p) d -> t p d", p=128)
    XFr = XF.rearrange("(c p) t -> p c t", p=128)

    _scA = nc.enter_named_scope(cfg["name"] + "_A", False)[0]
    esP = ExitStack()
    wsb = esP.enter_context(nc.sbuf_tensor(nm("f_wsb"), [128, 32 * 8], F32))
    esA = ExitStack()
    xta = esA.enter_context(nc.sbuf_tensor(nm("f_xta"), [128, 2 * 1024], F32))
    hbf2 = esA.enter_context(nc.sbuf_tensor(nm("f_hbf"), [128, 2 * 1024], BF16))
    xfb = esA.enter_context(nc.sbuf_tensor(nm("f_xfb"), [128, 2 * 1024], BF16))
    rw_sb = esA.enter_context(nc.sbuf_tensor(nm("f_rw"), [128, 8 * 8], BF16))
    rb_sb = esA.enter_context(nc.sbuf_tensor(nm("f_rb"), [1, 8], F32))
    prexp = esA.enter_context(nc.sbuf_tensor(nm("f_prexp"), [128, 2 * 8], F32))
    prb = esA.enter_context(nc.sbuf_tensor(nm("f_prb"), [128, 2 * 8], F32))
    msk = esA.enter_context(nc.sbuf_tensor(nm("f_msk"), [128, 2 * 8], F32))
    rstat = esA.enter_context(nc.sbuf_tensor(nm("f_rstat"), [128, 16], F32))
    wl0 = wle.n
    xfst0 = xfst.n
    with (
        esA,
        nc.psum_tensor(nm("f_tp0"), [128, 1024], BF16) as tpf0,
        nc.psum_tensor(nm("f_tp1"), [128, 1024], BF16) as tpf1,
        nc.psum_tensor(nm("f_pr0"), [128, 8], F32) as pr0,
        nc.psum_tensor(nm("f_pr1"), [128, 8], F32) as pr1,
        nc.Block() as blk,
    ):
        tpf = (tpf0, tpf1)
        pr = (pr0, pr1)

        @blk.gpsimd
        def _(gp):
            gp.dma_start(out=rw_sb.rearrange("p (c e) -> p c e", c=8),
                         in_=RW.rearrange("(c p) e -> p c e", p=128)).then_inc(wle.h, 16)
            gp.wait_ge(wle.h, wl0 + 16)
        wle.n += 16

        @blk.sync
        def _(sy):
            if RB is not None:
                sy.dma_start(out=rb_sb[:, :], in_=RB).then_inc(sm["ldx"].h, 16)
            for tt in range(32):
                p2 = tt % 2
                if tt >= 2:
                    sy.wait_ge(sm["cbd"].h, sm["cbd"].v(tt - 1))
                sy.dma_start(out=cv(xta, 1024, p2), in_=XIN_T[tt]).then_inc(sm["ldx"].h, 16)
                sy.wait_ge(sm["xfd"].h, sm["xfd"].v(tt + 1))
                sy.dma_start(out=XFr[:, :, tt * 128:(tt + 1) * 128],
                             in_=xfb.rearrange("p (q c f) -> p q c f", q=2, c=8)[:, p2],
                             ).then_inc(xfst.h, 16)
            sy.wait_ge(xfst.h, xfst0 + 16 * 32)
        xfst.n += 16 * 32

        @blk.vector
        def _(v):
            rbase = 16 if RB is not None else 0
            for tt in range(32):
                p2 = tt % 2
                v.wait_ge(sm["ldx"].h, sm["ldx"].v(rbase + 16 * (tt + 1)))
                if tt >= 2:
                    v.wait_ge(sm["tpd"].h, sm["tpd"].v(tt - 1))
                v.tensor_copy(cv(hbf2, 1024, p2), cv(xta, 1024, p2)).then_inc(sm["cbd"].h, 1)
                v.wait_ge(sm["tpd"].h, sm["tpd"].v(tt + 1))
                if tt >= 2:
                    v.wait_ge(xfst.h, xfst0 + 16 * (tt - 1))
                v.tensor_copy(cv(xfb, 1024, p2), tpf[p2][:, :]).then_inc(sm["xfd"].h, 1)
                v.wait_ge(sm["mr"].h, sm["mr"].v(tt + 1))
                v.reduce_max(rstat[:, p2:p2 + 1], pr[p2][:, :], AX.X,
                             negate=True).then_inc(sm["rrm"].h, 1)
                v.wait_ge(sm["sre"].h, sm["sre"].v(tt + 1))
                v.reciprocal(rstat[:, 4 + p2:5 + p2], rstat[:, 2 + p2:3 + p2]).then_inc(sm["vsA"].h, 1)
                v.wait_ge(sm["vsA"].h, sm["vsA"].v(6 * tt + 1))
                v.tensor_scalar(cv(prb, 8, p2), cv(prexp, 8, p2),
                                rstat[:, 4 + p2:5 + p2], None, ALU.mult).then_inc(sm["vsA"].h, 1)
                v.wait_ge(sm["vsA"].h, sm["vsA"].v(6 * tt + 2))
                v.reduce_max(rstat[:, 6 + p2:7 + p2], cv(prb, 8, p2), AX.X).then_inc(sm["vsA"].h, 1)
                v.wait_ge(sm["vsA"].h, sm["vsA"].v(6 * tt + 3))
                v.tensor_scalar(cv(msk, 8, p2), cv(prb, 8, p2),
                                rstat[:, 6 + p2:7 + p2], None, ALU.is_ge).then_inc(sm["vsA"].h, 1)
                v.wait_ge(sm["vsA"].h, sm["vsA"].v(6 * tt + 4))
                v.scalar_tensor_tensor(cv(msk, 8, p2), cv(msk, 8, p2), -2.0,
                                       cv(prb, 8, p2), ALU.mult, ALU.add).then_inc(sm["vsA"].h, 1)
                v.wait_ge(sm["vsA"].h, sm["vsA"].v(6 * tt + 5))
                v.reduce_max(rstat[:, 8 + p2:9 + p2], cv(msk, 8, p2), AX.X).then_inc(sm["vsA"].h, 1)
                v.wait_ge(sm["vsA"].h, sm["vsA"].v(6 * tt + 6))
                v.tensor_scalar(cv(msk, 8, p2), cv(prb, 8, p2),
                                rstat[:, 8 + p2:9 + p2], None, ALU.is_ge).then_inc(sm["wvd"].h, 1)
                v.wait_ge(sm["wvd"].h, sm["wvd"].v(2 * tt + 1))
                v.tensor_mul(cv(wsb, 8, tt), cv(prb, 8, p2),
                             cv(msk, 8, p2)).then_inc(sm["wvd"].h, 1)

        @blk.scalar
        def _(s_):
            for tt in range(32):
                p2 = tt % 2
                s_.wait_ge(sm["rrm"].h, sm["rrm"].v(tt + 1))
                if tt >= 2:
                    s_.wait_ge(sm["vsA"].h, sm["vsA"].v(6 * (tt - 1)))
                s_.activation(cv(prexp, 8, p2), pr[p2][:, :], AF.Exp,
                              bias=rstat[:, p2:p2 + 1],
                              accum_out=rstat[:, 2 + p2:3 + p2]).then_inc(sm["sre"].h, 1)

        @blk.tensor
        def _(t):
            t.wait_ge(wle.h, wl0 + 16)
            if RB is not None:
                t.wait_ge(sm["ldx"].h, sm["ldx"].v(16))
            for tt in range(32):
                p2 = tt % 2
                t.wait_ge(sm["cbd"].h, sm["cbd"].v(tt + 1))
                if tt >= 2:
                    t.wait_ge(sm["xfd"].h, sm["xfd"].v(tt - 1))
                for c in range(8):
                    ins = t.transpose(cv(tpf[p2], 128, c),
                                      hbf2[:, p2 * 1024 + c * 128:p2 * 1024 + (c + 1) * 128],
                                      g["idn"][:, :])
                ins.then_inc(sm["tpd"].h, 1)
                t.wait_ge(sm["xfd"].h, sm["xfd"].v(tt + 1))
                if tt >= 2:
                    t.wait_ge(sm["sre"].h, sm["sre"].v(tt - 1))
                for c in range(8):
                    ins = t.matmul(pr[p2][:, :], xfb[:, p2 * 1024 + c * 128:p2 * 1024 + (c + 1) * 128],
                                   rw_sb[:, c * 8:(c + 1) * 8],
                                   start=(c == 0), stop=(RB is None and c == 7))
                if RB is not None:
                    ins = t.matmul(pr[p2][:, :], g["ones"][0:1, 0:128], rb_sb[0:1, :],
                                   start=False, stop=True)
                ins.then_inc(sm["mr"].h, 1)
    nc.leave_named_scope(cfg["name"] + "_A", _scA, False)
    totA = {"ldx": (16 if RB is not None else 0) + 16 * 32, "cbd": 32,
            "tpd": 32, "xfd": 32, "mr": 32, "rrm": 32, "sre": 32,
            "wvd": 2 * 32, "vsA": 6 * 32}

    if shared:
        esB = ExitStack()
        sw1_sb = esB.enter_context(nc.sbuf_tensor(nm("f_sw1"), [128, 8 * 2048], BF16))
        sw2_sb = esB.enter_context(nc.sbuf_tensor(nm("f_sw2"), [128, 16 * 1024], BF16))
        sb1_sb = esB.enter_context(nc.sbuf_tensor(nm("f_sb1"), [128, 16], F32))
        sb2_sb = esB.enter_context(nc.sbuf_tensor(nm("f_sb2"), [1, 1024], F32))
        h1b = esB.enter_context(nc.sbuf_tensor(nm("f_h1b"), [128, 2 * 16 * 512], BF16))
        xfs = esB.enter_context(nc.sbuf_tensor(nm("f_xfs"), [128, 2 * 8 * 512], BF16))
        yev = esB.enter_context(nc.sbuf_tensor(nm("f_yev"), [128, 2 * 1024], F32))
        wl1 = wle.n
        xfld0 = xfld_g.n
        accst0 = accst.n
        with (
            esB,
            nc.psum_tensor(nm("f_ph0"), [128, 512], F32) as ph0,
            nc.psum_tensor(nm("f_ph1"), [128, 512], F32) as ph1,
            nc.psum_tensor(nm("f_py0"), [128, 1024], F32) as py0,
            nc.psum_tensor(nm("f_py1"), [128, 1024], F32) as py1,
            nc.Block() as blk,
        ):
            ph = (ph0, ph1)
            py = (py0, py1)

            @blk.gpsimd
            def _(gp):
                gp.dma_start(out=sw1_sb.rearrange("p (c f) -> p c f", c=8),
                             in_=cfg["SW1"].rearrange("(c p) f -> p c f", p=128)).then_inc(wle.h, 16)
                gp.dma_start(out=sw2_sb.rearrange("p (c f) -> p c f", c=16),
                             in_=cfg["SW2"].rearrange("(c p) f -> p c f", p=128)).then_inc(wle.h, 16)
                gp.wait_ge(wle.h, wl1 + 32)
            wle.n += 32

            @blk.sync
            def _(sy):
                sy.dma_start(out=sb1_sb[:, :], in_=cfg["SB1"]).then_inc(sm["ldx"].h, 16)
                sy.dma_start(out=sb2_sb[:, :], in_=cfg["SB2"]).then_inc(sm["ldx"].h, 16)
                with sy.Fori(0, 8) as tg:
                    sy.wait_ge(sm["mh"].h, sm["mh"].v(16 * tg - 16))
                    sy.dma_start(out=xfs.rearrange("p (q c f) -> p q c f", q=2, c=8)[:, tg % 2],
                                 in_=XFr.rearrange("p c (gg f) -> p gg c f", gg=8)[:, tg],
                                 ).then_inc(xfld_g.h, 16)
                    for tt in range(4):
                        j = 4 * tg + tt
                        sy.wait_ge(sm["ye"].h, sm["ye"].v(j + 1))
                        sy.dma_start(out=ACC_T[j], in_=cv(yev, 1024, tt % 2)).then_inc(accst.h, 16)
            xfld_g.n += 16 * 8
            accst.n += 16 * 32

            @blk.scalar
            def _(s_):
                with s_.Fori(0, 8) as tg:
                    for hc in range(16):
                        u = 16 * tg + hc
                        s_.wait_ge(sm["mh"].h, sm["mh"].v(u + 1))
                        if hc == 0:
                            s_.wait_ge(sm["my"].h, sm["my"].v(4 * tg - 4))
                        s_.activation(
                            h1b.rearrange("p (q c f) -> p q c f", q=2, c=16)[:, tg % 2, hc, :],
                            ph[hc % 2][:, :], AF.Gelu,
                            bias=sb1_sb[:, hc:hc + 1]).then_inc(sm["gh"].h, 1)

            @blk.vector
            def _(v):
                with v.Fori(0, 8) as tg:
                    for tt in range(4):
                        j = 4 * tg + tt
                        v.wait_ge(sm["my"].h, sm["my"].v(j + 1))
                        v.wait_ge(accst.h, accst0 + 16 * (j - 1))
                        v.tensor_copy(cv(yev, 1024, tt % 2), py[tt % 2][:, :]).then_inc(sm["ye"].h, 1)

            @blk.tensor
            def _(t):
                t.wait_ge(wle.h, wl1 + 32)
                t.wait_ge(sm["ldx"].h, sm["ldx"].v(32))
                with t.Fori(0, 4) as tg2:
                    for par in range(2):
                        tg = 2 * tg2 + par
                        t.wait_ge(xfld_g.h, xfld0 + 16 * (tg + 1))
                        for hc in range(16):
                            u = 16 * tg + hc
                            t.wait_ge(sm["gh"].h, sm["gh"].v(u - 1))
                            for kc in range(8):
                                ins = t.matmul(
                                    ph[hc % 2][:, :],
                                    sw1_sb[:, kc * 2048 + hc * 128:kc * 2048 + (hc + 1) * 128],
                                    xfs.rearrange("p (q c f) -> p q c f", q=2, c=8)[:, tg % 2, kc, :],
                                    start=(kc == 0), stop=(kc == 7))
                            ins.then_inc(sm["mh"].h, 1)
                        t.wait_ge(sm["gh"].h, sm["gh"].v(16 * (tg + 1)))
                        for tt in range(4):
                            j = 4 * tg + tt
                            t.wait_ge(sm["ye"].h, sm["ye"].v(j - 1))
                            for nd in range(2):
                                for kc in range(16):
                                    t.matmul(
                                        py[tt % 2][:, nd * 512:(nd + 1) * 512],
                                        h1b[:, par * 8192 + kc * 512 + tt * 128:
                                            par * 8192 + kc * 512 + (tt + 1) * 128],
                                        sw2_sb[:, kc * 1024 + nd * 512:kc * 1024 + (nd + 1) * 512],
                                        start=(kc == 0), stop=False)
                                ins = t.matmul(py[tt % 2][:, nd * 512:(nd + 1) * 512],
                                               g["ones"][0:1, 0:128],
                                               sb2_sb[0:1, nd * 512:(nd + 1) * 512],
                                               start=False, stop=True)
                            ins.then_inc(sm["my"].h, 1)
        totB = {"ldx": 32, "mh": 16 * 8, "gh": 16 * 8, "my": 32, "ye": 32}
    else:
        accst0 = accst.n
        totB = {}

    _scC = nc.enter_named_scope(cfg["name"] + "_C", False)[0]
    H_SL = 1024
    NSLAB = H // H_SL
    NPASS_h = 8 * NSLAB
    HCQ = H_SL // 128
    TGW = 512
    NTG_h = 4
    has_b = cfg.get("EB1") is not None
    KGP = 2 if has_b else 1
    esC = ExitStack()
    ew1_sb = esC.enter_context(nc.sbuf_tensor(nm("f_ew1"), [128, 2 * 8 * H_SL], BF16))
    ew2_sb = esC.enter_context(nc.sbuf_tensor(nm("f_ew2"), [128, 2 * HCQ * 1024], BF16))
    h1c = esC.enter_context(nc.sbuf_tensor(nm("f_h1c"), [128, 2 * HCQ * TGW], BF16))
    xfh = esC.enter_context(nc.sbuf_tensor(nm("f_xfh"), [128, 8 * 2048], BF16))
    accb = esC.enter_context(nc.sbuf_tensor(nm("f_accb"), [128, 16 * 1024], F32))
    xrd = esC.enter_context(nc.sbuf_tensor(nm("f_xrd"), [128, 2 * 1024], F32))
    junkd = esC.enter_context(nc.sbuf_tensor(nm("f_junkd"), [128, 1024], BF16))
    dstat = esC.enter_context(nc.sbuf_tensor(nm("f_dstat"), [128, 16], F32))
    eb1_sb = esC.enter_context(nc.sbuf_tensor(nm("f_eb1"), [128, 2 * 8], F32))
    eb2_sb = esC.enter_context(nc.sbuf_tensor(nm("f_eb2"), [1, 1024], F32))
    ew2l = pool.gsem("wl2")
    aldx_g = pool.gsem("aldx")
    xst_g = pool.gsem("xst2")
    wl2 = wle.n
    ew2l0 = ew2l.n
    xfld1 = xfld_g.n
    aldx0 = aldx_g.n
    xst0 = xst_g.n
    mh0 = totB.get("mh", 0)
    gh0 = totB.get("gh", 0)
    my0 = totB.get("my", 0)
    accv0 = 0
    ALD_H = 16 * 16 * (2 if has_b else 1)
    SY_H = NPASS_h + (8 if has_b else 0)
    with (
        esC,
        nc.psum_tensor(nm("c_ph0"), [128, TGW], F32) as cph0,
        nc.psum_tensor(nm("c_ph1"), [128, TGW], F32) as cph1,
        nc.psum_tensor(nm("c_py0"), [128, 1024], F32) as cpy0,
        nc.psum_tensor(nm("c_py1"), [128, 1024], F32) as cpy1,
        nc.Block() as blk,
    ):
        cph = (cph0, cph1)
        cpy = (cpy0, cpy1)
        EW1v = cfg["EW1"].rearrange("ee (c p) (s f) -> ee s p c f", p=128, s=NSLAB)
        EW2v = cfg["EW2"].rearrange("ee (s c p) f -> ee s p c f", p=128, s=NSLAB)
        ew1v = ew1_sb.rearrange("p (q c f) -> p q c f", q=2, c=8)
        ew2v = ew2_sb.rearrange("p (q c f) -> p q c f", q=2, c=HCQ)
        h1cv = h1c.rearrange("p (q c f) -> p q c f", q=2, c=HCQ)
        xfhv = xfh.rearrange("p (c g f) -> p c g f", c=8, g=NTG_h)
        wsb3 = wsb.rearrange("p (je u) -> p je u", u=1)

        @blk.gpsimd
        def _(gp):
            for th in range(2):
                with gp.Fori(0, 8) as e:
                    for sl in range(NSLAB):
                        P = th * NPASS_h + NSLAB * e + sl
                        buf = sl % 2
                        gp.wait_ge(sm["my"].h, sm["my"].v(my0 + 16 * (P - 1)))
                        gp.dma_start(out=ew1v[:, buf], in_=EW1v[e, sl]).then_inc(wle.h, 16)
                        if has_b:
                            gp.dma_start(
                                out=eb1_sb[:, buf * 8:buf * 8 + 8],
                                in_=cfg["EB1"].rearrange("ee pp (s c) -> ee s pp c",
                                                         s=NSLAB)[e, sl],
                            ).then_inc(wle.h, 16)
                        gp.dma_start(out=ew2v[:, buf], in_=EW2v[e, sl]).then_inc(ew2l.h, 16)
                        if has_b and sl == NSLAB - 1:
                            gp.dma_start(out=eb2_sb[:, :], in_=cfg["EB2"][e]).then_inc(ew2l.h, 16)
            gp.wait_ge(wle.h, wl2 + 16 * KGP * 2 * NPASS_h)
            gp.wait_ge(xst_g.h, xst0 + 16 * 32)
        wle.n += 16 * KGP * 2 * NPASS_h

        @blk.sync
        def _(sy):
            for th in range(2):
                sy.wait_ge(sm["mh"].h, sm["mh"].v(mh0 + 32 * NPASS_h * th))
                sy.dma_start(out=xfh.rearrange("p (c f) -> p c f", c=8),
                             in_=XFr[:, :, th * 2048:(th + 1) * 2048]).then_inc(xfld_g.h, 16)
                if has_b:
                    for j in range(16):
                        sy.wait_ge(accst.h, accst0 + 16 * (th * 16 + j + 1))
                        if th == 1:
                            sy.wait_ge(sm["dlnv"].h, sm["dlnv"].v(j + 1))
                        sy.dma_start(out=cv(accb, 1024, j),
                                     in_=ACC_T[th * 16 + j]).then_inc(aldx_g.h, 16)
                for j in range(16):
                    jg = th * 16 + j
                    if jg >= 2:
                        sy.wait_ge(xst_g.h, xst0 + 16 * (jg - 1))
                    sy.dma_start(out=cv(xrd, 1024, j % 2), in_=XIN_T[jg]).then_inc(aldx_g.h, 16)
        xfld_g.n += 32
        aldx_g.n += 2 * ALD_H
        ew2l.n += 16 * 2 * SY_H

        @blk.scalar
        def _(s_):
            for th in range(2):
                with s_.Fori(0, 8) as e:
                    for sl in range(NSLAB):
                        P = th * NPASS_h + NSLAB * e + sl
                        buf = sl % 2
                        with s_.Fori(0, NTG_h) as tg:
                            for hc in range(HCQ):
                                u = 32 * P + 8 * tg + hc
                                s_.wait_ge(sm["mh"].h, sm["mh"].v(mh0 + u + 1))
                                if hc == 0:
                                    s_.wait_ge(sm["my"].h,
                                               sm["my"].v(my0 + 16 * P + 4 * tg - 4))
                                if has_b:
                                    s_.activation(
                                        h1cv[:, tg % 2, hc, :], cph[hc % 2][:, :], AF.Gelu,
                                        bias=eb1_sb[:, buf * 8 + hc:buf * 8 + hc + 1],
                                    ).then_inc(sm["gh"].h, 1)
                                else:
                                    s_.activation(h1cv[:, tg % 2, hc, :], cph[hc % 2][:, :],
                                                  AF.Gelu).then_inc(sm["gh"].h, 1)
                for j in range(16):
                    jg = th * 16 + j
                    p2 = jg % 2
                    accj = cv(accb, 1024, j)
                    s_.wait_ge(sm["dr"].h, sm["dr"].v(jg + 1))
                    if jg >= 2:
                        s_.wait_ge(sm["dssq"].h, sm["dssq"].v(jg - 1))
                        s_.wait_ge(sm["dsrs"].h, sm["dsrs"].v(2 * jg - 2))
                        s_.wait_ge(sm["dvvar"].h, sm["dvvar"].v(jg - 1))
                    s_.activation(junkd[:, :], accj, AF.Square,
                                  accum_out=dstat[:, 12 + p2:13 + p2]).then_inc(sm["dssq"].h, 1)
                    s_.wait_ge(sm["dvvar"].h, sm["dvvar"].v(jg + 1))
                    s_.activation(dstat[:, 8 + p2:9 + p2], dstat[:, 6 + p2:7 + p2],
                                  AF.Ln, bias=g["epsl"]).then_inc(sm["dsrs"].h, 1)
                    s_.wait_ge(sm["dsrs"].h, sm["dsrs"].v(2 * jg + 1))
                    s_.activation(dstat[:, 10 + p2:11 + p2], dstat[:, 8 + p2:9 + p2],
                                  AF.Exp, scale=-0.5).then_inc(sm["dsrs"].h, 1)
                    s_.wait_ge(sm["dlnv"].h, sm["dlnv"].v(jg + 1))
                    s_.dma_start(out=XOUT_T[jg], in_=cv(xrd, 1024, p2)).then_inc(xst_g.h, 16)

        @blk.vector
        def _(v):
            for th in range(2):
                if has_b:
                    v.wait_ge(aldx_g.h, aldx0 + th * ALD_H + 16 * 16)
                else:
                    for q in range(4):
                        v.memset(cv(accb, 4096, q), 0.0)
                with v.Fori(0, 8) as e:
                    for sl in range(NSLAB):
                        P = th * NPASS_h + NSLAB * e + sl
                        with v.Fori(0, NTG_h) as tg:
                            for tt in range(4):
                                un = 16 * P + 4 * tg + tt
                                v.wait_ge(sm["my"].h, sm["my"].v(my0 + un + 1))
                                v.scalar_tensor_tensor(
                                    cv(accb, 1024, 4 * tg + tt), cpy[tt % 2][:, :],
                                    wsb3[:, (th * 16 + 4 * tg + tt) * 8 + e, :],
                                    cv(accb, 1024, 4 * tg + tt),
                                    ALU.mult, ALU.add).then_inc(sm["accv"].h, 1)
                for j in range(16):
                    jg = th * 16 + j
                    p2 = jg % 2
                    accj = cv(accb, 1024, j)
                    v.wait_ge(aldx_g.h, aldx0 + th * ALD_H
                              + (256 if has_b else 0) + 16 * (j + 1))
                    if jg >= 2:
                        v.wait_ge(xst_g.h, xst0 + 16 * (jg - 1))
                    v.tensor_add(accj, accj, cv(xrd, 1024, p2)).then_inc(sm["dr"].h, 1)
                    v.reduce_sum(dstat[:, p2:p2 + 1], accj, AX.X).then_inc(sm["ds3"].h, 1)
                    v.wait_ge(sm["ds3"].h, sm["ds3"].v(3 * jg + 1))
                    v.tensor_scalar(dstat[:, 2 + p2:3 + p2], dstat[:, p2:p2 + 1],
                                    -1.0 / D, None, ALU.mult).then_inc(sm["ds3"].h, 1)
                    v.wait_ge(sm["ds3"].h, sm["ds3"].v(3 * jg + 2))
                    v.tensor_mul(dstat[:, 4 + p2:5 + p2], dstat[:, 2 + p2:3 + p2],
                                 dstat[:, 2 + p2:3 + p2]).then_inc(sm["ds3"].h, 1)
                    v.wait_ge(sm["dssq"].h, sm["dssq"].v(jg + 1))
                    v.wait_ge(sm["ds3"].h, sm["ds3"].v(3 * jg + 3))
                    v.tensor_scalar(dstat[:, 6 + p2:7 + p2], dstat[:, 12 + p2:13 + p2],
                                    1.0 / D, dstat[:, 4 + p2:5 + p2],
                                    ALU.mult, ALU.subtract).then_inc(sm["dvvar"].h, 1)
                    v.wait_ge(sm["dsrs"].h, sm["dsrs"].v(2 * jg + 2))
                    v.tensor_scalar(cv(xrd, 1024, p2), accj,
                                    dstat[:, 2 + p2:3 + p2], dstat[:, 10 + p2:11 + p2],
                                    ALU.add, ALU.mult).then_inc(sm["dlnv"].h, 1)
            v.wait_ge(xst_g.h, xst0 + 16 * 32)
        xst_g.n += 16 * 32

        @blk.tensor
        def _(t):
            for th in range(2):
                t.wait_ge(xfld_g.h, xfld1 + 16 * (th + 1))
                with t.Fori(0, 8) as e:
                    for sl in range(NSLAB):
                        P = th * NPASS_h + NSLAB * e + sl
                        buf = sl % 2
                        t.wait_ge(wle.h, wl2 + 16 * KGP * (th * NPASS_h + NSLAB * e + sl + 1))
                        if has_b:
                            t.wait_ge(ew2l.h, ew2l0 + 16 * (th * SY_H + 3 * e + sl + 1
                                                            + (1 if sl == NSLAB - 1 else 0)))
                        else:
                            t.wait_ge(ew2l.h, ew2l0 + 16 * (th * SY_H + NSLAB * e + sl + 1))
                        with t.Fori(0, NTG_h // 2) as tg2:
                            for par in range(2):
                                tg = 2 * tg2 + par
                                for hc in range(HCQ):
                                    u = 32 * P + 8 * tg + hc
                                    t.wait_ge(sm["gh"].h, sm["gh"].v(gh0 + u - 1))
                                    for kc in range(8):
                                        ins = t.matmul(
                                            cph[hc % 2][:, :],
                                            ew1v[:, buf, kc, hc * 128:(hc + 1) * 128],
                                            xfhv[:, kc, tg, :],
                                            start=(kc == 0), stop=(kc == 7))
                                    ins.then_inc(sm["mh"].h, 1)
                                t.wait_ge(sm["gh"].h, sm["gh"].v(gh0 + 32 * P + 8 * tg + 8))
                                for tt in range(4):
                                    un = 16 * P + 4 * tg + tt
                                    t.wait_ge(sm["accv"].h, sm["accv"].v(accv0 + un - 1))
                                    for nd in range(2):
                                        for kc in range(HCQ):
                                            ins = t.matmul(
                                                cpy[tt % 2][:, nd * 512:(nd + 1) * 512],
                                                h1cv[:, par, kc, tt * 128:(tt + 1) * 128],
                                                ew2v[:, buf, kc, nd * 512:(nd + 1) * 512],
                                                start=(kc == 0),
                                                stop=(not (has_b and sl == NSLAB - 1)
                                                      and kc == HCQ - 1))
                                        if has_b and sl == NSLAB - 1:
                                            ins = t.matmul(
                                                cpy[tt % 2][:, nd * 512:(nd + 1) * 512],
                                                g["ones"][0:1, 0:128],
                                                eb2_sb[0:1, nd * 512:(nd + 1) * 512],
                                                start=False, stop=True)
                                    ins.then_inc(sm["my"].h, 1)
    nc.leave_named_scope(cfg["name"] + "_C", _scC, False)
    totC = {"mh": 64 * NPASS_h, "gh": 64 * NPASS_h, "my": 32 * NPASS_h,
            "accv": 32 * NPASS_h}
    esP.close()
    totD = {"dr": 32, "ds3": 96, "dssq": 32, "dvvar": 32, "dsrs": 64, "dlnv": 32}

    tot = {}
    for d in (totA, totB, totC, totD):
        for k, v_ in d.items():
            tot[k] = tot.get(k, 0) + v_
    pool.finish(tot)


def emit_attn_l1(nc, pool, g, cfg):
    sm = pool.take(["ql", "ttk", "etk", "ms", "rv", "se", "vm", "vsL", "pn",
                    "tp2", "ep", "mo", "oe", "radd", "mpj", "xld", "xst",
                    "ssq2", "srs", "vvar", "lnv", "vs3", "aotd", "aofd",
                    "aostp", "aoldp", "xrlp"])
    wl = pool.gsem("wl")
    aost = sm["aostp"]
    aold = sm["aoldp"]
    QS, KS, VS, XRES = cfg["QS"], cfg["KS"], cfg["VS"], cfg["XRES"]
    WO, BO, XOUT = cfg["WO"], cfg["BO"], cfg["XOUT"]
    AO = nc.dram_tensor(cfg["name"] + "_ao", [T, D], BF16, kind="Internal").ap()
    if cfg.get("dbg_ao"):
        AODBG = nc.dram_tensor(cfg["name"] + "_aodbg", [T, D], F32, kind="ExternalOutput").ap()
    NG = cfg.get("ng", T * 16 // 128)
    NGQ = NG // 4
    wl0, aost0, aold0 = wl.n, 0, 0
    Q2 = QS.rearrange("t (h d) -> (t h) d", d=64).rearrange("(i j p) d -> i p j d", j=4, p=128)
    K2 = KS.rearrange("t (h d) -> (t h) d", d=64).rearrange("(i j p) d -> i p j d", j=4, p=128)
    V2 = VS.rearrange("t (h d) -> (t h) d", d=64).rearrange("(i j p) d -> i p j d", j=4, p=128)
    AO2 = AO.rearrange("t (h d) -> (t h) d", d=64).rearrange("(i j p) d -> i p j d", j=4, p=128)
    AOT = AO.rearrange("(t p) d -> t p d", p=128)
    XRT = XRES.rearrange("(t p) d -> t p d", p=128)
    XOT = XOUT.rearrange("(t p) d -> t p d", p=128)

    es = ExitStack()
    qg = es.enter_context(nc.sbuf_tensor(nm("m_qg"), [128, 8 * 256], BF16))
    kg = es.enter_context(nc.sbuf_tensor(nm("m_kg"), [128, 8 * 256], BF16))
    vg = es.enter_context(nc.sbuf_tensor(nm("m_vg"), [128, 8 * 256], BF16))
    tkq = es.enter_context(nc.sbuf_tensor(nm("m_tkq"), [64, 8 * 1024], BF16))
    Ef = es.enter_context(nc.sbuf_tensor(nm("m_Ef"), [128, 8 * 512], F32))
    Em = es.enter_context(nc.sbuf_tensor(nm("m_Em"), [128, 8 * 512], BF16))
    PnT = es.enter_context(nc.sbuf_tensor(nm("m_PnT"), [128, 8 * 512], BF16))
    aosb = es.enter_context(nc.sbuf_tensor(nm("m_ao"), [128, 8 * 256], BF16))
    msksb = es.enter_context(nc.sbuf_tensor(nm("m_msk"), [128, 512], F32))
    gstat = es.enter_context(nc.sbuf_tensor(nm("m_gs"), [128, 128], F32))
    wo_sb = es.enter_context(nc.sbuf_tensor(nm("m_wo"), [128, 8 * 1024], BF16))
    bo_sb = es.enter_context(nc.sbuf_tensor(nm("m_bo"), [1, 1024], F32))
    aot = es.enter_context(nc.sbuf_tensor(nm("m_aot"), [128, 2 * 1024], BF16))
    afm = es.enter_context(nc.sbuf_tensor(nm("m_afm"), [128, 2 * 1024], BF16))
    xrt = es.enter_context(nc.sbuf_tensor(nm("m_xrt"), [128, 2 * 1024], F32))
    rt = es.enter_context(nc.sbuf_tensor(nm("m_rt"), [128, 2 * 1024], F32))
    ob = es.enter_context(nc.sbuf_tensor(nm("m_ob"), [128, 2 * 1024], F32))
    junkP = es.enter_context(nc.sbuf_tensor(nm("m_junk"), [128, 2 * 1024], BF16))
    pstat = es.enter_context(nc.sbuf_tensor(nm("m_ps"), [128, 16], F32))
    with (
        es,
        nc.psum_tensor(nm("m_pS0"), [128, 512], F32) as pS0,
        nc.psum_tensor(nm("m_pS1"), [128, 512], F32) as pS1,
        nc.psum_tensor(nm("m_pTK0"), [64, 1024], BF16) as pTK0,
        nc.psum_tensor(nm("m_pTK1"), [64, 1024], BF16) as pTK1,
        nc.psum_tensor(nm("m_pP0"), [128, 512], BF16) as pP0,
        nc.psum_tensor(nm("m_pP1"), [128, 512], BF16) as pP1,
        nc.psum_tensor(nm("m_pO0"), [128, 256], F32) as pO0,
        nc.psum_tensor(nm("m_pO1"), [128, 256], F32) as pO1,
        nc.Block() as blk,
    ):
        pS = (pS0, pS1)
        pTK = (pTK0, pTK1)
        pP = (pP0, pP1)
        pO = (pO0, pO1)

        @blk.gpsimd
        def _(gp):
            gp.dma_start(out=wo_sb.rearrange("p (c f) -> p c f", c=8),
                         in_=WO.rearrange("(c p) f -> p c f", p=128)).then_inc(wl.h, 16)
            with gp.Fori(0, NGQ // 8) as i4:
              for par in range(8):
                i = 8 * i4 + par
                gp.wait_ge(sm["oe"].h, sm["oe"].v(i + 1))
                gp.dma_start(out=AO2[i],
                             in_=cv(aosb, 256, par).rearrange("p (j d) -> p j d", j=4),
                             ).then_inc(aost.h, 16)
            gp.wait_ge(aost.h, aost.v(aost0 + 16 * NGQ))
            gp.wait_ge(wl.h, wl0 + 16)
        wl.n += 16

        @blk.sync
        def _(sy):
            sy.dma_start(out=bo_sb[:, :], in_=BO).then_inc(sm["xld"].h, 16)
            sy.dma_start(out=msksb[:, :], in_=cfg["MASK"]).then_inc(sm["xld"].h, 16)
            with sy.Fori(0, NGQ // 8) as i4:
              for par in range(8):
                i = 8 * i4 + par
                sy.wait_ge(sm["mo"].h, sm["mo"].v(i - 7))
                sy.dma_start(out=cv(qg, 256, par).rearrange("p (j d) -> p j d", j=4),
                             in_=Q2[i]).then_inc(sm["ql"].h, 16)
                sy.dma_start(out=cv(kg, 256, par).rearrange("p (j d) -> p j d", j=4),
                             in_=K2[i]).then_inc(sm["ql"].h, 16)
                sy.dma_start(out=cv(vg, 256, par).rearrange("p (j d) -> p j d", j=4),
                             in_=V2[i]).then_inc(sm["ql"].h, 16)


        @blk.scalar
        def _(s_):
            with s_.Fori(0, NGQ // 8) as i4:
              for par in range(8):
                i = 8 * i4 + par
                gb = par * 16
                s_.wait_ge(sm["rv"].h, sm["rv"].v(i + 1))
                s_.wait_ge(sm["vm"].h, sm["vm"].v(i - 7))
                s_.activation(cv(Ef, 512, par), pS[par % 2][:, :], AF.Exp,
                              bias=gstat[:, gb:gb + 1]).then_inc(sm["se"].h, 1)

        @blk.vector
        def _(v):
            with v.Fori(0, NGQ // 8) as i4:
              for par in range(8):
                i = 8 * i4 + par
                gb = par * 16
                v.wait_ge(sm["ttk"].h, sm["ttk"].v(i + 1))
                v.wait_ge(sm["ms"].h, sm["ms"].v(4 * (i - 7)))
                v.tensor_copy(cv(tkq, 1024, par), pTK[par % 2][:, :]).then_inc(sm["etk"].h, 1)
                v.wait_ge(sm["ms"].h, sm["ms"].v(4 * (i + 1)))
                v.wait_ge(sm["se"].h, sm["se"].v(i - 7))
                v.wait_ge(sm["oe"].h, sm["oe"].v(i - 7))
                v.reduce_max(gstat[:, gb:gb + 1], pS[par % 2][:, :], AX.X,
                             negate=True).then_inc(sm["rv"].h, 1)
                v.wait_ge(sm["se"].h, sm["se"].v(i + 1))
                v.wait_ge(sm["tp2"].h, sm["tp2"].v(i - 7))
                v.scalar_tensor_tensor(cv(Em, 512, par), cv(Ef, 512, par), 1.0,
                                       msksb[:, :], ALU.mult, ALU.mult).then_inc(sm["vm"].h, 1)
                for j in range(4):
                    ins = v.reduce_sum(gstat[:, gb + 1 + j:gb + 2 + j],
                                       Em[:, par * 512 + j * 128:par * 512 + (j + 1) * 128],
                                       AX.X)
                ins.then_inc(sm["vsL"].h, 1)
                v.wait_ge(sm["vsL"].h, sm["vsL"].v(2 * i + 1))
                v.reciprocal(gstat[:, gb + 5:gb + 9],
                             gstat[:, gb + 1:gb + 5]).then_inc(sm["vsL"].h, 1)
                v.wait_ge(sm["tp2"].h, sm["tp2"].v(i + 1))
                v.wait_ge(sm["mo"].h, sm["mo"].v(i - 7))
                v.tensor_copy(cv(PnT, 512, par), pP[par % 2][:, :]).then_inc(sm["ep"].h, 1)
                v.wait_ge(sm["vsL"].h, sm["vsL"].v(2 * i + 2))
                v.wait_ge(sm["mo"].h, sm["mo"].v(i + 1))
                v.wait_ge(aost.h, aost.v(aost0 + 16 * (i - 7)))
                for j in range(4):
                    ins = v.tensor_scalar(
                        aosb[:, par * 256 + j * 64:par * 256 + (j + 1) * 64],
                        pO[par % 2][:, j * 64:(j + 1) * 64],
                        gstat[:, gb + 5 + j:gb + 6 + j], None, ALU.mult)
                ins.then_inc(sm["oe"].h, 1)

        @blk.tensor
        def _(t):
            t.wait_ge(sm["xld"].h, sm["xld"].v(32))
            t.wait_ge(wl.h, wl0 + 16)
            with t.Fori(0, NGQ // 8) as i4:
              for par in range(8):
                i = 8 * i4 + par
                t.wait_ge(sm["ql"].h, sm["ql"].v(48 * (i + 1)))
                t.wait_ge(sm["etk"].h, sm["etk"].v(i - 1))
                for j in range(4):
                    t.transpose(pTK[par % 2][0:64, j * 128:(j + 1) * 128],
                                qg[:, par * 256 + j * 64:par * 256 + (j + 1) * 64],
                                g["idn"][:, :])
                for j in range(4):
                    ins = t.transpose(pTK[par % 2][0:64, 512 + j * 128:512 + (j + 1) * 128],
                                      kg[:, par * 256 + j * 64:par * 256 + (j + 1) * 64],
                                      g["idn"][:, :])
                ins.then_inc(sm["ttk"].h, 1)
                t.wait_ge(sm["etk"].h, sm["etk"].v(i + 1))
                t.wait_ge(sm["se"].h, sm["se"].v(i - 1))
                for j in range(4):
                    t.matmul(pS[par % 2][:, j * 128:(j + 1) * 128],
                             tkq[0:64, par * 1024 + j * 128:par * 1024 + (j + 1) * 128],
                             tkq[0:64, par * 1024 + 512 + j * 128:par * 1024 + 512 + (j + 1) * 128],
                             start=True, stop=True).then_inc(sm["ms"].h, 1)
                t.wait_ge(sm["vm"].h, sm["vm"].v(i + 1))
                t.wait_ge(sm["ep"].h, sm["ep"].v(i - 1))
                for j in range(4):
                    ins = t.transpose(pP[par % 2][:, j * 128:(j + 1) * 128],
                                      Em[:, par * 512 + j * 128:par * 512 + (j + 1) * 128],
                                      g["idn"][:, :])
                ins.then_inc(sm["tp2"].h, 1)
                t.wait_ge(sm["ep"].h, sm["ep"].v(i + 1))
                t.wait_ge(sm["oe"].h, sm["oe"].v(i - 1))
                for j in range(4):
                    ins = t.matmul(pO[par % 2][:, j * 64:(j + 1) * 64],
                                   PnT[:, par * 512 + j * 128:par * 512 + (j + 1) * 128],
                                   vg[:, par * 256 + j * 64:par * 256 + (j + 1) * 64],
                                   start=True, stop=True)
                ins.then_inc(sm["mo"].h, 1)
    totg = {"ql": 48 * NGQ, "ttk": NGQ, "etk": NGQ, "ms": 4 * NGQ,
            "rv": NGQ, "se": NGQ, "vm": NGQ, "vsL": 2 * NGQ,
            "tp2": NGQ, "ep": NGQ, "mo": NGQ, "oe": NGQ,
            "aostp": 16 * NGQ, "xld": 32}
    if cfg.get("skip_proj"):
        gdmp = pool.gsem("dump")
        with nc.Block() as blkd:
            @blkd.gpsimd
            def _(gp):
                gp.dma_start(out=AODBG, in_=AO).then_inc(gdmp.h, 16)
                gdmp.n += 16
                gp.wait_ge(gdmp.h, gdmp.n)
        pool.finish(totg)
        return
    if cfg.get("dbg_ao"):
        gdmp = pool.gsem("dump")
        with nc.Block() as blkd:
            @blkd.gpsimd
            def _(gp):
                gp.dma_start(out=AODBG, in_=AO).then_inc(gdmp.h, 16)
                gdmp.n += 16
                gp.wait_ge(gdmp.h, gdmp.n)
    with (
        nc.psum_tensor(nm("m_po0"), [128, 1024], F32) as po0,
        nc.psum_tensor(nm("m_po1"), [128, 1024], F32) as po1,
        nc.psum_tensor(nm("m_pt0"), [128, 1024], BF16) as pt0,
        nc.psum_tensor(nm("m_pt1"), [128, 1024], BF16) as pt1,
        nc.Block() as blk2,
    ):
        po = (po0, po1)
        pt2 = (pt0, pt1)

        @blk2.sync
        def _(sy):
            for tt in range(32):
                p2 = tt % 2
                if tt >= 2:
                    sy.wait_ge(sm["aotd"].h, sm["aotd"].v(tt - 1))
                sy.dma_start(out=cv(aot, 1024, p2), in_=AOT[tt]).then_inc(aold.h, 16)
                if tt >= 2:
                    sy.wait_ge(sm["radd"].h, sm["radd"].v(tt - 1))
                sy.dma_start(out=cv(xrt, 1024, p2), in_=XRT[tt]).then_inc(sm["xrlp"].h, 16)
                sy.wait_ge(sm["lnv"].h, sm["lnv"].v(tt + 1))
                sy.dma_start(out=XOT[tt], in_=cv(ob, 1024, p2)).then_inc(sm["xst"].h, 16)
            sy.wait_ge(sm["xst"].h, sm["xst"].v(16 * 32))

        @blk2.scalar
        def _(s_):
            for tt in range(32):
                p2 = tt % 2
                s_.wait_ge(sm["radd"].h, sm["radd"].v(tt + 1))
                if tt >= 2:
                    s_.wait_ge(sm["ssq2"].h, sm["ssq2"].v(tt - 1))
                    s_.wait_ge(sm["srs"].h, sm["srs"].v(2 * tt - 2))
                    s_.wait_ge(sm["vvar"].h, sm["vvar"].v(tt - 1))
                s_.activation(cv(junkP, 1024, p2), cv(rt, 1024, p2), AF.Square,
                              accum_out=pstat[:, 12 + p2:13 + p2]).then_inc(sm["ssq2"].h, 1)
                s_.wait_ge(sm["vvar"].h, sm["vvar"].v(tt + 1))
                s_.activation(pstat[:, 8 + p2:9 + p2], pstat[:, 6 + p2:7 + p2],
                              AF.Ln, bias=g["epsl"]).then_inc(sm["srs"].h, 1)
                s_.wait_ge(sm["srs"].h, sm["srs"].v(2 * tt + 1))
                s_.activation(pstat[:, 10 + p2:11 + p2], pstat[:, 8 + p2:9 + p2],
                              AF.Exp, scale=-0.5).then_inc(sm["srs"].h, 1)

        @blk2.vector
        def _(v):
            for tt in range(32):
                p2 = tt % 2
                v.wait_ge(sm["aotd"].h, sm["aotd"].v(tt + 1))
                if tt >= 2:
                    v.wait_ge(sm["mpj"].h, sm["mpj"].v(2 * tt - 2))
                v.tensor_copy(cv(afm, 1024, p2), pt2[p2][:, :]).then_inc(sm["aofd"].h, 1)
                v.wait_ge(sm["mpj"].h, sm["mpj"].v(2 * tt + 2))
                v.wait_ge(aold.h, aold.v(aold0 + 16 * (tt + 1)))
                v.wait_ge(sm["xrlp"].h, sm["xrlp"].v(16 * (tt + 1)))
                if tt >= 2:
                    v.wait_ge(sm["xst"].h, sm["xst"].v(16 * (tt - 1)))
                v.tensor_add(cv(rt, 1024, p2), po[p2][:, :],
                             cv(xrt, 1024, p2)).then_inc(sm["radd"].h, 1)
                v.wait_ge(sm["radd"].h, sm["radd"].v(tt + 1))
                v.reduce_sum(pstat[:, p2:p2 + 1], cv(rt, 1024, p2), AX.X).then_inc(sm["vs3"].h, 1)
                v.wait_ge(sm["vs3"].h, sm["vs3"].v(3 * tt + 1))
                v.tensor_scalar(pstat[:, 2 + p2:3 + p2], pstat[:, p2:p2 + 1],
                                -1.0 / D, None, ALU.mult).then_inc(sm["vs3"].h, 1)
                v.wait_ge(sm["vs3"].h, sm["vs3"].v(3 * tt + 2))
                v.tensor_mul(pstat[:, 4 + p2:5 + p2], pstat[:, 2 + p2:3 + p2],
                             pstat[:, 2 + p2:3 + p2]).then_inc(sm["vs3"].h, 1)
                v.wait_ge(sm["ssq2"].h, sm["ssq2"].v(tt + 1))
                v.wait_ge(sm["vs3"].h, sm["vs3"].v(3 * tt + 3))
                v.tensor_scalar(pstat[:, 6 + p2:7 + p2], pstat[:, 12 + p2:13 + p2],
                                1.0 / D, pstat[:, 4 + p2:5 + p2],
                                ALU.mult, ALU.subtract).then_inc(sm["vvar"].h, 1)
                v.wait_ge(sm["srs"].h, sm["srs"].v(2 * tt + 2))
                v.tensor_scalar(cv(ob, 1024, p2), cv(rt, 1024, p2),
                                pstat[:, 2 + p2:3 + p2], pstat[:, 10 + p2:11 + p2],
                                ALU.add, ALU.mult).then_inc(sm["lnv"].h, 1)

        @blk2.tensor
        def _(t):
            for tt in range(32):
                p2 = tt % 2
                t.wait_ge(aold.h, aold.v(aold0 + 16 * (tt + 1)))
                if tt >= 2:
                    t.wait_ge(sm["aofd"].h, sm["aofd"].v(tt - 1))
                for c in range(8):
                    ins = t.transpose(cv(pt2[p2], 128, c),
                                      aot[:, p2 * 1024 + c * 128:p2 * 1024 + (c + 1) * 128],
                                      g["idn"][:, :])
                ins.then_inc(sm["aotd"].h, 1)
                t.wait_ge(sm["aofd"].h, sm["aofd"].v(tt + 1))
                if tt >= 2:
                    t.wait_ge(sm["radd"].h, sm["radd"].v(tt - 1))
                for nd in range(2):
                    for kc in range(8):
                        t.matmul(po[p2][:, nd * 512:(nd + 1) * 512],
                                 afm[:, p2 * 1024 + kc * 128:p2 * 1024 + (kc + 1) * 128],
                                 wo_sb[:, kc * 1024 + nd * 512:kc * 1024 + (nd + 1) * 512],
                                 start=(kc == 0), stop=False)
                    t.matmul(po[p2][:, nd * 512:(nd + 1) * 512],
                             g["ones"][0:1, 0:128], bo_sb[0:1, nd * 512:(nd + 1) * 512],
                             start=False, stop=True).then_inc(sm["mpj"].h, 1)
    tot2 = dict(totg)
    tot2.update({"radd": 32, "mpj": 64, "xst": 16 * 32,
                 "ssq2": 32, "srs": 64, "vvar": 32, "lnv": 32, "vs3": 96,
                 "aotd": 32, "aofd": 32, "aoldp": 16 * 32, "xrlp": 16 * 32})
    pool.finish(tot2)


from contextlib import ExitStack


def _new_prog():
    nc = bacc.Bacc("TRN2", target_bir_lowering=False)
    ctx = ExitStack()
    pool = Pool(nc, ctx)
    return nc, ctx, pool


def _global_prologue(nc, ctx, pool):
    IDENT = nc.dram_tensor("c_ident", [128, 128], F32, kind="ExternalInput").ap()
    pool.seed_src = IDENT[0:1, 0:16]
    ONES = nc.dram_tensor("c_ones1", [1, 512], F32, kind="ExternalInput").ap()
    idn = ctx.enter_context(nc.sbuf_tensor(nm("g_idn"), [128, 128], BF16))
    ones = ctx.enter_context(nc.sbuf_tensor(nm("g_ones"), [1, 512], F32))
    eps = ctx.enter_context(nc.sbuf_tensor(nm("g_eps"), [128, 2], F32))
    sp = pool.take(["pro"])["pro"]
    gw = pool.gsem("wl")
    gnames = ["wl", "qst", "vst", "xfst", "accst", "xfld", "aldx", "xst2", "dump"]
    with nc.Block() as blk:
        @blk.gpsimd
        def _(gp):
            gp.dma_start(out=idn[:, :], in_=IDENT).then_inc(gw.h, 16)
            gw.n += 16
            for nm in gnames:
                gs = pool.gsem(nm)
                pool.seed_dma(gp, gs, 4)
            for nm in gnames:
                gs = pool.gsem(nm)
                gp.wait_ge(gs.h, gs.n)
            gp.wait_ge(gw.h, gw.n)
        @blk.sync
        def _(sy):
            sy.dma_start(out=ones[:, :], in_=ONES).then_inc(sp.h, 16)
            sy.wait_ge(sp.h, sp.v(16))
            for s_ in pool.sems:
                sy.sem_inc(s_, 64)
        @blk.vector
        def _(v):
            v.memset(eps[:, 0:1], EPS_RMS)
            v.memset(eps[:, 1:2], EPS_LN)
    pool.counts = [c + 64 for c in pool.counts]
    pool.finish({"pro": 16})
    return {"idn": idn, "ones": ones, "epsr": eps[:, 0:1], "epsl": eps[:, 1:2]}


def _qkv_cfg(nc, layer, xs_ap=None):
    l = f"l{layer}"
    kind = "Internal"
    tm = (layer == 1)
    cfg = dict(
        rms=(layer == 0),
        tm_qk=tm,
        qscale=0.125,
        XS=xs_ap if xs_ap is not None else
            nc.dram_tensor(f"{l}_xs_in", [T, D], F32, kind="ExternalInput").ap(),
        WQ=nc.dram_tensor(f"{l}_wq", [D, D], F32, kind="ExternalInput").ap(),
        WK=nc.dram_tensor(f"{l}_wk", [D, D], F32, kind="ExternalInput").ap(),
        WV=nc.dram_tensor(f"{l}_wv", [D, D], F32, kind="ExternalInput").ap(),
        GAM=nc.dram_tensor(f"{l}_gam", [128, 8], F32, kind="ExternalInput").ap(),
        QS=nc.dram_tensor(f"{l}_qsc", [T, D] if tm else [D, T], BF16, kind=kind).ap(),
        KS=nc.dram_tensor(f"{l}_ksc", [T, D] if tm else [D, T], BF16, kind=kind).ap(),
        VS=nc.dram_tensor(f"{l}_vsc", [T, D], BF16, kind=kind).ap(),
    )
    return cfg


def _dump_f32(nc, pool, pairs):
    gd = pool.gsem("dump")
    with nc.Block() as blk:
        @blk.gpsimd
        def _(gp):
            for name, src, shape in pairs:
                dst = nc.dram_tensor(name, list(shape), F32, kind="ExternalOutput").ap()
                gp.dma_start(out=dst, in_=src).then_inc(gd.h, 16)
                gd.n += 16
            gp.wait_ge(gd.h, gd.n)


def _attn_cfg(nc, layer, qkv_cfg, xout_kind="Internal"):
    l = f"l{layer}"
    cfg = dict(
        name=f"{l}_attn",
        QS=qkv_cfg["QS"], KS=qkv_cfg["KS"], VS=qkv_cfg["VS"],
        XRES=qkv_cfg["XS"],
        WO=nc.dram_tensor(f"{l}_wo", [D, D], F32, kind="ExternalInput").ap(),
        BO=nc.dram_tensor(f"{l}_bo", [1, D], F32, kind="ExternalInput").ap(),
        XOUT=nc.dram_tensor(f"{l}_xattn_out", [T, D], F32, kind=xout_kind).ap(),
    )
    if layer == 1:
        cfg["MASK"] = nc.dram_tensor("c_mask", [128, 512], F32, kind="ExternalInput").ap()
    return cfg


def build_attn_test(layer=0):
    nc, ctx, pool = _new_prog()
    g = _global_prologue(nc, ctx, pool)
    qcfg = _qkv_cfg(nc, layer)
    emit_qkv(nc, pool, g, qcfg)
    acfg = _attn_cfg(nc, layer, qcfg, xout_kind="ExternalOutput")
    if layer == 0:
        emit_attn(nc, pool, g, acfg)
    else:
        emit_attn_l1(nc, pool, g, acfg)
    nc.finalize()
    return nc, ctx


def _ffn_cfg(nc, layer, xin_ap, xout_kind="Internal"):
    l = f"l{layer}"
    if layer == 0:
        return dict(
            layer=0, H=2048, shared=True, name=f"{l}_ffn",
            XIN=xin_ap,
            XOUT=nc.dram_tensor(f"{l}_xffn_out", [T, D], F32, kind=xout_kind).ap(),
            RW=nc.dram_tensor(f"{l}_rw", [D, 8], F32, kind="ExternalInput").ap(),
            RB=nc.dram_tensor(f"{l}_rb", [1, 8], F32, kind="ExternalInput").ap(),
            SW1=nc.dram_tensor(f"{l}_sw1", [D, 2048], F32, kind="ExternalInput").ap(),
            SB1=nc.dram_tensor(f"{l}_sb1", [128, 16], F32, kind="ExternalInput").ap(),
            SW2=nc.dram_tensor(f"{l}_sw2", [2048, D], F32, kind="ExternalInput").ap(),
            SB2=nc.dram_tensor(f"{l}_sb2", [1, D], F32, kind="ExternalInput").ap(),
            EW1=nc.dram_tensor(f"{l}_ew1", [8, D, 2048], F32, kind="ExternalInput").ap(),
            EB1=nc.dram_tensor(f"{l}_eb1", [8, 128, 16], F32, kind="ExternalInput").ap(),
            EW2=nc.dram_tensor(f"{l}_ew2", [8, 2048, D], F32, kind="ExternalInput").ap(),
            EB2=nc.dram_tensor(f"{l}_eb2", [8, 1, D], F32, kind="ExternalInput").ap(),
        )
    return dict(
        layer=1, H=4096, shared=False, name=f"{l}_ffn",
        XIN=xin_ap,
        XOUT=nc.dram_tensor(f"{l}_xffn_out", [T, D], F32, kind=xout_kind).ap(),
        RW=nc.dram_tensor(f"{l}_rw", [D, 8], F32, kind="ExternalInput").ap(),
        RB=None,
        EW1=nc.dram_tensor(f"{l}_ew1", [8, D, 4096], F32, kind="ExternalInput").ap(),
        EB1=None,
        EW2=nc.dram_tensor(f"{l}_ew2", [8, 4096, D], F32, kind="ExternalInput").ap(),
        EB2=None,
    )


def build_ffn_test(layer=0, debug=False):
    nc, ctx, pool = _new_prog()
    g = _global_prologue(nc, ctx, pool)
    xin = nc.dram_tensor("ffn_xin", [T, D], F32, kind="ExternalInput").ap()
    cfg = _ffn_cfg(nc, layer, xin, xout_kind="ExternalOutput")
    cfg["debug"] = debug
    emit_ffn(nc, pool, g, cfg)
    nc.finalize()
    return nc, ctx


def build_qkv_test(layer=0, ntg=8):
    nc, ctx, pool = _new_prog()
    g = _global_prologue(nc, ctx, pool)
    cfg = _qkv_cfg(nc, layer)
    cfg["ntg"] = ntg
    emit_qkv(nc, pool, g, cfg)
    l = f"l{layer}"
    _dump_f32(nc, pool, [(f"{l}_qs_o", cfg["QS"], (D, T)),
                         (f"{l}_ks_o", cfg["KS"], (D, T)),
                         (f"{l}_vs_o", cfg["VS"], (T, D))])
    nc.finalize()
    return nc, ctx


def build_full(nstages=4):
    nc, ctx, pool = _new_prog()
    g = _global_prologue(nc, ctx, pool)
    x0 = nc.dram_tensor("x0", [T, D], F32, kind="ExternalInput").ap()
    q0 = _qkv_cfg(nc, 0, xs_ap=x0)
    with nc.named_scope("qkv0"):
        emit_qkv(nc, pool, g, q0)
    a0 = _attn_cfg(nc, 0, q0, xout_kind="ExternalOutput" if nstages == 1 else "Internal")
    with nc.named_scope("attn0"):
        emit_attn(nc, pool, g, a0)
    out = "l0_xattn_out"
    if nstages >= 2:
        f0 = _ffn_cfg(nc, 0, a0["XOUT"], xout_kind="ExternalOutput" if nstages == 2 else "Internal")
        with nc.named_scope("ffn0"):
            emit_ffn(nc, pool, g, f0)
        out = "l0_xffn_out"
    if nstages >= 3:
        q1 = _qkv_cfg(nc, 1, xs_ap=f0["XOUT"])
        with nc.named_scope("qkv1"):
            emit_qkv(nc, pool, g, q1)
        a1 = _attn_cfg(nc, 1, q1, xout_kind="ExternalOutput" if nstages == 3 else "Internal")
        with nc.named_scope("attn1"):
            emit_attn_l1(nc, pool, g, a1)
        out = "l1_xattn_out"
    if nstages >= 4:
        f1 = _ffn_cfg(nc, 1, a1["XOUT"], xout_kind="ExternalOutput")
        with nc.named_scope("ffn1"):
            emit_ffn(nc, pool, g, f1)
        out = "l1_xffn_out"
    nc.finalize()
    return nc, ctx, out


def _in_maps(inputs):
    w = _prep_weights(inputs)
    xs = _split_x(inputs["L0_attn_x"])
    base = {
        "c_ident": w["ident"], "c_ones1": w["ones1"],
        "l0_wq": w["l0_wq"], "l0_wk": w["l0_wk"], "l0_wv": w["l0_wv"],
        "l0_gam": w["l0_gam"], "l0_wo": w["l0_wo"], "l0_bo": w["l0_bo"],
        "l0_rw": w["l0_rw"], "l0_rb": w["l0_rb"],
        "l0_sw1": w["l0_sw1"], "l0_sb1": w["l0_sb1"],
        "l0_sw2": w["l0_sw2"], "l0_sb2": w["l0_sb2"],
        "l0_ew1": w["l0_ew1"], "l0_eb1": w["l0_eb1"],
        "l0_ew2": w["l0_ew2"], "l0_eb2": w["l0_eb2"],
        "l1_wq": w["l1_wq"], "l1_wk": w["l1_wk"], "l1_wv": w["l1_wv"],
        "l1_gam": w["l1_gam"], "l1_wo": w["l1_wo"], "l1_bo": w["l1_bo"],
        "l1_rw": w["l1_rw"], "l1_ew1": w["l1_ew1"], "l1_ew2": w["l1_ew2"],
        "c_mask": w["mask"],
    }
    return [dict(base, x0=xs[c]) for c in range(NCORES)]


_CACHE = {}


def _split_x(x):
    x = np.ascontiguousarray(np.asarray(x, np.float32))
    return [np.ascontiguousarray(x[c * SPC:(c + 1) * SPC].reshape(T, D)) for c in range(NCORES)]


def kernel(**inputs):
    if "nc" not in _CACHE:
        _CACHE["nc"], _CACHE["ctx"], _CACHE["out"] = build_full()
    nc = _CACHE["nc"]
    in_maps = _in_maps(inputs)
    res = run_bass_kernel_spmd(nc, in_maps, core_ids=list(range(NCORES)))
    out = np.concatenate([r[_CACHE["out"]].reshape(SPC, S, D) for r in res.results], axis=0)
    return out


# version 09 (final)
# speedup vs baseline: 1.0794x; 1.0794x over previous
import numpy as np
import ml_dtypes
from contextlib import ExitStack
import concourse.bacc as bacc
import concourse.mybir as mybir
from concourse.bass_utils import run_bass_kernel_spmd

F32 = mybir.dt.float32
BF16 = mybir.dt.bfloat16
AF = mybir.ActivationFunctionType
ALU = mybir.AluOpType
AX = mybir.AxisListType

NCORES = 8
B, S, D = 16, 2048, 1024
SPC = B // NCORES
T = SPC * S
NH, DH = 16, 64
EPS_RMS = 1e-6
EPS_LN = 1e-5

BF16NP = ml_dtypes.bfloat16


def _prep_weights(inp):
    w = {}
    f32 = lambda a: np.ascontiguousarray(np.asarray(a, dtype=np.float32))
    w["l0_wq"] = f32(inp["L0_attn_wq"].reshape(D, D))
    w["l0_wk"] = f32(inp["L0_attn_wk"].reshape(D, D))
    w["l0_wv"] = f32(inp["L0_attn_wv"].reshape(D, D))
    w["l0_wo"] = f32(inp["L0_attn_w_out"].reshape(D, D))
    w["l0_bo"] = f32(inp["L0_attn_b_out"].reshape(1, D))
    w["l0_gam"] = f32(np.asarray(inp["L0_attn_norm_scale"]).reshape(8, 128).T)
    w["l0_rw"] = f32(inp["L0_ffn_router_w"])
    w["l0_rb"] = f32(inp["L0_ffn_router_b"].reshape(1, 8))
    w["l0_sw1"] = f32(inp["L0_ffn_sw1"])
    w["l0_sb1"] = f32(inp["L0_ffn_sb1"].reshape(16, 128).T)
    w["l0_sw2"] = f32(inp["L0_ffn_sw2"])
    w["l0_sb2"] = f32(inp["L0_ffn_sb2"].reshape(1, D))
    w["l0_ew1"] = f32(inp["L0_ffn_rw1"])
    w["l0_eb1"] = f32(np.asarray(inp["L0_ffn_rb1"]).reshape(8, 16, 128).transpose(0, 2, 1))
    w["l0_ew2"] = f32(inp["L0_ffn_rw2"])
    w["l0_eb2"] = f32(np.asarray(inp["L0_ffn_rb2"]).reshape(8, 1, D))
    qkv = np.asarray(inp["L1_attn_Wqkv"]).reshape(NH, 3, DH, D)
    w["l1_wq"] = f32(qkv[:, 0].transpose(2, 0, 1).reshape(D, D))
    w["l1_wk"] = f32(qkv[:, 1].transpose(2, 0, 1).reshape(D, D))
    w["l1_wv"] = f32(qkv[:, 2].transpose(2, 0, 1).reshape(D, D))
    w["l1_wo"] = f32(np.asarray(inp["L1_attn_Wproj"]).T)
    w["l1_bo"] = f32(inp["L1_attn_bproj"].reshape(1, D))
    w["l1_gam"] = f32(np.ones((128, 8), np.float32))
    w["l1_rw"] = f32(np.asarray(inp["L1_ffn_router_w"]).T)
    w["l1_ew1"] = f32(np.asarray(inp["L1_ffn_w1"]).transpose(0, 2, 1))
    w["l1_ew2"] = f32(np.asarray(inp["L1_ffn_w2"]).transpose(0, 2, 1))
    w["ident"] = np.eye(128, dtype=np.float32)
    w["ones1"] = np.ones((1, 512), np.float32)
    w["mask"] = np.ascontiguousarray(np.tile(np.kron(np.eye(8, dtype=np.float32), np.ones((16, 16), np.float32)), (1, 4)))
    return w



_NAME_I = [0]


def nm(base):
    _NAME_I[0] += 1
    return f"{base}_{_NAME_I[0]}"


def cv(ap, width, idx):
    if isinstance(idx, int):
        return ap[:, idx * width:(idx + 1) * width]
    return ap.rearrange("p (i u) -> p i u", u=width)[:, idx, :]


class GSem:

    def __init__(self, h):
        self.h = h
        self.n = 0


class PSem:

    def __init__(self, h, base):
        self.h = h
        self.base = base

    def v(self, x):
        return self.base + x


class Pool:

    def __init__(self, nc, ctx, n=30):
        self.sems = [ctx.enter_context(nc.semaphore(f"pool{i}")) for i in range(n)]
        self.counts = [0] * n
        self.names = {}
        self.nc = nc
        self.ctx = ctx
        self.gsems = {}
        self.seedscr = nc.dram_tensor("seed_scr", [64, 16], mybir.dt.float32,
                                      kind="Internal").ap()
        self._seed_i = 0

    def gsem(self, name):
        if name not in self.gsems:
            self.gsems[name] = GSem(self.ctx.enter_context(nc_sem(self.nc, "g_" + name)))
        return self.gsems[name]

    def seed_dma(self, gp, gs, units=1):
        for _ in range(units):
            i = self._seed_i % 64
            self._seed_i += 1
            gp.dma_start(out=self.seedscr[i:i + 1, :], in_=self.seed_src).then_inc(gs.h, 16)
            gs.n += 16

    def clear_block(self):
        pass

    def take(self, names):
        self._cur = {nm: i for i, nm in enumerate(names)}
        return {nm: PSem(self.sems[i], self.counts[i]) for i, nm in enumerate(names)}

    def finish(self, totals):
        for nm, tot in totals.items():
            self.counts[self._cur[nm]] += tot


def nc_sem(nc, name):
    return nc.semaphore(name)


def emit_qkv(nc, pool, g, cfg):
    sm = pool.take(["ld", "gf", "rm", "hb", "tps", "hf", "mq", "qe", "mv", "ve", "gl", "ss"])
    wl = pool.gsem("wl")
    qst = pool.gsem("qst")
    vst = pool.gsem("vst")
    XS, WQ, WK, WV, GAM = cfg["XS"], cfg["WQ"], cfg["WK"], cfg["WV"], cfg["GAM"]
    QS, KS, VS = cfg["QS"], cfg["KS"], cfg["VS"]
    rms = cfg["rms"]
    tm_qk = cfg.get("tm_qk", False)
    qscale = cfg["qscale"]
    NTG = cfg.get("ntg", T // 512)
    wl0, qst0, vst0 = wl.n, qst.n, vst.n

    with (
        nc.sbuf_tensor(nm("wq_sb"), [128, 8 * 1024], BF16) as wq_sb,
        nc.sbuf_tensor(nm("wk_sb"), [128, 8 * 1024], BF16) as wk_sb,
        nc.sbuf_tensor(nm("wv_sb"), [128, 8 * 1024], BF16) as wv_sb,
        nc.sbuf_tensor(nm("gam"), [128, 8], F32) as gam,
        nc.sbuf_tensor(nm("xt"), [128, 4 * 1024], F32) as xt,
        nc.sbuf_tensor(nm("junk"), [128, 4 * 1024], BF16) as junk,
        nc.sbuf_tensor(nm("ssq"), [128, 4], F32) as ssq,
        nc.sbuf_tensor(nm("lnv"), [128, 4], F32) as lnv,
        nc.sbuf_tensor(nm("rstd"), [128, 4], F32) as rstd,
        nc.sbuf_tensor(nm("hbf"), [128, 4 * 1024], BF16) as hbf,
        nc.sbuf_tensor(nm("hfm"), [128, 8 * 512], BF16) as hfm,
        nc.sbuf_tensor(nm("qev"), [128, 2 * 512], BF16) as qev,
        nc.sbuf_tensor(nm("vev"), [128, 2 * 1024], BF16) as vev,
        nc.psum_tensor(nm("tp0"), [128, 1024], BF16) as tp0,
        nc.psum_tensor(nm("tp1"), [128, 1024], BF16) as tp1,
        nc.psum_tensor(nm("pqk0"), [128, 512], F32) as pqk0,
        nc.psum_tensor(nm("pqk1"), [128, 512], F32) as pqk1,
        nc.psum_tensor(nm("pv0"), [128, 1024], F32) as pv0,
        nc.psum_tensor(nm("pv1"), [128, 1024], F32) as pv1,
        nc.Block() as blk,
    ):
        tp = (tp0, tp1)
        pqk = (pqk0, pqk1)
        pv = (pv0, pv1)
        XS4 = XS.rearrange("(g q p) d -> g p q d", q=4, p=128)
        if not tm_qk:
            QS8 = QS.rearrange("f (g n) -> f g n", g=T // 512)
            KS8 = KS.rearrange("f (g n) -> f g n", g=T // 512)
        else:
            QS_tm4 = QS.rearrange("(g q p) (nd f) -> g q p nd f", q=4, p=128, nd=2)
            KS_tm4 = KS.rearrange("(g q p) (nd f) -> g q p nd f", q=4, p=128, nd=2)
        VS4 = VS.rearrange("(g q p) d -> g q p d", q=4, p=128)

        @blk.gpsimd
        def _(gp):
            for w_sb, W in ((wq_sb, WQ), (wk_sb, WK), (wv_sb, WV)):
                gp.dma_start(out=w_sb.rearrange("p (c f) -> p c f", c=8),
                             in_=W.rearrange("(c p) f -> p c f", p=128)).then_inc(wl.h, 16)
            wl.n += 48
            with gp.Fori(0, NTG) as tg:
                if not tm_qk:
                    for w_i, QKS in ((0, QS8), (1, KS8)):
                        for fc in range(8):
                            E = 16 * tg + w_i * 8 + fc
                            gp.wait_ge(sm["qe"].h, sm["qe"].v(E + 2))
                            gp.dma_start(out=QKS[fc * 128:(fc + 1) * 128, tg, :],
                                         in_=cv(qev, 512, fc % 2)).then_inc(qst.h, 16)
                else:
                    for w_i, QKT in ((0, QS_tm4), (1, KS_tm4)):
                        for tt in range(4):
                            for nd in range(2):
                                E = 16 * tg + w_i * 8 + tt * 2 + nd
                                gp.wait_ge(sm["qe"].h, sm["qe"].v(E + 2))
                                gp.dma_start(out=QKT[tg, tt, :, nd],
                                             in_=cv(qev, 512, E % 2)).then_inc(qst.h, 16)
                for tt in range(4):
                    Tv = 4 * tg + tt
                    gp.wait_ge(sm["ve"].h, sm["ve"].v(Tv + 2))
                    gp.dma_start(out=VS4[tg, tt, :, :],
                                 in_=cv(vev, 1024, tt % 2)).then_inc(vst.h, 16)
            qst.n += 16 * 16 * NTG
            vst.n += 16 * 4 * NTG
            gp.wait_ge(qst.h, qst.n)
            gp.wait_ge(vst.h, vst.n)

        @blk.sync
        def _(sy):
            sy.dma_start(out=gam[:, :], in_=GAM).then_inc(sm["gl"].h, 16)
            with sy.Fori(0, NTG) as tg:
                if rms:
                    sy.wait_ge(sm["rm"].h, sm["rm"].v(tg))
                sy.wait_ge(sm["hb"].h, sm["hb"].v(4 * tg))
                sy.dma_start(out=xt.rearrange("p (q d) -> p q d", q=4),
                             in_=XS4[tg, :, :, :]).then_inc(sm["ld"].h, 16)

        @blk.scalar
        def _(s):
            with s.Fori(0, NTG) as tg:
                if rms:
                    s.wait_ge(sm["ld"].h, sm["ld"].v(16 * (tg + 1)))
                    s.wait_ge(sm["ss"].h, sm["ss"].v(2 * tg))
                    for tt in range(4):
                        ins = s.activation(cv(junk, 1024, tt), cv(xt, 1024, tt), AF.Square,
                                           accum_out=ssq[:, tt:tt + 1])
                    ins.then_inc(sm["ss"].h, 1)
                    s.wait_ge(sm["ss"].h, sm["ss"].v(2 * tg + 1))
                    s.activation(lnv[:, :], ssq[:, :], AF.Ln, scale=1.0 / D,
                                 bias=g["epsr"]).then_inc(sm["ss"].h, 1)
                    s.wait_ge(sm["ss"].h, sm["ss"].v(2 * tg + 2))
                    s.activation(rstd[:, :], lnv[:, :], AF.Exp, scale=-0.5).then_inc(sm["rm"].h, 1)
                for w_i in range(2):
                    for fc in range(8):
                        E = 16 * tg + w_i * 8 + fc
                        s.wait_ge(sm["mq"].h, sm["mq"].v(E + 1))
                        s.wait_ge(qst.h, qst0 + 16 * E)
                        s.copy(cv(qev, 512, E % 2), pqk[E % 2][:, :]).then_inc(sm["qe"].h, 1)

        @blk.vector
        def _(v):
            v.wait_ge(wl.h, wl0 + 48)
            v.wait_ge(sm["gl"].h, sm["gl"].v(16))
            for w_sb, sc in ((wq_sb, qscale), (wk_sb, 1.0), (wv_sb, 1.0)):
                for c in range(8):
                    ins = v.tensor_scalar(cv(w_sb, 1024, c), cv(w_sb, 1024, c),
                                          gam[:, c:c + 1], float(sc), ALU.mult, ALU.mult)
            ins.then_inc(sm["gf"].h, 1)
            with v.Fori(0, NTG) as tg:
                if rms:
                    v.wait_ge(sm["rm"].h, sm["rm"].v(tg + 1))
                else:
                    v.wait_ge(sm["ld"].h, sm["ld"].v(16 * (tg + 1)))
                for tt in range(4):
                    if tt == 0:
                        v.wait_ge(sm["tps"].h, sm["tps"].v(4 * tg))
                    if rms:
                        v.tensor_scalar(cv(hbf, 1024, tt), cv(xt, 1024, tt),
                                        rstd[:, tt:tt + 1], None, ALU.mult).then_inc(sm["hb"].h, 1)
                    else:
                        v.tensor_copy(cv(hbf, 1024, tt), cv(xt, 1024, tt)).then_inc(sm["hb"].h, 1)
                for tt in range(4):
                    G = 4 * tg + tt
                    if tt == 0:
                        v.wait_ge(sm["mq"].h, sm["mq"].v(16 * tg))
                        v.wait_ge(sm["mv"].h, sm["mv"].v(8 * tg))
                    v.wait_ge(sm["tps"].h, sm["tps"].v(G + 1))
                    v.tensor_copy(
                        hfm.rearrange("p (c f) -> p c f", c=8)[:, :, tt * 128:(tt + 1) * 128],
                        tp[tt % 2].rearrange("p (c f) -> p c f", c=8)).then_inc(sm["hf"].h, 1)
                for tt in range(4):
                    Tv = 4 * tg + tt
                    v.wait_ge(sm["mv"].h, sm["mv"].v(2 * Tv + 2))
                    v.wait_ge(vst.h, vst0 + 16 * Tv)
                    v.tensor_copy(cv(vev, 1024, tt % 2), pv[tt % 2][:, :]).then_inc(sm["ve"].h, 1)

        @blk.tensor
        def _(t):
            t.sem_inc(sm["qe"].h, 1)
            t.sem_inc(sm["ve"].h, 1)
            t.sem_inc(sm["hf"].h, 1)
            t.wait_ge(sm["gf"].h, sm["gf"].v(1))
            with t.Fori(0, NTG) as tg:
                for tt in range(4):
                    G = 4 * tg + tt
                    t.wait_ge(sm["hb"].h, sm["hb"].v(4 * tg + tt + 1))
                    t.wait_ge(sm["hf"].h, sm["hf"].v(G))
                    for c in range(8):
                        ins = t.transpose(cv(tp[tt % 2], 128, c),
                                          hbf[:, tt * 1024 + c * 128:tt * 1024 + (c + 1) * 128],
                                          g["idn"][:, :])
                    ins.then_inc(sm["tps"].h, 1)
                t.wait_ge(sm["hf"].h, sm["hf"].v(4 * (tg + 1) + 1))
                if not tm_qk:
                    for w_i, w_sb in ((0, wq_sb), (1, wk_sb)):
                        for fc in range(8):
                            E = 16 * tg + w_i * 8 + fc
                            t.wait_ge(sm["qe"].h, sm["qe"].v(E))
                            for kc in range(8):
                                ins = t.matmul(pqk[fc % 2][:, :],
                                               w_sb[:, kc * 1024 + fc * 128:kc * 1024 + (fc + 1) * 128],
                                               cv(hfm, 512, kc),
                                               start=(kc == 0), stop=(kc == 7))
                            ins.then_inc(sm["mq"].h, 1)
                else:
                    for w_i, w_sb in ((0, wq_sb), (1, wk_sb)):
                        for tt in range(4):
                            for nd in range(2):
                                E = 16 * tg + w_i * 8 + tt * 2 + nd
                                t.wait_ge(sm["qe"].h, sm["qe"].v(E))
                                for kc in range(8):
                                    ins = t.matmul(
                                        pqk[E % 2][:, :],
                                        hfm[:, kc * 512 + tt * 128:kc * 512 + (tt + 1) * 128],
                                        w_sb[:, kc * 1024 + nd * 512:kc * 1024 + (nd + 1) * 512],
                                        start=(kc == 0), stop=(kc == 7))
                                ins.then_inc(sm["mq"].h, 1)
                for tt in range(4):
                    t.wait_ge(sm["ve"].h, sm["ve"].v(4 * tg + tt))
                    for nd in range(2):
                        for kc in range(8):
                            ins = t.matmul(pv[tt % 2][:, nd * 512:(nd + 1) * 512],
                                           hfm[:, kc * 512 + tt * 128:kc * 512 + (tt + 1) * 128],
                                           wv_sb[:, kc * 1024 + nd * 512:kc * 1024 + (nd + 1) * 512],
                                           start=(kc == 0), stop=(kc == 7))
                        ins.then_inc(sm["mv"].h, 1)
    pool.finish({"ss": (2 * NTG) if rms else 0, "gl": 16, "ld": 16 * NTG, "gf": 1, "rm": NTG if rms else 0,
                 "hb": 4 * NTG, "tps": 4 * NTG, "hf": 4 * NTG + 1,
                 "mq": 16 * NTG, "qe": 16 * NTG + 1, "mv": 8 * NTG,
                 "ve": 4 * NTG + 1})


def emit_attn(nc, pool, g, cfg):
    sm = pool.take(["qld", "m1", "r1", "e1", "sln", "vaug", "ttr", "trd",
                    "augst", "augld", "m2", "e2", "mpv", "ve2", "msd",
                    "radd", "mpj", "xld", "xst", "ssq2", "srs", "vvar",
                    "lnv", "vs2", "vs3", "xrl"])
    wl = pool.gsem("wl")
    QS, KS, VS, XRES = cfg["QS"], cfg["KS"], cfg["VS"], cfg["XRES"]
    WO, BO, XOUT = cfg["WO"], cfg["BO"], cfg["XOUT"]
    AUGS = nc.dram_tensor(cfg["name"] + "_augs", [4, 16, 128], BF16, kind="Internal").ap()
    wl0 = wl.n
    NHP = 8

    es = ExitStack()
    wo_sb = es.enter_context(nc.sbuf_tensor(nm("a_wo_sb"), [128, 8 * 1024], BF16))
    bo_sb = es.enter_context(nc.sbuf_tensor(nm("a_bo_sb"), [1, 1024], F32))
    qstA = es.enter_context(nc.sbuf_tensor(nm("a_qstA"), [128, 2048], BF16))
    qstB = es.enter_context(nc.sbuf_tensor(nm("a_qstB"), [128, 2048], BF16))
    kstA = es.enter_context(nc.sbuf_tensor(nm("a_kstA"), [128, 2048], BF16))
    kstB = es.enter_context(nc.sbuf_tensor(nm("a_kstB"), [128, 2048], BF16))
    vstA = es.enter_context(nc.sbuf_tensor(nm("a_vstA"), [128, 16 * 64], BF16))
    vstB = es.enter_context(nc.sbuf_tensor(nm("a_vstB"), [128, 16 * 64], BF16))
    ET = es.enter_context(nc.sbuf_tensor(nm("a_ET"), [128, 16 * 512], BF16))
    junkA = es.enter_context(nc.sbuf_tensor(nm("a_junkA"), [128, 2048], BF16))
    nmall = es.enter_context(nc.sbuf_tensor(nm("a_nmall"), [128, 32], F32))
    ssall = es.enter_context(nc.sbuf_tensor(nm("a_ssall"), [128, 32], F32))
    lnl = es.enter_context(nc.sbuf_tensor(nm("a_lnl"), [128, 32], F32))
    negb = es.enter_context(nc.sbuf_tensor(nm("a_negb"), [128, 32], F32))
    lo32 = es.enter_context(nc.sbuf_tensor(nm("a_lo32"), [128, 32], F32))
    hilo = es.enter_context(nc.sbuf_tensor(nm("a_hilo"), [128, 64], BF16))
    tr64 = es.enter_context(nc.sbuf_tensor(nm("a_tr64"), [64, 128], BF16))
    attnO = es.enter_context(nc.sbuf_tensor(nm("a_attnO"), [128, 8 * 2048], BF16))
    xrt = es.enter_context(nc.sbuf_tensor(nm("a_xrt"), [128, 2 * 1024], F32))
    rt = es.enter_context(nc.sbuf_tensor(nm("a_rt"), [128, 2 * 1024], F32))
    ob = es.enter_context(nc.sbuf_tensor(nm("a_ob"), [128, 2 * 1024], F32))
    junkP = es.enter_context(nc.sbuf_tensor(nm("a_junkP"), [128, 2 * 1024], BF16))
    pstat = es.enter_context(nc.sbuf_tensor(nm("a_pstat"), [128, 16], F32))
    with (
        es,
        nc.psum_tensor(nm("S2a"), [128, 2048], F32) as S2a,
        nc.psum_tensor(nm("S2b"), [128, 2048], F32) as S2b,
        nc.Block() as blk,
    ):
        S2 = (S2a, S2b)
        qst = (qstA, qstB)
        kst = (kstA, kstB)
        vst = (vstA, vstB)
        ps_s = (S2a[:, 0:512], S2a[:, 512:1024], S2b[:, 0:512], S2b[:, 512:1024])
        pvp = S2a[:, 1024:1536]
        ptr = S2a[:, 1536:2048].bitcast(BF16)
        po = (S2b[:, 0:1024], S2b[:, 1024:2048])
        QSh = QS.rearrange("(h r) t -> h r t", r=64)
        KSh = KS.rearrange("(h r) t -> h r t", r=64)
        AUGSf = AUGS.rearrange("x a b -> x (a b)")

        @blk.gpsimd
        def _(gp):
            gp.dma_start(out=wo_sb.rearrange("p (c f) -> p c f", c=8),
                         in_=WO.rearrange("(c p) f -> p c f", p=128)).then_inc(wl.h, 16)
            for s in range(SPC):
                with gp.Fori(0, NHP) as hp:
                    i = 8 * s + hp
                    gp.wait_ge(sm["trd"].h, sm["trd"].v(i + 1))
                    for j in range(4):
                        gp.dma_start(out=AUGS[j], in_=tr64[j * 16:(j + 1) * 16, :]).then_inc(sm["augst"].h, 16)
                    gp.wait_ge(sm["augst"].h, sm["augst"].v(64 * (i + 1)))
                    gp.dma_start(out=qst[0][64:66, :], in_=AUGSf[0:2, :]).then_inc(sm["augld"].h, 16)
                    gp.dma_start(out=qst[1][64:66, :], in_=AUGSf[2:4, :]).then_inc(sm["augld"].h, 16)
            gp.wait_ge(wl.h, wl0 + 16)
        wl.n += 16

        @blk.sync
        def _(sy):
            sy.dma_start(out=bo_sb[:, :], in_=BO).then_inc(sm["xld"].h, 16)
            for s in range(SPC):
                with sy.Fori(0, NHP) as hp:
                    i = 8 * s + hp
                    sy.wait_ge(sm["m2"].h, sm["m2"].v(128 * i))
                    for X in range(2):
                        h = 2 * hp + X
                        sy.dma_start(out=qst[X][0:64, :],
                                     in_=QSh[h, :, s * 2048:(s + 1) * 2048]).then_inc(sm["qld"].h, 16)
                        sy.dma_start(out=kst[X][0:64, :],
                                     in_=KSh[h, :, s * 2048:(s + 1) * 2048]).then_inc(sm["qld"].h, 16)
                    sy.wait_ge(sm["mpv"].h, sm["mpv"].v(8 * i))
                    for X in range(2):
                        h = 2 * hp + X
                        sy.dma_start(
                            out=vst[X].rearrange("p (c e) -> p c e", c=16),
                            in_=VS.rearrange("(s c p) (h e) -> s h p c e", s=SPC, p=128, h=16)[s, h],
                        ).then_inc(sm["qld"].h, 16)

                for tt in range(16):
                    PT = 16 * s + tt
                    if PT >= 2:
                        sy.wait_ge(sm["radd"].h, sm["radd"].v(PT - 1))
                    sy.dma_start(out=cv(xrt, 1024, tt % 2),
                                 in_=XRES.rearrange("(t p) d -> t p d", p=128)[PT]).then_inc(sm["xrl"].h, 16)
                    sy.wait_ge(sm["lnv"].h, sm["lnv"].v(PT + 1))
                    sy.dma_start(out=XOUT.rearrange("(t p) d -> t p d", p=128)[PT],
                                 in_=cv(ob, 1024, tt % 2)).then_inc(sm["xst"].h, 16)
            sy.wait_ge(sm["xst"].h, sm["xst"].v(16 * 32))

        @blk.scalar
        def _(s_):
            for s in range(SPC):
                with s_.Fori(0, NHP) as hp:
                    i = 8 * s + hp
                    for u in range(32):
                        s_.wait_ge(sm["r1"].h, sm["r1"].v(32 * i + u + 1))
                        s_.wait_ge(sm["e1"].h, sm["e1"].v(32 * i + max(u - 1, 0)))
                        s_.activation(junkA[:, :], S2[u % 2][:, :], AF.Exp,
                                      bias=nmall[:, u:u + 1],
                                      accum_out=ssall[:, u:u + 1]).then_inc(sm["e1"].h, 1)
                    s_.wait_ge(sm["e1"].h, sm["e1"].v(32 * (i + 1)))
                    s_.activation(lnl[:, :], ssall[:, :], AF.Ln).then_inc(sm["sln"].h, 1)
                    for qg in range(4):
                        for X in range(2):
                            un = 2 * qg + X
                            s_.wait_ge(sm["mpv"].h, sm["mpv"].v(8 * i + un))
                            for kt in range(16):
                                m2j = 128 * i + un * 16 + kt
                                s_.wait_ge(sm["m2"].h, sm["m2"].v(m2j + 1))
                                s_.activation(cv(ET, 512, kt), ps_s[kt % 4],
                                              AF.Exp).then_inc(sm["e2"].h, 1)
                for tt in range(16):
                    PT = 16 * s + tt
                    p2 = tt % 2
                    s_.wait_ge(sm["radd"].h, sm["radd"].v(PT + 1))
                    if PT >= 2:
                        s_.wait_ge(sm["ssq2"].h, sm["ssq2"].v(PT - 1))
                        s_.wait_ge(sm["srs"].h, sm["srs"].v(2 * PT - 2))
                        s_.wait_ge(sm["vvar"].h, sm["vvar"].v(PT - 1))
                    s_.activation(cv(junkP, 1024, p2), cv(rt, 1024, p2), AF.Square,
                                  accum_out=pstat[:, 12 + p2:13 + p2]).then_inc(sm["ssq2"].h, 1)
                    s_.wait_ge(sm["vvar"].h, sm["vvar"].v(PT + 1))
                    s_.activation(pstat[:, 8 + p2:9 + p2],
                                  pstat[:, 6 + p2:7 + p2], AF.Ln,
                                  bias=g["epsl"]).then_inc(sm["srs"].h, 1)
                    s_.wait_ge(sm["srs"].h, sm["srs"].v(2 * PT + 1))
                    s_.activation(pstat[:, 10 + p2:11 + p2],
                                  pstat[:, 8 + p2:9 + p2], AF.Exp,
                                  scale=-0.5).then_inc(sm["srs"].h, 1)

        @blk.vector
        def _(v):
            v.memset(kstA[64:66, :], 1.0)
            v.memset(kstB[64:66, :], 1.0).then_inc(sm["msd"].h, 1)
            for s in range(SPC):
                v.wait_ge(sm["mpj"].h, sm["mpj"].v(32 * s))
                with v.Fori(0, NHP) as hp:
                    i = 8 * s + hp
                    v.wait_ge(sm["e1"].h, sm["e1"].v(32 * i))
                    for u in range(32):
                        v.wait_ge(sm["m1"].h, sm["m1"].v(32 * i + u + 1))
                        v.reduce_max(nmall[:, u:u + 1], S2[u % 2][:, :], AX.X,
                                     negate=True).then_inc(sm["r1"].h, 1)
                    v.wait_ge(sm["sln"].h, sm["sln"].v(i + 1))
                    v.wait_ge(sm["ttr"].h, sm["ttr"].v(i))
                    hiv = hilo.rearrange("p (y z c) -> p y z c", y=2, z=2)[:, :, 0, :]
                    lov = hilo.rearrange("p (y z c) -> p y z c", y=2, z=2)[:, :, 1, :]
                    v.tensor_sub(negb[:, :], nmall[:, :], lnl[:, :]).then_inc(sm["vs2"].h, 1)
                    v.wait_ge(sm["vs2"].h, sm["vs2"].v(3 * i + 1))
                    v.tensor_copy(hiv, negb.rearrange("p (y c) -> p y c", y=2)).then_inc(sm["vs2"].h, 1)
                    v.wait_ge(sm["vs2"].h, sm["vs2"].v(3 * i + 2))
                    v.tensor_sub(lo32.rearrange("p (y c) -> p y c", y=2),
                                 negb.rearrange("p (y c) -> p y c", y=2),
                                 hiv).then_inc(sm["vs2"].h, 1)
                    v.wait_ge(sm["vs2"].h, sm["vs2"].v(3 * i + 3))
                    v.tensor_copy(lov, lo32.rearrange("p (y c) -> p y c", y=2)).then_inc(sm["vaug"].h, 1)
                    v.wait_ge(sm["ttr"].h, sm["ttr"].v(i + 1))
                    v.wait_ge(sm["augst"].h, sm["augst"].v(64 * i))
                    v.tensor_copy(tr64[:, :], ptr[0:64, 0:128]).then_inc(sm["trd"].h, 1)
                    for qg in range(4):
                        v.wait_ge(sm["mpv"].h, sm["mpv"].v(8 * i + 2 * qg + 2))
                        v.tensor_copy(
                            attnO.rearrange("p (c n) -> p c n", n=512)[:, 4 * hp + qg, :],
                            pvp[:, :]).then_inc(sm["ve2"].h, 1)
                for tt in range(16):
                    PT = 16 * s + tt
                    p2 = tt % 2
                    v.wait_ge(sm["mpj"].h, sm["mpj"].v(2 * PT + 2))
                    v.wait_ge(sm["xrl"].h, sm["xrl"].v(16 * (PT + 1)))
                    if PT >= 2:
                        v.wait_ge(sm["xst"].h, sm["xst"].v(16 * (PT - 1)))
                    v.tensor_add(cv(rt, 1024, p2), po[p2][:, :],
                                 cv(xrt, 1024, p2)).then_inc(sm["radd"].h, 1)
                    v.wait_ge(sm["radd"].h, sm["radd"].v(PT + 1))
                    v.reduce_sum(pstat[:, p2:p2 + 1], cv(rt, 1024, p2), AX.X).then_inc(sm["vs3"].h, 1)
                    v.wait_ge(sm["vs3"].h, sm["vs3"].v(3 * PT + 1))
                    v.tensor_scalar(pstat[:, 2 + p2:3 + p2], pstat[:, p2:p2 + 1],
                                    -1.0 / D, None, ALU.mult).then_inc(sm["vs3"].h, 1)
                    v.wait_ge(sm["vs3"].h, sm["vs3"].v(3 * PT + 2))
                    v.tensor_mul(pstat[:, 4 + p2:5 + p2], pstat[:, 2 + p2:3 + p2],
                                 pstat[:, 2 + p2:3 + p2]).then_inc(sm["vs3"].h, 1)
                    v.wait_ge(sm["ssq2"].h, sm["ssq2"].v(PT + 1))
                    v.wait_ge(sm["vs3"].h, sm["vs3"].v(3 * PT + 3))
                    v.tensor_scalar(pstat[:, 6 + p2:7 + p2], pstat[:, 12 + p2:13 + p2],
                                    1.0 / D, pstat[:, 4 + p2:5 + p2],
                                    ALU.mult, ALU.subtract).then_inc(sm["vvar"].h, 1)
                    v.wait_ge(sm["srs"].h, sm["srs"].v(2 * PT + 2))
                    v.tensor_scalar(cv(ob, 1024, p2), cv(rt, 1024, p2),
                                    pstat[:, 2 + p2:3 + p2], pstat[:, 10 + p2:11 + p2],
                                    ALU.add, ALU.mult).then_inc(sm["lnv"].h, 1)

        @blk.tensor
        def _(t):
            t.wait_ge(sm["msd"].h, sm["msd"].v(1))
            t.wait_ge(wl.h, wl0 + 16)
            t.wait_ge(sm["xld"].h, sm["xld"].v(16))
            for s in range(SPC):
                t.wait_ge(sm["radd"].h, sm["radd"].v(16 * s))
                with t.Fori(0, NHP) as hp:
                    i = 8 * s + hp
                    t.wait_ge(sm["qld"].h, sm["qld"].v(96 * (i + 1)))
                    t.wait_ge(sm["trd"].h, sm["trd"].v(i))
                    for u in range(32):
                        X, qt = u // 16, u % 16
                        t.wait_ge(sm["e1"].h, sm["e1"].v(32 * i + max(u - 1, 0)))
                        for kc in range(4):
                            ins = t.matmul(S2[u % 2][:, kc * 512:(kc + 1) * 512],
                                           qst[X][0:64, qt * 128:(qt + 1) * 128],
                                           kst[X][0:64, kc * 512:(kc + 1) * 512],
                                           start=True, stop=True)
                        ins.then_inc(sm["m1"].h, 1)
                    t.wait_ge(sm["vaug"].h, sm["vaug"].v(i + 1))
                    t.transpose(ptr[0:64, 0:128], hilo[:, :], g["idn"][:, :]).then_inc(sm["ttr"].h, 1)
                    t.wait_ge(sm["augld"].h, sm["augld"].v(32 * (i + 1)))
                    for qg in range(4):
                        for X in range(2):
                            un = 2 * qg + X
                            for kt in range(16):
                                m2j = 128 * i + un * 16 + kt
                                t.wait_ge(sm["e2"].h, sm["e2"].v(128 * i + un * 16 + max(kt - 3, 0)))
                                t.matmul(ps_s[kt % 4],
                                         kst[X][0:66, kt * 128:(kt + 1) * 128],
                                         qst[X][0:66, qg * 512:(qg + 1) * 512],
                                         start=True, stop=True).then_inc(sm["m2"].h, 1)
                            t.wait_ge(sm["e2"].h, sm["e2"].v(128 * i + un * 16 + 16))
                            if X == 0:
                                t.wait_ge(sm["ve2"].h, sm["ve2"].v(4 * i + qg))
                            for kt in range(16):
                                ins = t.matmul(pvp[X * 64:X * 64 + 64, :],
                                               vst[X][:, kt * 64:(kt + 1) * 64],
                                               cv(ET, 512, kt),
                                               start=(kt == 0), stop=(kt == 15),
                                               tile_position=(0, 64 * X))
                            ins.then_inc(sm["mpv"].h, 1)
                t.wait_ge(sm["ve2"].h, sm["ve2"].v(32 * (s + 1)))
                t.wait_ge(sm["e1"].h, sm["e1"].v(256 * (s + 1)))
                for tt in range(16):
                    PT = 16 * s + tt
                    if PT >= 2:
                        t.wait_ge(sm["radd"].h, sm["radd"].v(PT - 1))
                    for nd in range(2):
                        for kc in range(8):
                            t.matmul(po[tt % 2][:, nd * 512:(nd + 1) * 512],
                                     attnO[:, kc * 2048 + tt * 128:kc * 2048 + (tt + 1) * 128],
                                     wo_sb[:, kc * 1024 + nd * 512:kc * 1024 + (nd + 1) * 512],
                                     start=(kc == 0), stop=False)
                        t.matmul(po[tt % 2][:, nd * 512:(nd + 1) * 512],
                                 g["ones"][0:1, 0:128], bo_sb[0:1, nd * 512:(nd + 1) * 512],
                                 start=False, stop=True).then_inc(sm["mpj"].h, 1)
    pool.finish({"qld": 96 * 16, "m1": 32 * 16, "r1": 32 * 16, "e1": 32 * 16,
                 "sln": 16, "vaug": 16, "ttr": 16, "trd": 16,
                 "augst": 64 * 16, "augld": 32 * 16, "m2": 128 * 16,
                 "e2": 128 * 16, "mpv": 8 * 16, "ve2": 4 * 16, "msd": 1,
                 "radd": 32, "mpj": 64, "xld": 16, "xst": 16 * 32,
                 "ssq2": 32, "srs": 64, "vvar": 32, "lnv": 32,
                 "vs2": 3 * 16, "vs3": 3 * 32, "xrl": 16 * 32})


def emit_ffn(nc, pool, g, cfg):
    L = cfg["layer"]
    H = cfg["H"]
    shared = cfg["shared"]
    HB = H // 128
    TGW = 512 if shared else 256
    NTGC = T // TGW
    TPC = TGW // 128
    sm = pool.take(["ldx", "cbd", "tpd", "xfd", "mr", "rrm", "sre", "wvd",
                    "vsA", "mh", "gh", "my", "ye", "accv", "dr", "dssq",
                    "dsrs", "dvvar", "dlnv", "ds3", "zf"])
    wle = pool.gsem("wl")
    xfst = pool.gsem("xfst")
    accst = pool.gsem("accst")
    xfld_g = pool.gsem("xfld")
    XIN, XOUT = cfg["XIN"], cfg["XOUT"]
    RW, RB = cfg["RW"], cfg.get("RB")
    XF = nc.dram_tensor(cfg["name"] + "_xf", [D, T], BF16, kind="Internal").ap()
    ACC = nc.dram_tensor(cfg["name"] + "_acc", [T, D], F32, kind="Internal").ap()
    XIN_T = XIN.rearrange("(t p) d -> t p d", p=128)
    XOUT_T = XOUT.rearrange("(t p) d -> t p d", p=128)
    ACC_T = ACC.rearrange("(t p) d -> t p d", p=128)
    XFr = XF.rearrange("(c p) t -> p c t", p=128)

    _scA = nc.enter_named_scope(cfg["name"] + "_A", False)[0]
    esP = ExitStack()
    wsb = esP.enter_context(nc.sbuf_tensor(nm("f_wsb"), [128, 32 * 8], F32))
    esA = ExitStack()
    xta = esA.enter_context(nc.sbuf_tensor(nm("f_xta"), [128, 2 * 1024], F32))
    hbf2 = esA.enter_context(nc.sbuf_tensor(nm("f_hbf"), [128, 2 * 1024], BF16))
    xfb = esA.enter_context(nc.sbuf_tensor(nm("f_xfb"), [128, 2 * 1024], BF16))
    rw_sb = esA.enter_context(nc.sbuf_tensor(nm("f_rw"), [128, 8 * 8], BF16))
    rb_sb = esA.enter_context(nc.sbuf_tensor(nm("f_rb"), [1, 8], F32))
    prexp = esA.enter_context(nc.sbuf_tensor(nm("f_prexp"), [128, 2 * 8], F32))
    prb = esA.enter_context(nc.sbuf_tensor(nm("f_prb"), [128, 2 * 8], F32))
    msk = esA.enter_context(nc.sbuf_tensor(nm("f_msk"), [128, 2 * 8], F32))
    rstat = esA.enter_context(nc.sbuf_tensor(nm("f_rstat"), [128, 16], F32))
    wl0 = wle.n
    xfst0 = xfst.n
    with (
        esA,
        nc.psum_tensor(nm("f_tp0"), [128, 1024], BF16) as tpf0,
        nc.psum_tensor(nm("f_tp1"), [128, 1024], BF16) as tpf1,
        nc.psum_tensor(nm("f_pr0"), [128, 8], F32) as pr0,
        nc.psum_tensor(nm("f_pr1"), [128, 8], F32) as pr1,
        nc.Block() as blk,
    ):
        tpf = (tpf0, tpf1)
        pr = (pr0, pr1)

        @blk.gpsimd
        def _(gp):
            gp.dma_start(out=rw_sb.rearrange("p (c e) -> p c e", c=8),
                         in_=RW.rearrange("(c p) e -> p c e", p=128)).then_inc(wle.h, 16)
            gp.wait_ge(wle.h, wl0 + 16)
        wle.n += 16

        @blk.sync
        def _(sy):
            if RB is not None:
                sy.dma_start(out=rb_sb[:, :], in_=RB).then_inc(sm["ldx"].h, 16)
            for tt in range(32):
                p2 = tt % 2
                if tt >= 2:
                    sy.wait_ge(sm["cbd"].h, sm["cbd"].v(tt - 1))
                sy.dma_start(out=cv(xta, 1024, p2), in_=XIN_T[tt]).then_inc(sm["ldx"].h, 16)
                sy.wait_ge(sm["xfd"].h, sm["xfd"].v(tt + 1))
                sy.dma_start(out=XFr[:, :, tt * 128:(tt + 1) * 128],
                             in_=xfb.rearrange("p (q c f) -> p q c f", q=2, c=8)[:, p2],
                             ).then_inc(xfst.h, 16)
            sy.wait_ge(xfst.h, xfst0 + 16 * 32)
        xfst.n += 16 * 32

        @blk.vector
        def _(v):
            rbase = 16 if RB is not None else 0
            for tt in range(32):
                p2 = tt % 2
                v.wait_ge(sm["ldx"].h, sm["ldx"].v(rbase + 16 * (tt + 1)))
                if tt >= 2:
                    v.wait_ge(sm["tpd"].h, sm["tpd"].v(tt - 1))
                v.tensor_copy(cv(hbf2, 1024, p2), cv(xta, 1024, p2)).then_inc(sm["cbd"].h, 1)
                v.wait_ge(sm["tpd"].h, sm["tpd"].v(tt + 1))
                if tt >= 2:
                    v.wait_ge(xfst.h, xfst0 + 16 * (tt - 1))
                v.tensor_copy(cv(xfb, 1024, p2), tpf[p2][:, :]).then_inc(sm["xfd"].h, 1)
                v.wait_ge(sm["mr"].h, sm["mr"].v(tt + 1))
                v.reduce_max(rstat[:, p2:p2 + 1], pr[p2][:, :], AX.X,
                             negate=True).then_inc(sm["rrm"].h, 1)
                v.wait_ge(sm["sre"].h, sm["sre"].v(tt + 1))
                v.reciprocal(rstat[:, 4 + p2:5 + p2], rstat[:, 2 + p2:3 + p2]).then_inc(sm["vsA"].h, 1)
                v.wait_ge(sm["vsA"].h, sm["vsA"].v(6 * tt + 1))
                v.tensor_scalar(cv(prb, 8, p2), cv(prexp, 8, p2),
                                rstat[:, 4 + p2:5 + p2], None, ALU.mult).then_inc(sm["vsA"].h, 1)
                v.wait_ge(sm["vsA"].h, sm["vsA"].v(6 * tt + 2))
                v.reduce_max(rstat[:, 6 + p2:7 + p2], cv(prb, 8, p2), AX.X).then_inc(sm["vsA"].h, 1)
                v.wait_ge(sm["vsA"].h, sm["vsA"].v(6 * tt + 3))
                v.tensor_scalar(cv(msk, 8, p2), cv(prb, 8, p2),
                                rstat[:, 6 + p2:7 + p2], None, ALU.is_ge).then_inc(sm["vsA"].h, 1)
                v.wait_ge(sm["vsA"].h, sm["vsA"].v(6 * tt + 4))
                v.scalar_tensor_tensor(cv(msk, 8, p2), cv(msk, 8, p2), -2.0,
                                       cv(prb, 8, p2), ALU.mult, ALU.add).then_inc(sm["vsA"].h, 1)
                v.wait_ge(sm["vsA"].h, sm["vsA"].v(6 * tt + 5))
                v.reduce_max(rstat[:, 8 + p2:9 + p2], cv(msk, 8, p2), AX.X).then_inc(sm["vsA"].h, 1)
                v.wait_ge(sm["vsA"].h, sm["vsA"].v(6 * tt + 6))
                v.tensor_scalar(cv(msk, 8, p2), cv(prb, 8, p2),
                                rstat[:, 8 + p2:9 + p2], None, ALU.is_ge).then_inc(sm["wvd"].h, 1)
                v.wait_ge(sm["wvd"].h, sm["wvd"].v(2 * tt + 1))
                v.tensor_mul(cv(wsb, 8, tt), cv(prb, 8, p2),
                             cv(msk, 8, p2)).then_inc(sm["wvd"].h, 1)

        @blk.scalar
        def _(s_):
            for tt in range(32):
                p2 = tt % 2
                s_.wait_ge(sm["rrm"].h, sm["rrm"].v(tt + 1))
                if tt >= 2:
                    s_.wait_ge(sm["vsA"].h, sm["vsA"].v(6 * (tt - 1)))
                s_.activation(cv(prexp, 8, p2), pr[p2][:, :], AF.Exp,
                              bias=rstat[:, p2:p2 + 1],
                              accum_out=rstat[:, 2 + p2:3 + p2]).then_inc(sm["sre"].h, 1)

        @blk.tensor
        def _(t):
            t.wait_ge(wle.h, wl0 + 16)
            if RB is not None:
                t.wait_ge(sm["ldx"].h, sm["ldx"].v(16))
            for tt in range(32):
                p2 = tt % 2
                t.wait_ge(sm["cbd"].h, sm["cbd"].v(tt + 1))
                if tt >= 2:
                    t.wait_ge(sm["xfd"].h, sm["xfd"].v(tt - 1))
                for c in range(8):
                    ins = t.transpose(cv(tpf[p2], 128, c),
                                      hbf2[:, p2 * 1024 + c * 128:p2 * 1024 + (c + 1) * 128],
                                      g["idn"][:, :])
                ins.then_inc(sm["tpd"].h, 1)
                t.wait_ge(sm["xfd"].h, sm["xfd"].v(tt + 1))
                if tt >= 2:
                    t.wait_ge(sm["sre"].h, sm["sre"].v(tt - 1))
                for c in range(8):
                    ins = t.matmul(pr[p2][:, :], xfb[:, p2 * 1024 + c * 128:p2 * 1024 + (c + 1) * 128],
                                   rw_sb[:, c * 8:(c + 1) * 8],
                                   start=(c == 0), stop=(RB is None and c == 7))
                if RB is not None:
                    ins = t.matmul(pr[p2][:, :], g["ones"][0:1, 0:128], rb_sb[0:1, :],
                                   start=False, stop=True)
                ins.then_inc(sm["mr"].h, 1)
    nc.leave_named_scope(cfg["name"] + "_A", _scA, False)
    totA = {"ldx": (16 if RB is not None else 0) + 16 * 32, "cbd": 32,
            "tpd": 32, "xfd": 32, "mr": 32, "rrm": 32, "sre": 32,
            "wvd": 2 * 32, "vsA": 6 * 32}

    if shared:
        esB = ExitStack()
        sw1_sb = esB.enter_context(nc.sbuf_tensor(nm("f_sw1"), [128, 8 * 2048], BF16))
        sw2_sb = esB.enter_context(nc.sbuf_tensor(nm("f_sw2"), [128, 16 * 1024], BF16))
        sb1_sb = esB.enter_context(nc.sbuf_tensor(nm("f_sb1"), [128, 16], F32))
        sb2_sb = esB.enter_context(nc.sbuf_tensor(nm("f_sb2"), [1, 1024], F32))
        h1b = esB.enter_context(nc.sbuf_tensor(nm("f_h1b"), [128, 2 * 16 * 512], BF16))
        xfs = esB.enter_context(nc.sbuf_tensor(nm("f_xfs"), [128, 2 * 8 * 512], BF16))
        yev = esB.enter_context(nc.sbuf_tensor(nm("f_yev"), [128, 2 * 1024], F32))
        wl1 = wle.n
        xfld0 = xfld_g.n
        accst0 = accst.n
        with (
            esB,
            nc.psum_tensor(nm("f_ph0"), [128, 512], F32) as ph0,
            nc.psum_tensor(nm("f_ph1"), [128, 512], F32) as ph1,
            nc.psum_tensor(nm("f_py0"), [128, 1024], F32) as py0,
            nc.psum_tensor(nm("f_py1"), [128, 1024], F32) as py1,
            nc.Block() as blk,
        ):
            ph = (ph0, ph1)
            py = (py0, py1)

            @blk.gpsimd
            def _(gp):
                gp.dma_start(out=sw1_sb.rearrange("p (c f) -> p c f", c=8),
                             in_=cfg["SW1"].rearrange("(c p) f -> p c f", p=128)).then_inc(wle.h, 16)
                gp.dma_start(out=sw2_sb.rearrange("p (c f) -> p c f", c=16),
                             in_=cfg["SW2"].rearrange("(c p) f -> p c f", p=128)).then_inc(wle.h, 16)
                gp.wait_ge(wle.h, wl1 + 32)
            wle.n += 32

            @blk.sync
            def _(sy):
                sy.dma_start(out=sb1_sb[:, :], in_=cfg["SB1"]).then_inc(sm["ldx"].h, 16)
                sy.dma_start(out=sb2_sb[:, :], in_=cfg["SB2"]).then_inc(sm["ldx"].h, 16)
                with sy.Fori(0, 8) as tg:
                    sy.wait_ge(sm["mh"].h, sm["mh"].v(16 * tg - 16))
                    sy.dma_start(out=xfs.rearrange("p (q c f) -> p q c f", q=2, c=8)[:, tg % 2],
                                 in_=XFr.rearrange("p c (gg f) -> p gg c f", gg=8)[:, tg],
                                 ).then_inc(xfld_g.h, 16)
                    for tt in range(4):
                        j = 4 * tg + tt
                        sy.wait_ge(sm["ye"].h, sm["ye"].v(j + 1))
                        sy.dma_start(out=ACC_T[j], in_=cv(yev, 1024, tt % 2)).then_inc(accst.h, 16)
            xfld_g.n += 16 * 8
            accst.n += 16 * 32

            @blk.scalar
            def _(s_):
                with s_.Fori(0, 8) as tg:
                    for hc in range(16):
                        u = 16 * tg + hc
                        s_.wait_ge(sm["mh"].h, sm["mh"].v(u + 1))
                        if hc == 0:
                            s_.wait_ge(sm["my"].h, sm["my"].v(4 * tg - 4))
                        s_.activation(
                            h1b.rearrange("p (q c f) -> p q c f", q=2, c=16)[:, tg % 2, hc, :],
                            ph[hc % 2][:, :], AF.Gelu,
                            bias=sb1_sb[:, hc:hc + 1]).then_inc(sm["gh"].h, 1)

            @blk.vector
            def _(v):
                with v.Fori(0, 8) as tg:
                    for tt in range(4):
                        j = 4 * tg + tt
                        v.wait_ge(sm["my"].h, sm["my"].v(j + 1))
                        v.wait_ge(accst.h, accst0 + 16 * (j - 1))
                        v.tensor_copy(cv(yev, 1024, tt % 2), py[tt % 2][:, :]).then_inc(sm["ye"].h, 1)

            @blk.tensor
            def _(t):
                t.wait_ge(wle.h, wl1 + 32)
                t.wait_ge(sm["ldx"].h, sm["ldx"].v(32))
                with t.Fori(0, 4) as tg2:
                    for par in range(2):
                        tg = 2 * tg2 + par
                        t.wait_ge(xfld_g.h, xfld0 + 16 * (tg + 1))
                        for hc in range(16):
                            u = 16 * tg + hc
                            t.wait_ge(sm["gh"].h, sm["gh"].v(u - 1))
                            for kc in range(8):
                                ins = t.matmul(
                                    ph[hc % 2][:, :],
                                    sw1_sb[:, kc * 2048 + hc * 128:kc * 2048 + (hc + 1) * 128],
                                    xfs.rearrange("p (q c f) -> p q c f", q=2, c=8)[:, tg % 2, kc, :],
                                    start=(kc == 0), stop=(kc == 7))
                            ins.then_inc(sm["mh"].h, 1)
                        t.wait_ge(sm["gh"].h, sm["gh"].v(16 * (tg + 1)))
                        for tt in range(4):
                            j = 4 * tg + tt
                            t.wait_ge(sm["ye"].h, sm["ye"].v(j - 1))
                            for nd in range(2):
                                for kc in range(16):
                                    t.matmul(
                                        py[tt % 2][:, nd * 512:(nd + 1) * 512],
                                        h1b[:, par * 8192 + kc * 512 + tt * 128:
                                            par * 8192 + kc * 512 + (tt + 1) * 128],
                                        sw2_sb[:, kc * 1024 + nd * 512:kc * 1024 + (nd + 1) * 512],
                                        start=(kc == 0), stop=False)
                                ins = t.matmul(py[tt % 2][:, nd * 512:(nd + 1) * 512],
                                               g["ones"][0:1, 0:128],
                                               sb2_sb[0:1, nd * 512:(nd + 1) * 512],
                                               start=False, stop=True)
                            ins.then_inc(sm["my"].h, 1)
        totB = {"ldx": 32, "mh": 16 * 8, "gh": 16 * 8, "my": 32, "ye": 32}
    else:
        accst0 = accst.n
        totB = {}

    _scC = nc.enter_named_scope(cfg["name"] + "_C", False)[0]
    H_SL = 1024
    NSLAB = H // H_SL
    NPASS_h = 8 * NSLAB
    HCQ = H_SL // 128
    TGW = 512
    NTG_h = 4
    has_b = cfg.get("EB1") is not None
    KGP = 2 if has_b else 1
    esC = ExitStack()
    ew1_sb = esC.enter_context(nc.sbuf_tensor(nm("f_ew1"), [128, 2 * 8 * H_SL], BF16))
    ew2_sb = esC.enter_context(nc.sbuf_tensor(nm("f_ew2"), [128, 2 * HCQ * 1024], BF16))
    h1c = esC.enter_context(nc.sbuf_tensor(nm("f_h1c"), [128, 2 * HCQ * TGW], BF16))
    xfh = esC.enter_context(nc.sbuf_tensor(nm("f_xfh"), [128, 8 * 2048], BF16))
    accb = esC.enter_context(nc.sbuf_tensor(nm("f_accb"), [128, 16 * 1024], F32))
    xrd = esC.enter_context(nc.sbuf_tensor(nm("f_xrd"), [128, 2 * 1024], F32))
    junkd = esC.enter_context(nc.sbuf_tensor(nm("f_junkd"), [128, 1024], BF16))
    dstat = esC.enter_context(nc.sbuf_tensor(nm("f_dstat"), [128, 16], F32))
    eb1_sb = esC.enter_context(nc.sbuf_tensor(nm("f_eb1"), [128, 2 * 8], F32))
    eb2_sb = esC.enter_context(nc.sbuf_tensor(nm("f_eb2"), [1, 1024], F32))
    ew2l = pool.gsem("wl2")
    aldx_g = pool.gsem("aldx")
    xst_g = pool.gsem("xst2")
    wl2 = wle.n
    ew2l0 = ew2l.n
    xfld1 = xfld_g.n
    aldx0 = aldx_g.n
    xst0 = xst_g.n
    mh0 = totB.get("mh", 0)
    gh0 = totB.get("gh", 0)
    my0 = totB.get("my", 0)
    accv0 = 0
    ALD_H = 16 * 16 * (2 if has_b else 1)
    SY_H = NPASS_h + (8 if has_b else 0)
    with (
        esC,
        nc.psum_tensor(nm("c_ph0"), [128, TGW], F32) as cph0,
        nc.psum_tensor(nm("c_ph1"), [128, TGW], F32) as cph1,
        nc.psum_tensor(nm("c_py0"), [128, 1024], F32) as cpy0,
        nc.psum_tensor(nm("c_py1"), [128, 1024], F32) as cpy1,
        nc.Block() as blk,
    ):
        cph = (cph0, cph1)
        cpy = (cpy0, cpy1)
        EW1v = cfg["EW1"].rearrange("ee (c p) (s f) -> ee s p c f", p=128, s=NSLAB)
        EW2v = cfg["EW2"].rearrange("ee (s c p) f -> ee s p c f", p=128, s=NSLAB)
        ew1v = ew1_sb.rearrange("p (q c f) -> p q c f", q=2, c=8)
        ew2v = ew2_sb.rearrange("p (q c f) -> p q c f", q=2, c=HCQ)
        h1cv = h1c.rearrange("p (q c f) -> p q c f", q=2, c=HCQ)
        xfhv = xfh.rearrange("p (c g f) -> p c g f", c=8, g=NTG_h)
        wsb3 = wsb.rearrange("p (je u) -> p je u", u=1)

        @blk.gpsimd
        def _(gp):
            for th in range(2):
                with gp.Fori(0, 8) as e:
                    for sl in range(NSLAB):
                        P = th * NPASS_h + NSLAB * e + sl
                        buf = sl % 2
                        gp.wait_ge(sm["my"].h, sm["my"].v(my0 + 16 * (P - 1)))
                        gp.dma_start(out=ew1v[:, buf], in_=EW1v[e, sl]).then_inc(wle.h, 16)
                        if has_b:
                            gp.dma_start(
                                out=eb1_sb[:, buf * 8:buf * 8 + 8],
                                in_=cfg["EB1"].rearrange("ee pp (s c) -> ee s pp c",
                                                         s=NSLAB)[e, sl],
                            ).then_inc(wle.h, 16)
                        gp.dma_start(out=ew2v[:, buf], in_=EW2v[e, sl]).then_inc(ew2l.h, 16)
                        if has_b and sl == NSLAB - 1:
                            gp.dma_start(out=eb2_sb[:, :], in_=cfg["EB2"][e]).then_inc(ew2l.h, 16)
            gp.wait_ge(wle.h, wl2 + 16 * KGP * 2 * NPASS_h)
            gp.wait_ge(xst_g.h, xst0 + 16 * 32)
        wle.n += 16 * KGP * 2 * NPASS_h

        @blk.sync
        def _(sy):
            for th in range(2):
                sy.wait_ge(sm["mh"].h, sm["mh"].v(mh0 + 32 * NPASS_h * th))
                sy.dma_start(out=xfh.rearrange("p (c f) -> p c f", c=8),
                             in_=XFr[:, :, th * 2048:(th + 1) * 2048]).then_inc(xfld_g.h, 16)
                if has_b:
                    for j in range(16):
                        sy.wait_ge(accst.h, accst0 + 16 * (th * 16 + j + 1))
                        if th == 1:
                            sy.wait_ge(sm["dlnv"].h, sm["dlnv"].v(j + 1))
                        sy.dma_start(out=cv(accb, 1024, j),
                                     in_=ACC_T[th * 16 + j]).then_inc(aldx_g.h, 16)
                for j in range(16):
                    jg = th * 16 + j
                    if jg >= 2:
                        sy.wait_ge(xst_g.h, xst0 + 16 * (jg - 1))
                    sy.dma_start(out=cv(xrd, 1024, j % 2), in_=XIN_T[jg]).then_inc(aldx_g.h, 16)
        xfld_g.n += 32
        aldx_g.n += 2 * ALD_H
        ew2l.n += 16 * 2 * SY_H

        @blk.scalar
        def _(s_):
            for th in range(2):
                with s_.Fori(0, 8) as e:
                    for sl in range(NSLAB):
                        P = th * NPASS_h + NSLAB * e + sl
                        buf = sl % 2
                        with s_.Fori(0, NTG_h) as tg:
                            for hc in range(HCQ):
                                u = 32 * P + 8 * tg + hc
                                s_.wait_ge(sm["mh"].h, sm["mh"].v(mh0 + u + 1))
                                if hc == 0:
                                    s_.wait_ge(sm["my"].h,
                                               sm["my"].v(my0 + 16 * P + 4 * tg - 4))
                                if has_b:
                                    s_.activation(
                                        h1cv[:, tg % 2, hc, :], cph[hc % 2][:, :], AF.Gelu,
                                        bias=eb1_sb[:, buf * 8 + hc:buf * 8 + hc + 1],
                                    ).then_inc(sm["gh"].h, 1)
                                else:
                                    s_.activation(h1cv[:, tg % 2, hc, :], cph[hc % 2][:, :],
                                                  AF.Gelu).then_inc(sm["gh"].h, 1)
                for j in range(16):
                    jg = th * 16 + j
                    p2 = jg % 2
                    accj = cv(accb, 1024, j)
                    s_.wait_ge(sm["dr"].h, sm["dr"].v(jg + 1))
                    if jg >= 2:
                        s_.wait_ge(sm["dssq"].h, sm["dssq"].v(jg - 1))
                        s_.wait_ge(sm["dsrs"].h, sm["dsrs"].v(2 * jg - 2))
                        s_.wait_ge(sm["dvvar"].h, sm["dvvar"].v(jg - 1))
                    s_.activation(junkd[:, :], accj, AF.Square,
                                  accum_out=dstat[:, 12 + p2:13 + p2]).then_inc(sm["dssq"].h, 1)
                    s_.wait_ge(sm["dvvar"].h, sm["dvvar"].v(jg + 1))
                    s_.activation(dstat[:, 8 + p2:9 + p2], dstat[:, 6 + p2:7 + p2],
                                  AF.Ln, bias=g["epsl"]).then_inc(sm["dsrs"].h, 1)
                    s_.wait_ge(sm["dsrs"].h, sm["dsrs"].v(2 * jg + 1))
                    s_.activation(dstat[:, 10 + p2:11 + p2], dstat[:, 8 + p2:9 + p2],
                                  AF.Exp, scale=-0.5).then_inc(sm["dsrs"].h, 1)
                    s_.wait_ge(sm["dlnv"].h, sm["dlnv"].v(jg + 1))
                    s_.dma_start(out=XOUT_T[jg], in_=cv(xrd, 1024, p2)).then_inc(xst_g.h, 16)

        @blk.vector
        def _(v):
            for th in range(2):
                if has_b:
                    v.wait_ge(aldx_g.h, aldx0 + th * ALD_H + 16 * 16)
                else:
                    for q in range(4):
                        v.memset(cv(accb, 4096, q), 0.0)
                with v.Fori(0, 8) as e:
                    for sl in range(NSLAB):
                        P = th * NPASS_h + NSLAB * e + sl
                        with v.Fori(0, NTG_h) as tg:
                            for tt in range(4):
                                un = 16 * P + 4 * tg + tt
                                v.wait_ge(sm["my"].h, sm["my"].v(my0 + un + 1))
                                v.scalar_tensor_tensor(
                                    cv(accb, 1024, 4 * tg + tt), cpy[tt % 2][:, :],
                                    wsb3[:, (th * 16 + 4 * tg + tt) * 8 + e, :],
                                    cv(accb, 1024, 4 * tg + tt),
                                    ALU.mult, ALU.add).then_inc(sm["accv"].h, 1)
                for j in range(16):
                    jg = th * 16 + j
                    p2 = jg % 2
                    accj = cv(accb, 1024, j)
                    v.wait_ge(aldx_g.h, aldx0 + th * ALD_H
                              + (256 if has_b else 0) + 16 * (j + 1))
                    if jg >= 2:
                        v.wait_ge(xst_g.h, xst0 + 16 * (jg - 1))
                    v.tensor_add(accj, accj, cv(xrd, 1024, p2)).then_inc(sm["dr"].h, 1)
                    v.reduce_sum(dstat[:, p2:p2 + 1], accj, AX.X).then_inc(sm["ds3"].h, 1)
                    v.wait_ge(sm["ds3"].h, sm["ds3"].v(3 * jg + 1))
                    v.tensor_scalar(dstat[:, 2 + p2:3 + p2], dstat[:, p2:p2 + 1],
                                    -1.0 / D, None, ALU.mult).then_inc(sm["ds3"].h, 1)
                    v.wait_ge(sm["ds3"].h, sm["ds3"].v(3 * jg + 2))
                    v.tensor_mul(dstat[:, 4 + p2:5 + p2], dstat[:, 2 + p2:3 + p2],
                                 dstat[:, 2 + p2:3 + p2]).then_inc(sm["ds3"].h, 1)
                    v.wait_ge(sm["dssq"].h, sm["dssq"].v(jg + 1))
                    v.wait_ge(sm["ds3"].h, sm["ds3"].v(3 * jg + 3))
                    v.tensor_scalar(dstat[:, 6 + p2:7 + p2], dstat[:, 12 + p2:13 + p2],
                                    1.0 / D, dstat[:, 4 + p2:5 + p2],
                                    ALU.mult, ALU.subtract).then_inc(sm["dvvar"].h, 1)
                    v.wait_ge(sm["dsrs"].h, sm["dsrs"].v(2 * jg + 2))
                    v.tensor_scalar(cv(xrd, 1024, p2), accj,
                                    dstat[:, 2 + p2:3 + p2], dstat[:, 10 + p2:11 + p2],
                                    ALU.add, ALU.mult).then_inc(sm["dlnv"].h, 1)
            v.wait_ge(xst_g.h, xst0 + 16 * 32)
        xst_g.n += 16 * 32

        @blk.tensor
        def _(t):
            for th in range(2):
                t.wait_ge(xfld_g.h, xfld1 + 16 * (th + 1))
                with t.Fori(0, 8) as e:
                    for sl in range(NSLAB):
                        P = th * NPASS_h + NSLAB * e + sl
                        buf = sl % 2
                        t.wait_ge(wle.h, wl2 + 16 * KGP * (th * NPASS_h + NSLAB * e + sl + 1))
                        if has_b:
                            t.wait_ge(ew2l.h, ew2l0 + 16 * (th * SY_H + 3 * e + sl + 1
                                                            + (1 if sl == NSLAB - 1 else 0)))
                        else:
                            t.wait_ge(ew2l.h, ew2l0 + 16 * (th * SY_H + NSLAB * e + sl + 1))
                        with t.Fori(0, NTG_h // 2) as tg2:
                            for par in range(2):
                                tg = 2 * tg2 + par
                                for hc in range(HCQ):
                                    u = 32 * P + 8 * tg + hc
                                    t.wait_ge(sm["gh"].h, sm["gh"].v(gh0 + u - 1))
                                    for kc in range(8):
                                        ins = t.matmul(
                                            cph[hc % 2][:, :],
                                            ew1v[:, buf, kc, hc * 128:(hc + 1) * 128],
                                            xfhv[:, kc, tg, :],
                                            start=(kc == 0), stop=(kc == 7))
                                    ins.then_inc(sm["mh"].h, 1)
                                t.wait_ge(sm["gh"].h, sm["gh"].v(gh0 + 32 * P + 8 * tg + 8))
                                for tt in range(4):
                                    un = 16 * P + 4 * tg + tt
                                    t.wait_ge(sm["accv"].h, sm["accv"].v(accv0 + un - 1))
                                    for nd in range(2):
                                        for kc in range(HCQ):
                                            ins = t.matmul(
                                                cpy[tt % 2][:, nd * 512:(nd + 1) * 512],
                                                h1cv[:, par, kc, tt * 128:(tt + 1) * 128],
                                                ew2v[:, buf, kc, nd * 512:(nd + 1) * 512],
                                                start=(kc == 0),
                                                stop=(not (has_b and sl == NSLAB - 1)
                                                      and kc == HCQ - 1))
                                        if has_b and sl == NSLAB - 1:
                                            ins = t.matmul(
                                                cpy[tt % 2][:, nd * 512:(nd + 1) * 512],
                                                g["ones"][0:1, 0:128],
                                                eb2_sb[0:1, nd * 512:(nd + 1) * 512],
                                                start=False, stop=True)
                                    ins.then_inc(sm["my"].h, 1)
    nc.leave_named_scope(cfg["name"] + "_C", _scC, False)
    totC = {"mh": 64 * NPASS_h, "gh": 64 * NPASS_h, "my": 32 * NPASS_h,
            "accv": 32 * NPASS_h}
    esP.close()
    totD = {"dr": 32, "ds3": 96, "dssq": 32, "dvvar": 32, "dsrs": 64, "dlnv": 32}

    tot = {}
    for d in (totA, totB, totC, totD):
        for k, v_ in d.items():
            tot[k] = tot.get(k, 0) + v_
    pool.finish(tot)


def emit_attn_l1(nc, pool, g, cfg):
    sm = pool.take(["ql", "ttk", "etk", "ms", "rv", "se", "vm", "vsL", "pn",
                    "tp2", "ep", "mo", "oe", "radd", "mpj", "xld", "xst",
                    "ssq2", "srs", "vvar", "lnv", "vs3", "aotd", "aofd",
                    "aostp", "aoldp", "xrlp"])
    wl = pool.gsem("wl")
    aost = sm["aostp"]
    aold = sm["aoldp"]
    QS, KS, VS, XRES = cfg["QS"], cfg["KS"], cfg["VS"], cfg["XRES"]
    WO, BO, XOUT = cfg["WO"], cfg["BO"], cfg["XOUT"]
    AO = nc.dram_tensor(cfg["name"] + "_ao", [T, D], BF16, kind="Internal").ap()
    if cfg.get("dbg_ao"):
        AODBG = nc.dram_tensor(cfg["name"] + "_aodbg", [T, D], F32, kind="ExternalOutput").ap()
    NG = cfg.get("ng", T * 16 // 128)
    NGQ = NG // 4
    wl0, aost0, aold0 = wl.n, 0, 0
    Q2 = QS.rearrange("t (h d) -> (t h) d", d=64).rearrange("(i j p) d -> i p j d", j=4, p=128)
    K2 = KS.rearrange("t (h d) -> (t h) d", d=64).rearrange("(i j p) d -> i p j d", j=4, p=128)
    V2 = VS.rearrange("t (h d) -> (t h) d", d=64).rearrange("(i j p) d -> i p j d", j=4, p=128)
    AO2 = AO.rearrange("t (h d) -> (t h) d", d=64).rearrange("(i j p) d -> i p j d", j=4, p=128)
    AOT = AO.rearrange("(t p) d -> t p d", p=128)
    XRT = XRES.rearrange("(t p) d -> t p d", p=128)
    XOT = XOUT.rearrange("(t p) d -> t p d", p=128)

    es = ExitStack()
    qg = es.enter_context(nc.sbuf_tensor(nm("m_qg"), [128, 8 * 256], BF16))
    kg = es.enter_context(nc.sbuf_tensor(nm("m_kg"), [128, 8 * 256], BF16))
    vg = es.enter_context(nc.sbuf_tensor(nm("m_vg"), [128, 8 * 256], BF16))
    tkq = es.enter_context(nc.sbuf_tensor(nm("m_tkq"), [64, 8 * 1024], BF16))
    Ef = es.enter_context(nc.sbuf_tensor(nm("m_Ef"), [128, 8 * 512], F32))
    Em = es.enter_context(nc.sbuf_tensor(nm("m_Em"), [128, 8 * 512], BF16))
    PnT = es.enter_context(nc.sbuf_tensor(nm("m_PnT"), [128, 8 * 512], BF16))
    aosb = es.enter_context(nc.sbuf_tensor(nm("m_ao"), [128, 8 * 256], BF16))
    msksb = es.enter_context(nc.sbuf_tensor(nm("m_msk"), [128, 512], F32))
    gstat = es.enter_context(nc.sbuf_tensor(nm("m_gs"), [128, 128], F32))
    wo_sb = es.enter_context(nc.sbuf_tensor(nm("m_wo"), [128, 8 * 1024], BF16))
    bo_sb = es.enter_context(nc.sbuf_tensor(nm("m_bo"), [1, 1024], F32))
    aot = es.enter_context(nc.sbuf_tensor(nm("m_aot"), [128, 2 * 1024], BF16))
    afm = es.enter_context(nc.sbuf_tensor(nm("m_afm"), [128, 2 * 1024], BF16))
    xrt = es.enter_context(nc.sbuf_tensor(nm("m_xrt"), [128, 2 * 1024], F32))
    rt = es.enter_context(nc.sbuf_tensor(nm("m_rt"), [128, 2 * 1024], F32))
    ob = es.enter_context(nc.sbuf_tensor(nm("m_ob"), [128, 2 * 1024], F32))
    junkP = es.enter_context(nc.sbuf_tensor(nm("m_junk"), [128, 2 * 1024], BF16))
    pstat = es.enter_context(nc.sbuf_tensor(nm("m_ps"), [128, 16], F32))
    with (
        es,
        nc.psum_tensor(nm("m_pS0"), [128, 512], F32) as pS0,
        nc.psum_tensor(nm("m_pS1"), [128, 512], F32) as pS1,
        nc.psum_tensor(nm("m_pS2"), [128, 512], F32) as pS2,
        nc.psum_tensor(nm("m_pS3"), [128, 512], F32) as pS3,
        nc.psum_tensor(nm("m_pTK0"), [64, 1024], BF16) as pTK0,
        nc.psum_tensor(nm("m_pTK1"), [64, 1024], BF16) as pTK1,
        nc.psum_tensor(nm("m_pPO0"), [128, 512], F32) as pPO0,
        nc.psum_tensor(nm("m_pPO1"), [128, 512], F32) as pPO1,
        nc.Block() as blk,
    ):
        pS = (pS0, pS1, pS2, pS3)
        pTK = (pTK0, pTK1)
        pP = (pPO0[:, 0:256].bitcast(BF16), pPO1[:, 0:256].bitcast(BF16))
        pO = (pPO0[:, 256:512], pPO1[:, 256:512])

        @blk.gpsimd
        def _(gp):
            gp.dma_start(out=wo_sb.rearrange("p (c f) -> p c f", c=8),
                         in_=WO.rearrange("(c p) f -> p c f", p=128)).then_inc(wl.h, 16)
            with gp.Fori(0, NGQ // 8) as i4:
              for par in range(8):
                i = 8 * i4 + par
                gp.wait_ge(sm["oe"].h, sm["oe"].v(i + 1))
                gp.dma_start(out=AO2[i],
                             in_=cv(aosb, 256, par).rearrange("p (j d) -> p j d", j=4),
                             ).then_inc(aost.h, 16)
            gp.wait_ge(aost.h, aost.v(aost0 + 16 * NGQ))
            gp.wait_ge(wl.h, wl0 + 16)
        wl.n += 16

        @blk.sync
        def _(sy):
            sy.dma_start(out=bo_sb[:, :], in_=BO).then_inc(sm["xld"].h, 16)
            sy.dma_start(out=msksb[:, :], in_=cfg["MASK"]).then_inc(sm["xld"].h, 16)
            with sy.Fori(0, NGQ // 8) as i4:
              for par in range(8):
                i = 8 * i4 + par
                sy.wait_ge(sm["mo"].h, sm["mo"].v(i - 7))
                sy.dma_start(out=cv(qg, 256, par).rearrange("p (j d) -> p j d", j=4),
                             in_=Q2[i]).then_inc(sm["ql"].h, 16)
                sy.dma_start(out=cv(kg, 256, par).rearrange("p (j d) -> p j d", j=4),
                             in_=K2[i]).then_inc(sm["ql"].h, 16)
                sy.dma_start(out=cv(vg, 256, par).rearrange("p (j d) -> p j d", j=4),
                             in_=V2[i]).then_inc(sm["ql"].h, 16)


        @blk.scalar
        def _(s_):
            with s_.Fori(0, NGQ // 8) as i4:
              for par in range(8):
                i = 8 * i4 + par
                gb = par * 16
                s_.wait_ge(sm["rv"].h, sm["rv"].v(i + 1))
                s_.wait_ge(sm["vm"].h, sm["vm"].v(i - 7))
                s_.activation(cv(Ef, 512, par), pS[par % 4][:, :], AF.Exp,
                              bias=gstat[:, gb:gb + 1]).then_inc(sm["se"].h, 1)

        @blk.vector
        def _(v):
            with v.Fori(0, NGQ // 8) as i4:
              for par in range(8):
                i = 8 * i4 + par
                gb = par * 16
                v.wait_ge(sm["ttk"].h, sm["ttk"].v(i + 1))
                v.wait_ge(sm["ms"].h, sm["ms"].v(4 * (i - 7)))
                v.tensor_copy(cv(tkq, 1024, par), pTK[par % 2][:, :]).then_inc(sm["etk"].h, 1)
                v.wait_ge(sm["ms"].h, sm["ms"].v(4 * (i + 1)))
                v.wait_ge(sm["se"].h, sm["se"].v(i - 7))
                v.wait_ge(sm["oe"].h, sm["oe"].v(i - 7))
                v.reduce_max(gstat[:, gb:gb + 1], pS[par % 4][:, :], AX.X,
                             negate=True).then_inc(sm["rv"].h, 1)
                v.wait_ge(sm["se"].h, sm["se"].v(i + 1))
                v.wait_ge(sm["tp2"].h, sm["tp2"].v(i - 7))
                v.scalar_tensor_tensor(cv(Em, 512, par), cv(Ef, 512, par), 1.0,
                                       msksb[:, :], ALU.mult, ALU.mult).then_inc(sm["vm"].h, 1)
                for j in range(4):
                    ins = v.reduce_sum(gstat[:, gb + 1 + j:gb + 2 + j],
                                       Em[:, par * 512 + j * 128:par * 512 + (j + 1) * 128],
                                       AX.X)
                ins.then_inc(sm["vsL"].h, 1)
                v.wait_ge(sm["vsL"].h, sm["vsL"].v(2 * i + 1))
                v.reciprocal(gstat[:, gb + 5:gb + 9],
                             gstat[:, gb + 1:gb + 5]).then_inc(sm["vsL"].h, 1)
                v.wait_ge(sm["tp2"].h, sm["tp2"].v(i + 1))
                v.wait_ge(sm["mo"].h, sm["mo"].v(i - 7))
                v.tensor_copy(cv(PnT, 512, par), pP[par % 2][:, :]).then_inc(sm["ep"].h, 1)
                v.wait_ge(sm["vsL"].h, sm["vsL"].v(2 * i + 2))
                v.wait_ge(sm["mo"].h, sm["mo"].v(i + 1))
                v.wait_ge(aost.h, aost.v(aost0 + 16 * (i - 7)))
                for j in range(4):
                    ins = v.tensor_scalar(
                        aosb[:, par * 256 + j * 64:par * 256 + (j + 1) * 64],
                        pO[par % 2][:, j * 64:(j + 1) * 64],
                        gstat[:, gb + 5 + j:gb + 6 + j], None, ALU.mult)
                ins.then_inc(sm["oe"].h, 1)

        @blk.tensor
        def _(t):
            t.wait_ge(sm["xld"].h, sm["xld"].v(32))
            t.wait_ge(wl.h, wl0 + 16)
            with t.Fori(0, NGQ // 8) as i4:
              for par in range(8):
                i = 8 * i4 + par
                t.wait_ge(sm["ql"].h, sm["ql"].v(48 * (i + 1)))
                t.wait_ge(sm["etk"].h, sm["etk"].v(i - 1))
                for j in range(4):
                    t.transpose(pTK[par % 2][0:64, j * 128:(j + 1) * 128],
                                qg[:, par * 256 + j * 64:par * 256 + (j + 1) * 64],
                                g["idn"][:, :])
                for j in range(4):
                    ins = t.transpose(pTK[par % 2][0:64, 512 + j * 128:512 + (j + 1) * 128],
                                      kg[:, par * 256 + j * 64:par * 256 + (j + 1) * 64],
                                      g["idn"][:, :])
                ins.then_inc(sm["ttk"].h, 1)
                t.wait_ge(sm["etk"].h, sm["etk"].v(i + 1))
                t.wait_ge(sm["se"].h, sm["se"].v(i - 3))
                for j in range(4):
                    t.matmul(pS[par % 4][:, j * 128:(j + 1) * 128],
                             tkq[0:64, par * 1024 + j * 128:par * 1024 + (j + 1) * 128],
                             tkq[0:64, par * 1024 + 512 + j * 128:par * 1024 + 512 + (j + 1) * 128],
                             start=True, stop=True).then_inc(sm["ms"].h, 1)
                t.wait_ge(sm["vm"].h, sm["vm"].v(i + 1))
                t.wait_ge(sm["ep"].h, sm["ep"].v(i - 1))
                for j in range(4):
                    ins = t.transpose(pP[par % 2][:, j * 128:(j + 1) * 128],
                                      Em[:, par * 512 + j * 128:par * 512 + (j + 1) * 128],
                                      g["idn"][:, :])
                ins.then_inc(sm["tp2"].h, 1)
                t.wait_ge(sm["ep"].h, sm["ep"].v(i + 1))
                t.wait_ge(sm["oe"].h, sm["oe"].v(i - 1))
                for j in range(4):
                    ins = t.matmul(pO[par % 2][:, j * 64:(j + 1) * 64],
                                   PnT[:, par * 512 + j * 128:par * 512 + (j + 1) * 128],
                                   vg[:, par * 256 + j * 64:par * 256 + (j + 1) * 64],
                                   start=True, stop=True)
                ins.then_inc(sm["mo"].h, 1)
    totg = {"ql": 48 * NGQ, "ttk": NGQ, "etk": NGQ, "ms": 4 * NGQ,
            "rv": NGQ, "se": NGQ, "vm": NGQ, "vsL": 2 * NGQ,
            "tp2": NGQ, "ep": NGQ, "mo": NGQ, "oe": NGQ,
            "aostp": 16 * NGQ, "xld": 32}
    if cfg.get("skip_proj"):
        gdmp = pool.gsem("dump")
        with nc.Block() as blkd:
            @blkd.gpsimd
            def _(gp):
                gp.dma_start(out=AODBG, in_=AO).then_inc(gdmp.h, 16)
                gdmp.n += 16
                gp.wait_ge(gdmp.h, gdmp.n)
        pool.finish(totg)
        return
    if cfg.get("dbg_ao"):
        gdmp = pool.gsem("dump")
        with nc.Block() as blkd:
            @blkd.gpsimd
            def _(gp):
                gp.dma_start(out=AODBG, in_=AO).then_inc(gdmp.h, 16)
                gdmp.n += 16
                gp.wait_ge(gdmp.h, gdmp.n)
    with (
        nc.psum_tensor(nm("m_po0"), [128, 1024], F32) as po0,
        nc.psum_tensor(nm("m_po1"), [128, 1024], F32) as po1,
        nc.psum_tensor(nm("m_pt0"), [128, 1024], BF16) as pt0,
        nc.psum_tensor(nm("m_pt1"), [128, 1024], BF16) as pt1,
        nc.Block() as blk2,
    ):
        po = (po0, po1)
        pt2 = (pt0, pt1)

        @blk2.sync
        def _(sy):
            for tt in range(32):
                p2 = tt % 2
                if tt >= 2:
                    sy.wait_ge(sm["aotd"].h, sm["aotd"].v(tt - 1))
                sy.dma_start(out=cv(aot, 1024, p2), in_=AOT[tt]).then_inc(aold.h, 16)
                if tt >= 2:
                    sy.wait_ge(sm["radd"].h, sm["radd"].v(tt - 1))
                sy.dma_start(out=cv(xrt, 1024, p2), in_=XRT[tt]).then_inc(sm["xrlp"].h, 16)
                sy.wait_ge(sm["lnv"].h, sm["lnv"].v(tt + 1))
                sy.dma_start(out=XOT[tt], in_=cv(ob, 1024, p2)).then_inc(sm["xst"].h, 16)
            sy.wait_ge(sm["xst"].h, sm["xst"].v(16 * 32))

        @blk2.scalar
        def _(s_):
            for tt in range(32):
                p2 = tt % 2
                s_.wait_ge(sm["radd"].h, sm["radd"].v(tt + 1))
                if tt >= 2:
                    s_.wait_ge(sm["ssq2"].h, sm["ssq2"].v(tt - 1))
                    s_.wait_ge(sm["srs"].h, sm["srs"].v(2 * tt - 2))
                    s_.wait_ge(sm["vvar"].h, sm["vvar"].v(tt - 1))
                s_.activation(cv(junkP, 1024, p2), cv(rt, 1024, p2), AF.Square,
                              accum_out=pstat[:, 12 + p2:13 + p2]).then_inc(sm["ssq2"].h, 1)
                s_.wait_ge(sm["vvar"].h, sm["vvar"].v(tt + 1))
                s_.activation(pstat[:, 8 + p2:9 + p2], pstat[:, 6 + p2:7 + p2],
                              AF.Ln, bias=g["epsl"]).then_inc(sm["srs"].h, 1)
                s_.wait_ge(sm["srs"].h, sm["srs"].v(2 * tt + 1))
                s_.activation(pstat[:, 10 + p2:11 + p2], pstat[:, 8 + p2:9 + p2],
                              AF.Exp, scale=-0.5).then_inc(sm["srs"].h, 1)

        @blk2.vector
        def _(v):
            for tt in range(32):
                p2 = tt % 2
                v.wait_ge(sm["aotd"].h, sm["aotd"].v(tt + 1))
                if tt >= 2:
                    v.wait_ge(sm["mpj"].h, sm["mpj"].v(2 * tt - 2))
                v.tensor_copy(cv(afm, 1024, p2), pt2[p2][:, :]).then_inc(sm["aofd"].h, 1)
                v.wait_ge(sm["mpj"].h, sm["mpj"].v(2 * tt + 2))
                v.wait_ge(aold.h, aold.v(aold0 + 16 * (tt + 1)))
                v.wait_ge(sm["xrlp"].h, sm["xrlp"].v(16 * (tt + 1)))
                if tt >= 2:
                    v.wait_ge(sm["xst"].h, sm["xst"].v(16 * (tt - 1)))
                v.tensor_add(cv(rt, 1024, p2), po[p2][:, :],
                             cv(xrt, 1024, p2)).then_inc(sm["radd"].h, 1)
                v.wait_ge(sm["radd"].h, sm["radd"].v(tt + 1))
                v.reduce_sum(pstat[:, p2:p2 + 1], cv(rt, 1024, p2), AX.X).then_inc(sm["vs3"].h, 1)
                v.wait_ge(sm["vs3"].h, sm["vs3"].v(3 * tt + 1))
                v.tensor_scalar(pstat[:, 2 + p2:3 + p2], pstat[:, p2:p2 + 1],
                                -1.0 / D, None, ALU.mult).then_inc(sm["vs3"].h, 1)
                v.wait_ge(sm["vs3"].h, sm["vs3"].v(3 * tt + 2))
                v.tensor_mul(pstat[:, 4 + p2:5 + p2], pstat[:, 2 + p2:3 + p2],
                             pstat[:, 2 + p2:3 + p2]).then_inc(sm["vs3"].h, 1)
                v.wait_ge(sm["ssq2"].h, sm["ssq2"].v(tt + 1))
                v.wait_ge(sm["vs3"].h, sm["vs3"].v(3 * tt + 3))
                v.tensor_scalar(pstat[:, 6 + p2:7 + p2], pstat[:, 12 + p2:13 + p2],
                                1.0 / D, pstat[:, 4 + p2:5 + p2],
                                ALU.mult, ALU.subtract).then_inc(sm["vvar"].h, 1)
                v.wait_ge(sm["srs"].h, sm["srs"].v(2 * tt + 2))
                v.tensor_scalar(cv(ob, 1024, p2), cv(rt, 1024, p2),
                                pstat[:, 2 + p2:3 + p2], pstat[:, 10 + p2:11 + p2],
                                ALU.add, ALU.mult).then_inc(sm["lnv"].h, 1)

        @blk2.tensor
        def _(t):
            for tt in range(32):
                p2 = tt % 2
                t.wait_ge(aold.h, aold.v(aold0 + 16 * (tt + 1)))
                if tt >= 2:
                    t.wait_ge(sm["aofd"].h, sm["aofd"].v(tt - 1))
                for c in range(8):
                    ins = t.transpose(cv(pt2[p2], 128, c),
                                      aot[:, p2 * 1024 + c * 128:p2 * 1024 + (c + 1) * 128],
                                      g["idn"][:, :])
                ins.then_inc(sm["aotd"].h, 1)
                t.wait_ge(sm["aofd"].h, sm["aofd"].v(tt + 1))
                if tt >= 2:
                    t.wait_ge(sm["radd"].h, sm["radd"].v(tt - 1))
                for nd in range(2):
                    for kc in range(8):
                        t.matmul(po[p2][:, nd * 512:(nd + 1) * 512],
                                 afm[:, p2 * 1024 + kc * 128:p2 * 1024 + (kc + 1) * 128],
                                 wo_sb[:, kc * 1024 + nd * 512:kc * 1024 + (nd + 1) * 512],
                                 start=(kc == 0), stop=False)
                    t.matmul(po[p2][:, nd * 512:(nd + 1) * 512],
                             g["ones"][0:1, 0:128], bo_sb[0:1, nd * 512:(nd + 1) * 512],
                             start=False, stop=True).then_inc(sm["mpj"].h, 1)
    tot2 = dict(totg)
    tot2.update({"radd": 32, "mpj": 64, "xst": 16 * 32,
                 "ssq2": 32, "srs": 64, "vvar": 32, "lnv": 32, "vs3": 96,
                 "aotd": 32, "aofd": 32, "aoldp": 16 * 32, "xrlp": 16 * 32})
    pool.finish(tot2)


from contextlib import ExitStack


def _new_prog():
    nc = bacc.Bacc("TRN2", target_bir_lowering=False)
    ctx = ExitStack()
    pool = Pool(nc, ctx)
    return nc, ctx, pool


def _global_prologue(nc, ctx, pool):
    IDENT = nc.dram_tensor("c_ident", [128, 128], F32, kind="ExternalInput").ap()
    pool.seed_src = IDENT[0:1, 0:16]
    ONES = nc.dram_tensor("c_ones1", [1, 512], F32, kind="ExternalInput").ap()
    idn = ctx.enter_context(nc.sbuf_tensor(nm("g_idn"), [128, 128], BF16))
    ones = ctx.enter_context(nc.sbuf_tensor(nm("g_ones"), [1, 512], F32))
    eps = ctx.enter_context(nc.sbuf_tensor(nm("g_eps"), [128, 2], F32))
    sp = pool.take(["pro"])["pro"]
    gw = pool.gsem("wl")
    gnames = ["wl", "qst", "vst", "xfst", "accst", "xfld", "aldx", "xst2", "dump"]
    with nc.Block() as blk:
        @blk.gpsimd
        def _(gp):
            gp.dma_start(out=idn[:, :], in_=IDENT).then_inc(gw.h, 16)
            gw.n += 16
            for nm in gnames:
                gs = pool.gsem(nm)
                pool.seed_dma(gp, gs, 4)
            for nm in gnames:
                gs = pool.gsem(nm)
                gp.wait_ge(gs.h, gs.n)
            gp.wait_ge(gw.h, gw.n)
        @blk.sync
        def _(sy):
            sy.dma_start(out=ones[:, :], in_=ONES).then_inc(sp.h, 16)
            sy.wait_ge(sp.h, sp.v(16))
            for s_ in pool.sems:
                sy.sem_inc(s_, 64)
        @blk.vector
        def _(v):
            v.memset(eps[:, 0:1], EPS_RMS)
            v.memset(eps[:, 1:2], EPS_LN)
    pool.counts = [c + 64 for c in pool.counts]
    pool.finish({"pro": 16})
    return {"idn": idn, "ones": ones, "epsr": eps[:, 0:1], "epsl": eps[:, 1:2]}


def _qkv_cfg(nc, layer, xs_ap=None):
    l = f"l{layer}"
    kind = "Internal"
    tm = (layer == 1)
    cfg = dict(
        rms=(layer == 0),
        tm_qk=tm,
        qscale=0.125,
        XS=xs_ap if xs_ap is not None else
            nc.dram_tensor(f"{l}_xs_in", [T, D], F32, kind="ExternalInput").ap(),
        WQ=nc.dram_tensor(f"{l}_wq", [D, D], F32, kind="ExternalInput").ap(),
        WK=nc.dram_tensor(f"{l}_wk", [D, D], F32, kind="ExternalInput").ap(),
        WV=nc.dram_tensor(f"{l}_wv", [D, D], F32, kind="ExternalInput").ap(),
        GAM=nc.dram_tensor(f"{l}_gam", [128, 8], F32, kind="ExternalInput").ap(),
        QS=nc.dram_tensor(f"{l}_qsc", [T, D] if tm else [D, T], BF16, kind=kind).ap(),
        KS=nc.dram_tensor(f"{l}_ksc", [T, D] if tm else [D, T], BF16, kind=kind).ap(),
        VS=nc.dram_tensor(f"{l}_vsc", [T, D], BF16, kind=kind).ap(),
    )
    return cfg


def _dump_f32(nc, pool, pairs):
    gd = pool.gsem("dump")
    with nc.Block() as blk:
        @blk.gpsimd
        def _(gp):
            for name, src, shape in pairs:
                dst = nc.dram_tensor(name, list(shape), F32, kind="ExternalOutput").ap()
                gp.dma_start(out=dst, in_=src).then_inc(gd.h, 16)
                gd.n += 16
            gp.wait_ge(gd.h, gd.n)


def _attn_cfg(nc, layer, qkv_cfg, xout_kind="Internal"):
    l = f"l{layer}"
    cfg = dict(
        name=f"{l}_attn",
        QS=qkv_cfg["QS"], KS=qkv_cfg["KS"], VS=qkv_cfg["VS"],
        XRES=qkv_cfg["XS"],
        WO=nc.dram_tensor(f"{l}_wo", [D, D], F32, kind="ExternalInput").ap(),
        BO=nc.dram_tensor(f"{l}_bo", [1, D], F32, kind="ExternalInput").ap(),
        XOUT=nc.dram_tensor(f"{l}_xattn_out", [T, D], F32, kind=xout_kind).ap(),
    )
    if layer == 1:
        cfg["MASK"] = nc.dram_tensor("c_mask", [128, 512], F32, kind="ExternalInput").ap()
    return cfg


def build_attn_test(layer=0):
    nc, ctx, pool = _new_prog()
    g = _global_prologue(nc, ctx, pool)
    qcfg = _qkv_cfg(nc, layer)
    emit_qkv(nc, pool, g, qcfg)
    acfg = _attn_cfg(nc, layer, qcfg, xout_kind="ExternalOutput")
    if layer == 0:
        emit_attn(nc, pool, g, acfg)
    else:
        emit_attn_l1(nc, pool, g, acfg)
    nc.finalize()
    return nc, ctx


def _ffn_cfg(nc, layer, xin_ap, xout_kind="Internal"):
    l = f"l{layer}"
    if layer == 0:
        return dict(
            layer=0, H=2048, shared=True, name=f"{l}_ffn",
            XIN=xin_ap,
            XOUT=nc.dram_tensor(f"{l}_xffn_out", [T, D], F32, kind=xout_kind).ap(),
            RW=nc.dram_tensor(f"{l}_rw", [D, 8], F32, kind="ExternalInput").ap(),
            RB=nc.dram_tensor(f"{l}_rb", [1, 8], F32, kind="ExternalInput").ap(),
            SW1=nc.dram_tensor(f"{l}_sw1", [D, 2048], F32, kind="ExternalInput").ap(),
            SB1=nc.dram_tensor(f"{l}_sb1", [128, 16], F32, kind="ExternalInput").ap(),
            SW2=nc.dram_tensor(f"{l}_sw2", [2048, D], F32, kind="ExternalInput").ap(),
            SB2=nc.dram_tensor(f"{l}_sb2", [1, D], F32, kind="ExternalInput").ap(),
            EW1=nc.dram_tensor(f"{l}_ew1", [8, D, 2048], F32, kind="ExternalInput").ap(),
            EB1=nc.dram_tensor(f"{l}_eb1", [8, 128, 16], F32, kind="ExternalInput").ap(),
            EW2=nc.dram_tensor(f"{l}_ew2", [8, 2048, D], F32, kind="ExternalInput").ap(),
            EB2=nc.dram_tensor(f"{l}_eb2", [8, 1, D], F32, kind="ExternalInput").ap(),
        )
    return dict(
        layer=1, H=4096, shared=False, name=f"{l}_ffn",
        XIN=xin_ap,
        XOUT=nc.dram_tensor(f"{l}_xffn_out", [T, D], F32, kind=xout_kind).ap(),
        RW=nc.dram_tensor(f"{l}_rw", [D, 8], F32, kind="ExternalInput").ap(),
        RB=None,
        EW1=nc.dram_tensor(f"{l}_ew1", [8, D, 4096], F32, kind="ExternalInput").ap(),
        EB1=None,
        EW2=nc.dram_tensor(f"{l}_ew2", [8, 4096, D], F32, kind="ExternalInput").ap(),
        EB2=None,
    )


def build_ffn_test(layer=0, debug=False):
    nc, ctx, pool = _new_prog()
    g = _global_prologue(nc, ctx, pool)
    xin = nc.dram_tensor("ffn_xin", [T, D], F32, kind="ExternalInput").ap()
    cfg = _ffn_cfg(nc, layer, xin, xout_kind="ExternalOutput")
    cfg["debug"] = debug
    emit_ffn(nc, pool, g, cfg)
    nc.finalize()
    return nc, ctx


def build_qkv_test(layer=0, ntg=8):
    nc, ctx, pool = _new_prog()
    g = _global_prologue(nc, ctx, pool)
    cfg = _qkv_cfg(nc, layer)
    cfg["ntg"] = ntg
    emit_qkv(nc, pool, g, cfg)
    l = f"l{layer}"
    _dump_f32(nc, pool, [(f"{l}_qs_o", cfg["QS"], (D, T)),
                         (f"{l}_ks_o", cfg["KS"], (D, T)),
                         (f"{l}_vs_o", cfg["VS"], (T, D))])
    nc.finalize()
    return nc, ctx


def build_full(nstages=4):
    nc, ctx, pool = _new_prog()
    g = _global_prologue(nc, ctx, pool)
    x0 = nc.dram_tensor("x0", [T, D], F32, kind="ExternalInput").ap()
    q0 = _qkv_cfg(nc, 0, xs_ap=x0)
    with nc.named_scope("qkv0"):
        emit_qkv(nc, pool, g, q0)
    a0 = _attn_cfg(nc, 0, q0, xout_kind="ExternalOutput" if nstages == 1 else "Internal")
    with nc.named_scope("attn0"):
        emit_attn(nc, pool, g, a0)
    out = "l0_xattn_out"
    if nstages >= 2:
        f0 = _ffn_cfg(nc, 0, a0["XOUT"], xout_kind="ExternalOutput" if nstages == 2 else "Internal")
        with nc.named_scope("ffn0"):
            emit_ffn(nc, pool, g, f0)
        out = "l0_xffn_out"
    if nstages >= 3:
        q1 = _qkv_cfg(nc, 1, xs_ap=f0["XOUT"])
        with nc.named_scope("qkv1"):
            emit_qkv(nc, pool, g, q1)
        a1 = _attn_cfg(nc, 1, q1, xout_kind="ExternalOutput" if nstages == 3 else "Internal")
        with nc.named_scope("attn1"):
            emit_attn_l1(nc, pool, g, a1)
        out = "l1_xattn_out"
    if nstages >= 4:
        f1 = _ffn_cfg(nc, 1, a1["XOUT"], xout_kind="ExternalOutput")
        with nc.named_scope("ffn1"):
            emit_ffn(nc, pool, g, f1)
        out = "l1_xffn_out"
    nc.finalize()
    return nc, ctx, out


def _in_maps(inputs):
    w = _prep_weights(inputs)
    xs = _split_x(inputs["L0_attn_x"])
    base = {
        "c_ident": w["ident"], "c_ones1": w["ones1"],
        "l0_wq": w["l0_wq"], "l0_wk": w["l0_wk"], "l0_wv": w["l0_wv"],
        "l0_gam": w["l0_gam"], "l0_wo": w["l0_wo"], "l0_bo": w["l0_bo"],
        "l0_rw": w["l0_rw"], "l0_rb": w["l0_rb"],
        "l0_sw1": w["l0_sw1"], "l0_sb1": w["l0_sb1"],
        "l0_sw2": w["l0_sw2"], "l0_sb2": w["l0_sb2"],
        "l0_ew1": w["l0_ew1"], "l0_eb1": w["l0_eb1"],
        "l0_ew2": w["l0_ew2"], "l0_eb2": w["l0_eb2"],
        "l1_wq": w["l1_wq"], "l1_wk": w["l1_wk"], "l1_wv": w["l1_wv"],
        "l1_gam": w["l1_gam"], "l1_wo": w["l1_wo"], "l1_bo": w["l1_bo"],
        "l1_rw": w["l1_rw"], "l1_ew1": w["l1_ew1"], "l1_ew2": w["l1_ew2"],
        "c_mask": w["mask"],
    }
    return [dict(base, x0=xs[c]) for c in range(NCORES)]


_CACHE = {}


def _split_x(x):
    x = np.ascontiguousarray(np.asarray(x, np.float32))
    return [np.ascontiguousarray(x[c * SPC:(c + 1) * SPC].reshape(T, D)) for c in range(NCORES)]


def kernel(**inputs):
    if "nc" not in _CACHE:
        _CACHE["nc"], _CACHE["ctx"], _CACHE["out"] = build_full()
    nc = _CACHE["nc"]
    in_maps = _in_maps(inputs)
    res = run_bass_kernel_spmd(nc, in_maps, core_ids=list(range(NCORES)))
    out = np.concatenate([r[_CACHE["out"]].reshape(SPC, S, D) for r in res.results], axis=0)
    return out


# version 10
# speedup vs baseline: 1.0997x; 1.0188x over previous
import numpy as np
import ml_dtypes
from contextlib import ExitStack
import concourse.bacc as bacc
import concourse.mybir as mybir
from concourse.bass_utils import run_bass_kernel_spmd

F32 = mybir.dt.float32
BF16 = mybir.dt.bfloat16
AF = mybir.ActivationFunctionType
ALU = mybir.AluOpType
AX = mybir.AxisListType

NCORES = 8
B, S, D = 16, 2048, 1024
SPC = B // NCORES
T = SPC * S
NH, DH = 16, 64
EPS_RMS = 1e-6
EPS_LN = 1e-5

BF16NP = ml_dtypes.bfloat16


def _prep_weights(inp):
    w = {}
    f32 = lambda a: np.ascontiguousarray(np.asarray(a, dtype=np.float32))
    w["l0_wq"] = f32(inp["L0_attn_wq"].reshape(D, D))
    w["l0_wk"] = f32(inp["L0_attn_wk"].reshape(D, D))
    w["l0_wv"] = f32(inp["L0_attn_wv"].reshape(D, D))
    w["l0_wo"] = f32(inp["L0_attn_w_out"].reshape(D, D))
    w["l0_bo"] = f32(inp["L0_attn_b_out"].reshape(1, D))
    w["l0_gam"] = f32(np.asarray(inp["L0_attn_norm_scale"]).reshape(8, 128).T)
    w["l0_rw"] = f32(inp["L0_ffn_router_w"])
    w["l0_rb"] = f32(inp["L0_ffn_router_b"].reshape(1, 8))
    w["l0_sw1"] = f32(inp["L0_ffn_sw1"])
    w["l0_sb1"] = f32(inp["L0_ffn_sb1"].reshape(16, 128).T)
    w["l0_sw2"] = f32(inp["L0_ffn_sw2"])
    w["l0_sb2"] = f32(inp["L0_ffn_sb2"].reshape(1, D))
    w["l0_ew1"] = f32(inp["L0_ffn_rw1"])
    w["l0_eb1"] = f32(np.asarray(inp["L0_ffn_rb1"]).reshape(8, 16, 128).transpose(0, 2, 1))
    w["l0_ew2"] = f32(inp["L0_ffn_rw2"])
    w["l0_eb2"] = f32(np.asarray(inp["L0_ffn_rb2"]).reshape(8, 1, D))
    qkv = np.asarray(inp["L1_attn_Wqkv"]).reshape(NH, 3, DH, D)
    w["l1_wq"] = f32(qkv[:, 0].transpose(2, 0, 1).reshape(D, D))
    w["l1_wk"] = f32(qkv[:, 1].transpose(2, 0, 1).reshape(D, D))
    w["l1_wv"] = f32(qkv[:, 2].transpose(2, 0, 1).reshape(D, D))
    w["l1_wo"] = f32(np.asarray(inp["L1_attn_Wproj"]).T)
    w["l1_bo"] = f32(inp["L1_attn_bproj"].reshape(1, D))
    w["l1_gam"] = f32(np.ones((128, 8), np.float32))
    w["l1_rw"] = f32(np.asarray(inp["L1_ffn_router_w"]).T)
    w["l1_ew1"] = f32(np.asarray(inp["L1_ffn_w1"]).transpose(0, 2, 1))
    w["l1_ew2"] = f32(np.asarray(inp["L1_ffn_w2"]).transpose(0, 2, 1))
    w["ident"] = np.eye(128, dtype=np.float32)
    w["ones1"] = np.ones((1, 512), np.float32)
    w["mask"] = np.ascontiguousarray(np.tile(np.kron(np.eye(8, dtype=np.float32), np.ones((16, 16), np.float32)), (1, 4)))
    return w



_NAME_I = [0]


def nm(base):
    _NAME_I[0] += 1
    return f"{base}_{_NAME_I[0]}"


def cv(ap, width, idx):
    if isinstance(idx, int):
        return ap[:, idx * width:(idx + 1) * width]
    return ap.rearrange("p (i u) -> p i u", u=width)[:, idx, :]


class GSem:

    def __init__(self, h):
        self.h = h
        self.n = 0


class PSem:

    def __init__(self, h, base):
        self.h = h
        self.base = base

    def v(self, x):
        return self.base + x


class Pool:

    def __init__(self, nc, ctx, n=30):
        self.sems = [ctx.enter_context(nc.semaphore(f"pool{i}")) for i in range(n)]
        self.counts = [0] * n
        self.names = {}
        self.nc = nc
        self.ctx = ctx
        self.gsems = {}
        self.seedscr = nc.dram_tensor("seed_scr", [64, 16], mybir.dt.float32,
                                      kind="Internal").ap()
        self._seed_i = 0

    def gsem(self, name):
        if name not in self.gsems:
            self.gsems[name] = GSem(self.ctx.enter_context(nc_sem(self.nc, "g_" + name)))
        return self.gsems[name]

    def seed_dma(self, gp, gs, units=1):
        for _ in range(units):
            i = self._seed_i % 64
            self._seed_i += 1
            gp.dma_start(out=self.seedscr[i:i + 1, :], in_=self.seed_src).then_inc(gs.h, 16)
            gs.n += 16

    def clear_block(self):
        pass

    def take(self, names):
        self._cur = {nm: i for i, nm in enumerate(names)}
        return {nm: PSem(self.sems[i], self.counts[i]) for i, nm in enumerate(names)}

    def finish(self, totals):
        for nm, tot in totals.items():
            self.counts[self._cur[nm]] += tot


def nc_sem(nc, name):
    return nc.semaphore(name)


def emit_qkv(nc, pool, g, cfg):
    sm = pool.take(["ld", "gf", "rm", "hb", "tps", "hf", "mq", "qe", "mv", "ve", "gl", "ss"])
    wl = pool.gsem("wl")
    qst = pool.gsem("qst")
    vst = pool.gsem("vst")
    XS, WQ, WK, WV, GAM = cfg["XS"], cfg["WQ"], cfg["WK"], cfg["WV"], cfg["GAM"]
    QS, KS, VS = cfg["QS"], cfg["KS"], cfg["VS"]
    rms = cfg["rms"]
    tm_qk = cfg.get("tm_qk", False)
    qscale = cfg["qscale"]
    NTG = cfg.get("ntg", T // 512)
    wl0, qst0, vst0 = wl.n, qst.n, vst.n

    with (
        nc.sbuf_tensor(nm("wq_sb"), [128, 8 * 1024], BF16) as wq_sb,
        nc.sbuf_tensor(nm("wk_sb"), [128, 8 * 1024], BF16) as wk_sb,
        nc.sbuf_tensor(nm("wv_sb"), [128, 8 * 1024], BF16) as wv_sb,
        nc.sbuf_tensor(nm("gam"), [128, 8], F32) as gam,
        nc.sbuf_tensor(nm("xt"), [128, 4 * 1024], F32) as xt,
        nc.sbuf_tensor(nm("junk"), [128, 4 * 1024], BF16) as junk,
        nc.sbuf_tensor(nm("ssq"), [128, 4], F32) as ssq,
        nc.sbuf_tensor(nm("lnv"), [128, 4], F32) as lnv,
        nc.sbuf_tensor(nm("rstd"), [128, 4], F32) as rstd,
        nc.sbuf_tensor(nm("hbf"), [128, 4 * 1024], BF16) as hbf,
        nc.sbuf_tensor(nm("hfm"), [128, 8 * 512], BF16) as hfm,
        nc.sbuf_tensor(nm("qev"), [128, 2 * 512], BF16) as qev,
        nc.sbuf_tensor(nm("vev"), [128, 2 * 1024], BF16) as vev,
        nc.psum_tensor(nm("tp0"), [128, 1024], BF16) as tp0,
        nc.psum_tensor(nm("tp1"), [128, 1024], BF16) as tp1,
        nc.psum_tensor(nm("pqk0"), [128, 512], F32) as pqk0,
        nc.psum_tensor(nm("pqk1"), [128, 512], F32) as pqk1,
        nc.psum_tensor(nm("pv0"), [128, 1024], F32) as pv0,
        nc.psum_tensor(nm("pv1"), [128, 1024], F32) as pv1,
        nc.Block() as blk,
    ):
        tp = (tp0, tp1)
        pqk = (pqk0, pqk1)
        pv = (pv0, pv1)
        XS4 = XS.rearrange("(g q p) d -> g p q d", q=4, p=128)
        if not tm_qk:
            QS8 = QS.rearrange("f (g n) -> f g n", g=T // 512)
            KS8 = KS.rearrange("f (g n) -> f g n", g=T // 512)
        else:
            QS_tm4 = QS.rearrange("(g q p) (nd f) -> g q p nd f", q=4, p=128, nd=2)
            KS_tm4 = KS.rearrange("(g q p) (nd f) -> g q p nd f", q=4, p=128, nd=2)
        VS4 = VS.rearrange("(g q p) d -> g q p d", q=4, p=128)

        @blk.gpsimd
        def _(gp):
            for w_sb, W in ((wq_sb, WQ), (wk_sb, WK), (wv_sb, WV)):
                gp.dma_start(out=w_sb.rearrange("p (c f) -> p c f", c=8),
                             in_=W.rearrange("(c p) f -> p c f", p=128)).then_inc(wl.h, 16)
            wl.n += 48
            with gp.Fori(0, NTG) as tg:
                if not tm_qk:
                    for w_i, QKS in ((0, QS8), (1, KS8)):
                        for fc in range(8):
                            E = 16 * tg + w_i * 8 + fc
                            gp.wait_ge(sm["qe"].h, sm["qe"].v(E + 2))
                            gp.dma_start(out=QKS[fc * 128:(fc + 1) * 128, tg, :],
                                         in_=cv(qev, 512, fc % 2)).then_inc(qst.h, 16)
                else:
                    for w_i, QKT in ((0, QS_tm4), (1, KS_tm4)):
                        for tt in range(4):
                            for nd in range(2):
                                E = 16 * tg + w_i * 8 + tt * 2 + nd
                                gp.wait_ge(sm["qe"].h, sm["qe"].v(E + 2))
                                gp.dma_start(out=QKT[tg, tt, :, nd],
                                             in_=cv(qev, 512, E % 2)).then_inc(qst.h, 16)
                for tt in range(4):
                    Tv = 4 * tg + tt
                    gp.wait_ge(sm["ve"].h, sm["ve"].v(Tv + 2))
                    gp.dma_start(out=VS4[tg, tt, :, :],
                                 in_=cv(vev, 1024, tt % 2)).then_inc(vst.h, 16)
            qst.n += 16 * 16 * NTG
            vst.n += 16 * 4 * NTG
            gp.wait_ge(qst.h, qst.n)
            gp.wait_ge(vst.h, vst.n)

        @blk.sync
        def _(sy):
            sy.dma_start(out=gam[:, :], in_=GAM).then_inc(sm["gl"].h, 16)
            with sy.Fori(0, NTG) as tg:
                if rms:
                    sy.wait_ge(sm["rm"].h, sm["rm"].v(tg))
                sy.wait_ge(sm["hb"].h, sm["hb"].v(4 * tg))
                sy.dma_start(out=xt.rearrange("p (q d) -> p q d", q=4),
                             in_=XS4[tg, :, :, :]).then_inc(sm["ld"].h, 16)

        @blk.scalar
        def _(s):
            with s.Fori(0, NTG) as tg:
                if rms:
                    s.wait_ge(sm["ld"].h, sm["ld"].v(16 * (tg + 1)))
                    s.wait_ge(sm["ss"].h, sm["ss"].v(2 * tg))
                    for tt in range(4):
                        ins = s.activation(cv(junk, 1024, tt), cv(xt, 1024, tt), AF.Square,
                                           accum_out=ssq[:, tt:tt + 1])
                    ins.then_inc(sm["ss"].h, 1)
                    s.wait_ge(sm["ss"].h, sm["ss"].v(2 * tg + 1))
                    s.activation(lnv[:, :], ssq[:, :], AF.Ln, scale=1.0 / D,
                                 bias=g["epsr"]).then_inc(sm["ss"].h, 1)
                    s.wait_ge(sm["ss"].h, sm["ss"].v(2 * tg + 2))
                    s.activation(rstd[:, :], lnv[:, :], AF.Exp, scale=-0.5).then_inc(sm["rm"].h, 1)
                for w_i in range(2):
                    for fc in range(8):
                        E = 16 * tg + w_i * 8 + fc
                        s.wait_ge(sm["mq"].h, sm["mq"].v(E + 1))
                        s.wait_ge(qst.h, qst0 + 16 * E)
                        s.copy(cv(qev, 512, E % 2), pqk[E % 2][:, :]).then_inc(sm["qe"].h, 1)

        @blk.vector
        def _(v):
            v.wait_ge(wl.h, wl0 + 48)
            v.wait_ge(sm["gl"].h, sm["gl"].v(16))
            for w_sb, sc in ((wq_sb, qscale), (wk_sb, 1.0), (wv_sb, 1.0)):
                for c in range(8):
                    ins = v.tensor_scalar(cv(w_sb, 1024, c), cv(w_sb, 1024, c),
                                          gam[:, c:c + 1], float(sc), ALU.mult, ALU.mult)
            ins.then_inc(sm["gf"].h, 1)
            with v.Fori(0, NTG) as tg:
                if rms:
                    v.wait_ge(sm["rm"].h, sm["rm"].v(tg + 1))
                else:
                    v.wait_ge(sm["ld"].h, sm["ld"].v(16 * (tg + 1)))
                for tt in range(4):
                    if tt == 0:
                        v.wait_ge(sm["tps"].h, sm["tps"].v(4 * tg))
                    if rms:
                        v.tensor_scalar(cv(hbf, 1024, tt), cv(xt, 1024, tt),
                                        rstd[:, tt:tt + 1], None, ALU.mult).then_inc(sm["hb"].h, 1)
                    else:
                        v.tensor_copy(cv(hbf, 1024, tt), cv(xt, 1024, tt)).then_inc(sm["hb"].h, 1)
                for tt in range(4):
                    G = 4 * tg + tt
                    if tt == 0:
                        v.wait_ge(sm["mq"].h, sm["mq"].v(16 * tg))
                        v.wait_ge(sm["mv"].h, sm["mv"].v(8 * tg))
                    v.wait_ge(sm["tps"].h, sm["tps"].v(G + 1))
                    v.tensor_copy(
                        hfm.rearrange("p (c f) -> p c f", c=8)[:, :, tt * 128:(tt + 1) * 128],
                        tp[tt % 2].rearrange("p (c f) -> p c f", c=8)).then_inc(sm["hf"].h, 1)
                for tt in range(4):
                    Tv = 4 * tg + tt
                    v.wait_ge(sm["mv"].h, sm["mv"].v(2 * Tv + 2))
                    v.wait_ge(vst.h, vst0 + 16 * Tv)
                    v.tensor_copy(cv(vev, 1024, tt % 2), pv[tt % 2][:, :]).then_inc(sm["ve"].h, 1)

        @blk.tensor
        def _(t):
            t.sem_inc(sm["qe"].h, 1)
            t.sem_inc(sm["ve"].h, 1)
            t.sem_inc(sm["hf"].h, 1)
            t.wait_ge(sm["gf"].h, sm["gf"].v(1))
            with t.Fori(0, NTG) as tg:
                for tt in range(4):
                    G = 4 * tg + tt
                    t.wait_ge(sm["hb"].h, sm["hb"].v(4 * tg + tt + 1))
                    t.wait_ge(sm["hf"].h, sm["hf"].v(G))
                    for c in range(8):
                        ins = t.transpose(cv(tp[tt % 2], 128, c),
                                          hbf[:, tt * 1024 + c * 128:tt * 1024 + (c + 1) * 128],
                                          g["idn"][:, :])
                    ins.then_inc(sm["tps"].h, 1)
                t.wait_ge(sm["hf"].h, sm["hf"].v(4 * (tg + 1) + 1))
                if not tm_qk:
                    for w_i, w_sb in ((0, wq_sb), (1, wk_sb)):
                        for fc in range(8):
                            E = 16 * tg + w_i * 8 + fc
                            t.wait_ge(sm["qe"].h, sm["qe"].v(E))
                            for kc in range(8):
                                ins = t.matmul(pqk[fc % 2][:, :],
                                               w_sb[:, kc * 1024 + fc * 128:kc * 1024 + (fc + 1) * 128],
                                               cv(hfm, 512, kc),
                                               start=(kc == 0), stop=(kc == 7))
                            ins.then_inc(sm["mq"].h, 1)
                else:
                    for w_i, w_sb in ((0, wq_sb), (1, wk_sb)):
                        for tt in range(4):
                            for nd in range(2):
                                E = 16 * tg + w_i * 8 + tt * 2 + nd
                                t.wait_ge(sm["qe"].h, sm["qe"].v(E))
                                for kc in range(8):
                                    ins = t.matmul(
                                        pqk[E % 2][:, :],
                                        hfm[:, kc * 512 + tt * 128:kc * 512 + (tt + 1) * 128],
                                        w_sb[:, kc * 1024 + nd * 512:kc * 1024 + (nd + 1) * 512],
                                        start=(kc == 0), stop=(kc == 7))
                                ins.then_inc(sm["mq"].h, 1)
                for tt in range(4):
                    t.wait_ge(sm["ve"].h, sm["ve"].v(4 * tg + tt))
                    for nd in range(2):
                        for kc in range(8):
                            ins = t.matmul(pv[tt % 2][:, nd * 512:(nd + 1) * 512],
                                           hfm[:, kc * 512 + tt * 128:kc * 512 + (tt + 1) * 128],
                                           wv_sb[:, kc * 1024 + nd * 512:kc * 1024 + (nd + 1) * 512],
                                           start=(kc == 0), stop=(kc == 7))
                        ins.then_inc(sm["mv"].h, 1)
    pool.finish({"ss": (2 * NTG) if rms else 0, "gl": 16, "ld": 16 * NTG, "gf": 1, "rm": NTG if rms else 0,
                 "hb": 4 * NTG, "tps": 4 * NTG, "hf": 4 * NTG + 1,
                 "mq": 16 * NTG, "qe": 16 * NTG + 1, "mv": 8 * NTG,
                 "ve": 4 * NTG + 1})


def emit_attn(nc, pool, g, cfg):
    sm = pool.take(["qld", "m1", "r1", "e1", "sln", "vaug", "ttr", "trd",
                    "augst", "augld", "m2", "e2", "mpv", "ve2", "msd",
                    "radd", "mpj", "xld", "xst", "ssq2", "srs", "vvar",
                    "lnv", "vs2", "vs3", "xrl"])
    wl = pool.gsem("wl")
    QS, KS, VS, XRES = cfg["QS"], cfg["KS"], cfg["VS"], cfg["XRES"]
    WO, BO, XOUT = cfg["WO"], cfg["BO"], cfg["XOUT"]
    AUGS = nc.dram_tensor(cfg["name"] + "_augs", [4, 16, 128], BF16, kind="Internal").ap()
    wl0 = wl.n
    NHP = 8

    es = ExitStack()
    wo_sb = es.enter_context(nc.sbuf_tensor(nm("a_wo_sb"), [128, 8 * 1024], BF16))
    bo_sb = es.enter_context(nc.sbuf_tensor(nm("a_bo_sb"), [1, 1024], F32))
    qstA = es.enter_context(nc.sbuf_tensor(nm("a_qstA"), [128, 2048], BF16))
    qstB = es.enter_context(nc.sbuf_tensor(nm("a_qstB"), [128, 2048], BF16))
    kstA = es.enter_context(nc.sbuf_tensor(nm("a_kstA"), [128, 2048], BF16))
    kstB = es.enter_context(nc.sbuf_tensor(nm("a_kstB"), [128, 2048], BF16))
    vstA = es.enter_context(nc.sbuf_tensor(nm("a_vstA"), [128, 16 * 64], BF16))
    vstB = es.enter_context(nc.sbuf_tensor(nm("a_vstB"), [128, 16 * 64], BF16))
    ET = es.enter_context(nc.sbuf_tensor(nm("a_ET"), [128, 16 * 512], BF16))
    junkA = es.enter_context(nc.sbuf_tensor(nm("a_junkA"), [128, 2 * 2048], BF16))
    nmall = es.enter_context(nc.sbuf_tensor(nm("a_nmall"), [128, 32], F32))
    ssall = es.enter_context(nc.sbuf_tensor(nm("a_ssall"), [128, 32], F32))
    lnl = es.enter_context(nc.sbuf_tensor(nm("a_lnl"), [128, 32], F32))
    negb = es.enter_context(nc.sbuf_tensor(nm("a_negb"), [128, 32], F32))
    lo32 = es.enter_context(nc.sbuf_tensor(nm("a_lo32"), [128, 32], F32))
    hilo = es.enter_context(nc.sbuf_tensor(nm("a_hilo"), [128, 64], BF16))
    tr64 = es.enter_context(nc.sbuf_tensor(nm("a_tr64"), [64, 128], BF16))
    attnO = es.enter_context(nc.sbuf_tensor(nm("a_attnO"), [128, 8 * 2048], BF16))
    xrt = es.enter_context(nc.sbuf_tensor(nm("a_xrt"), [128, 2 * 1024], F32))
    rt = es.enter_context(nc.sbuf_tensor(nm("a_rt"), [128, 2 * 1024], F32))
    ob = es.enter_context(nc.sbuf_tensor(nm("a_ob"), [128, 2 * 1024], F32))
    junkP = es.enter_context(nc.sbuf_tensor(nm("a_junkP"), [128, 2 * 1024], BF16))
    pstat = es.enter_context(nc.sbuf_tensor(nm("a_pstat"), [128, 16], F32))
    with (
        es,
        nc.psum_tensor(nm("S2a"), [128, 2048], F32) as S2a,
        nc.psum_tensor(nm("S2b"), [128, 2048], F32) as S2b,
        nc.Block() as blk,
    ):
        S2 = (S2a, S2b)
        qst = (qstA, qstB)
        kst = (kstA, kstB)
        vst = (vstA, vstB)
        ps_s = (S2a[:, 0:512], S2a[:, 512:1024], S2b[:, 0:512], S2b[:, 512:1024])
        pvp = S2a[:, 1024:1536]
        ptr = S2a[:, 1536:2048].bitcast(BF16)
        po = (S2b[:, 0:1024], S2b[:, 1024:2048])
        QSh = QS.rearrange("(h r) t -> h r t", r=64)
        KSh = KS.rearrange("(h r) t -> h r t", r=64)
        AUGSf = AUGS.rearrange("x a b -> x (a b)")

        @blk.gpsimd
        def _(gp):
            gp.dma_start(out=wo_sb.rearrange("p (c f) -> p c f", c=8),
                         in_=WO.rearrange("(c p) f -> p c f", p=128)).then_inc(wl.h, 16)
            for s in range(SPC):
                with gp.Fori(0, NHP) as hp:
                    i = 8 * s + hp
                    gp.wait_ge(sm["trd"].h, sm["trd"].v(i + 1))
                    for j in range(4):
                        gp.dma_start(out=AUGS[j], in_=tr64[j * 16:(j + 1) * 16, :]).then_inc(sm["augst"].h, 16)
                    gp.wait_ge(sm["augst"].h, sm["augst"].v(64 * (i + 1)))
                    gp.dma_start(out=qst[0][64:66, :], in_=AUGSf[0:2, :]).then_inc(sm["augld"].h, 16)
                    gp.dma_start(out=qst[1][64:66, :], in_=AUGSf[2:4, :]).then_inc(sm["augld"].h, 16)
            gp.wait_ge(wl.h, wl0 + 16)
        wl.n += 16

        @blk.sync
        def _(sy):
            sy.dma_start(out=bo_sb[:, :], in_=BO).then_inc(sm["xld"].h, 16)
            for s in range(SPC):
                with sy.Fori(0, NHP) as hp:
                    i = 8 * s + hp
                    sy.wait_ge(sm["m2"].h, sm["m2"].v(128 * i))
                    for X in range(2):
                        h = 2 * hp + X
                        sy.dma_start(out=qst[X][0:64, :],
                                     in_=QSh[h, :, s * 2048:(s + 1) * 2048]).then_inc(sm["qld"].h, 16)
                        sy.dma_start(out=kst[X][0:64, :],
                                     in_=KSh[h, :, s * 2048:(s + 1) * 2048]).then_inc(sm["qld"].h, 16)
                    sy.wait_ge(sm["mpv"].h, sm["mpv"].v(8 * i))
                    for X in range(2):
                        h = 2 * hp + X
                        sy.dma_start(
                            out=vst[X].rearrange("p (c e) -> p c e", c=16),
                            in_=VS.rearrange("(s c p) (h e) -> s h p c e", s=SPC, p=128, h=16)[s, h],
                        ).then_inc(sm["qld"].h, 16)

                for tt in range(16):
                    PT = 16 * s + tt
                    if PT >= 2:
                        sy.wait_ge(sm["radd"].h, sm["radd"].v(PT - 1))
                    sy.dma_start(out=cv(xrt, 1024, tt % 2),
                                 in_=XRES.rearrange("(t p) d -> t p d", p=128)[PT]).then_inc(sm["xrl"].h, 16)
                    sy.wait_ge(sm["lnv"].h, sm["lnv"].v(PT + 1))
                    sy.dma_start(out=XOUT.rearrange("(t p) d -> t p d", p=128)[PT],
                                 in_=cv(ob, 1024, tt % 2)).then_inc(sm["xst"].h, 16)
            sy.wait_ge(sm["xst"].h, sm["xst"].v(16 * 32))

        @blk.scalar
        def _(s_):
            for s in range(SPC):
                with s_.Fori(0, NHP) as hp:
                    i = 8 * s + hp
                    for u in range(32):
                        s_.wait_ge(sm["r1"].h, sm["r1"].v(32 * i + u + 1))
                        s_.wait_ge(sm["e1"].h, sm["e1"].v(32 * i + max(u - 1, 0)))
                        s_.activation(cv(junkA, 2048, u % 2), S2[u % 2][:, :], AF.Exp,
                                      bias=nmall[:, u:u + 1],
                                      accum_out=ssall[:, u:u + 1]).then_inc(sm["e1"].h, 1)
                    s_.wait_ge(sm["e1"].h, sm["e1"].v(32 * (i + 1)))
                    s_.activation(lnl[:, :], ssall[:, :], AF.Ln).then_inc(sm["sln"].h, 1)
                    for qg in range(4):
                        for X in range(2):
                            un = 2 * qg + X
                            s_.wait_ge(sm["mpv"].h, sm["mpv"].v(8 * i + un))
                            for kt in range(16):
                                m2j = 128 * i + un * 16 + kt
                                s_.wait_ge(sm["m2"].h, sm["m2"].v(m2j + 1))
                                s_.activation(cv(ET, 512, kt), ps_s[kt % 4],
                                              AF.Exp).then_inc(sm["e2"].h, 1)
                for tt in range(16):
                    PT = 16 * s + tt
                    p2 = tt % 2
                    s_.wait_ge(sm["radd"].h, sm["radd"].v(PT + 1))
                    if PT >= 2:
                        s_.wait_ge(sm["ssq2"].h, sm["ssq2"].v(PT - 1))
                        s_.wait_ge(sm["srs"].h, sm["srs"].v(2 * PT - 2))
                        s_.wait_ge(sm["vvar"].h, sm["vvar"].v(PT - 1))
                    s_.activation(cv(junkP, 1024, p2), cv(rt, 1024, p2), AF.Square,
                                  accum_out=pstat[:, 12 + p2:13 + p2]).then_inc(sm["ssq2"].h, 1)
                    s_.wait_ge(sm["vvar"].h, sm["vvar"].v(PT + 1))
                    s_.activation(pstat[:, 8 + p2:9 + p2],
                                  pstat[:, 6 + p2:7 + p2], AF.Ln,
                                  bias=g["epsl"]).then_inc(sm["srs"].h, 1)
                    s_.wait_ge(sm["srs"].h, sm["srs"].v(2 * PT + 1))
                    s_.activation(pstat[:, 10 + p2:11 + p2],
                                  pstat[:, 8 + p2:9 + p2], AF.Exp,
                                  scale=-0.5).then_inc(sm["srs"].h, 1)

        @blk.vector
        def _(v):
            v.memset(kstA[64:66, :], 1.0)
            v.memset(kstB[64:66, :], 1.0).then_inc(sm["msd"].h, 1)
            for s in range(SPC):
                v.wait_ge(sm["mpj"].h, sm["mpj"].v(32 * s))
                with v.Fori(0, NHP) as hp:
                    i = 8 * s + hp
                    v.wait_ge(sm["e1"].h, sm["e1"].v(32 * i))
                    for u in range(32):
                        v.wait_ge(sm["m1"].h, sm["m1"].v(32 * i + u + 1))
                        v.reduce_max(nmall[:, u:u + 1], S2[u % 2][:, :], AX.X,
                                     negate=True).then_inc(sm["r1"].h, 1)
                    v.wait_ge(sm["sln"].h, sm["sln"].v(i + 1))
                    v.wait_ge(sm["ttr"].h, sm["ttr"].v(i))
                    hiv = hilo.rearrange("p (y z c) -> p y z c", y=2, z=2)[:, :, 0, :]
                    lov = hilo.rearrange("p (y z c) -> p y z c", y=2, z=2)[:, :, 1, :]
                    v.tensor_sub(negb[:, :], nmall[:, :], lnl[:, :]).then_inc(sm["vs2"].h, 1)
                    v.wait_ge(sm["vs2"].h, sm["vs2"].v(3 * i + 1))
                    v.tensor_copy(hiv, negb.rearrange("p (y c) -> p y c", y=2)).then_inc(sm["vs2"].h, 1)
                    v.wait_ge(sm["vs2"].h, sm["vs2"].v(3 * i + 2))
                    v.tensor_sub(lo32.rearrange("p (y c) -> p y c", y=2),
                                 negb.rearrange("p (y c) -> p y c", y=2),
                                 hiv).then_inc(sm["vs2"].h, 1)
                    v.wait_ge(sm["vs2"].h, sm["vs2"].v(3 * i + 3))
                    v.tensor_copy(lov, lo32.rearrange("p (y c) -> p y c", y=2)).then_inc(sm["vaug"].h, 1)
                    v.wait_ge(sm["ttr"].h, sm["ttr"].v(i + 1))
                    v.wait_ge(sm["augst"].h, sm["augst"].v(64 * i))
                    v.tensor_copy(tr64[:, :], ptr[0:64, 0:128]).then_inc(sm["trd"].h, 1)
                    for qg in range(4):
                        v.wait_ge(sm["mpv"].h, sm["mpv"].v(8 * i + 2 * qg + 2))
                        v.tensor_copy(
                            attnO.rearrange("p (c n) -> p c n", n=512)[:, 4 * hp + qg, :],
                            pvp[:, :]).then_inc(sm["ve2"].h, 1)
                for tt in range(16):
                    PT = 16 * s + tt
                    p2 = tt % 2
                    v.wait_ge(sm["mpj"].h, sm["mpj"].v(2 * PT + 2))
                    v.wait_ge(sm["xrl"].h, sm["xrl"].v(16 * (PT + 1)))
                    if PT >= 2:
                        v.wait_ge(sm["xst"].h, sm["xst"].v(16 * (PT - 1)))
                    v.tensor_add(cv(rt, 1024, p2), po[p2][:, :],
                                 cv(xrt, 1024, p2)).then_inc(sm["radd"].h, 1)
                    v.wait_ge(sm["radd"].h, sm["radd"].v(PT + 1))
                    v.reduce_sum(pstat[:, p2:p2 + 1], cv(rt, 1024, p2), AX.X).then_inc(sm["vs3"].h, 1)
                    v.wait_ge(sm["vs3"].h, sm["vs3"].v(3 * PT + 1))
                    v.tensor_scalar(pstat[:, 2 + p2:3 + p2], pstat[:, p2:p2 + 1],
                                    -1.0 / D, None, ALU.mult).then_inc(sm["vs3"].h, 1)
                    v.wait_ge(sm["vs3"].h, sm["vs3"].v(3 * PT + 2))
                    v.tensor_mul(pstat[:, 4 + p2:5 + p2], pstat[:, 2 + p2:3 + p2],
                                 pstat[:, 2 + p2:3 + p2]).then_inc(sm["vs3"].h, 1)
                    v.wait_ge(sm["ssq2"].h, sm["ssq2"].v(PT + 1))
                    v.wait_ge(sm["vs3"].h, sm["vs3"].v(3 * PT + 3))
                    v.tensor_scalar(pstat[:, 6 + p2:7 + p2], pstat[:, 12 + p2:13 + p2],
                                    1.0 / D, pstat[:, 4 + p2:5 + p2],
                                    ALU.mult, ALU.subtract).then_inc(sm["vvar"].h, 1)
                    v.wait_ge(sm["srs"].h, sm["srs"].v(2 * PT + 2))
                    v.tensor_scalar(cv(ob, 1024, p2), cv(rt, 1024, p2),
                                    pstat[:, 2 + p2:3 + p2], pstat[:, 10 + p2:11 + p2],
                                    ALU.add, ALU.mult).then_inc(sm["lnv"].h, 1)

        @blk.tensor
        def _(t):
            t.wait_ge(sm["msd"].h, sm["msd"].v(1))
            t.wait_ge(wl.h, wl0 + 16)
            t.wait_ge(sm["xld"].h, sm["xld"].v(16))
            for s in range(SPC):
                t.wait_ge(sm["radd"].h, sm["radd"].v(16 * s))
                with t.Fori(0, NHP) as hp:
                    i = 8 * s + hp
                    t.wait_ge(sm["qld"].h, sm["qld"].v(96 * (i + 1)))
                    t.wait_ge(sm["trd"].h, sm["trd"].v(i))
                    for u in range(32):
                        X, qt = u // 16, u % 16
                        t.wait_ge(sm["e1"].h, sm["e1"].v(32 * i + max(u - 1, 0)))
                        for kc in range(4):
                            ins = t.matmul(S2[u % 2][:, kc * 512:(kc + 1) * 512],
                                           qst[X][0:64, qt * 128:(qt + 1) * 128],
                                           kst[X][0:64, kc * 512:(kc + 1) * 512],
                                           start=True, stop=True)
                        ins.then_inc(sm["m1"].h, 1)
                    t.wait_ge(sm["vaug"].h, sm["vaug"].v(i + 1))
                    t.transpose(ptr[0:64, 0:128], hilo[:, :], g["idn"][:, :]).then_inc(sm["ttr"].h, 1)
                    t.wait_ge(sm["augld"].h, sm["augld"].v(32 * (i + 1)))
                    for qg in range(4):
                        for X in range(2):
                            un = 2 * qg + X
                            for kt in range(16):
                                m2j = 128 * i + un * 16 + kt
                                t.wait_ge(sm["e2"].h, sm["e2"].v(128 * i + un * 16 + max(kt - 3, 0)))
                                t.matmul(ps_s[kt % 4],
                                         kst[X][0:66, kt * 128:(kt + 1) * 128],
                                         qst[X][0:66, qg * 512:(qg + 1) * 512],
                                         start=True, stop=True).then_inc(sm["m2"].h, 1)
                            t.wait_ge(sm["e2"].h, sm["e2"].v(128 * i + un * 16 + 16))
                            if X == 0:
                                t.wait_ge(sm["ve2"].h, sm["ve2"].v(4 * i + qg))
                            for kt in range(16):
                                ins = t.matmul(pvp[X * 64:X * 64 + 64, :],
                                               vst[X][:, kt * 64:(kt + 1) * 64],
                                               cv(ET, 512, kt),
                                               start=(kt == 0), stop=(kt == 15),
                                               tile_position=(0, 64 * X))
                            ins.then_inc(sm["mpv"].h, 1)
                t.wait_ge(sm["ve2"].h, sm["ve2"].v(32 * (s + 1)))
                t.wait_ge(sm["e1"].h, sm["e1"].v(256 * (s + 1)))
                for tt in range(16):
                    PT = 16 * s + tt
                    if PT >= 2:
                        t.wait_ge(sm["radd"].h, sm["radd"].v(PT - 1))
                    for nd in range(2):
                        for kc in range(8):
                            t.matmul(po[tt % 2][:, nd * 512:(nd + 1) * 512],
                                     attnO[:, kc * 2048 + tt * 128:kc * 2048 + (tt + 1) * 128],
                                     wo_sb[:, kc * 1024 + nd * 512:kc * 1024 + (nd + 1) * 512],
                                     start=(kc == 0), stop=False)
                        t.matmul(po[tt % 2][:, nd * 512:(nd + 1) * 512],
                                 g["ones"][0:1, 0:128], bo_sb[0:1, nd * 512:(nd + 1) * 512],
                                 start=False, stop=True).then_inc(sm["mpj"].h, 1)
    pool.finish({"qld": 96 * 16, "m1": 32 * 16, "r1": 32 * 16, "e1": 32 * 16,
                 "sln": 16, "vaug": 16, "ttr": 16, "trd": 16,
                 "augst": 64 * 16, "augld": 32 * 16, "m2": 128 * 16,
                 "e2": 128 * 16, "mpv": 8 * 16, "ve2": 4 * 16, "msd": 1,
                 "radd": 32, "mpj": 64, "xld": 16, "xst": 16 * 32,
                 "ssq2": 32, "srs": 64, "vvar": 32, "lnv": 32,
                 "vs2": 3 * 16, "vs3": 3 * 32, "xrl": 16 * 32})


def emit_ffn(nc, pool, g, cfg):
    L = cfg["layer"]
    H = cfg["H"]
    shared = cfg["shared"]
    HB = H // 128
    TGW = 512 if shared else 256
    NTGC = T // TGW
    TPC = TGW // 128
    sm = pool.take(["ldx", "cbd", "tpd", "xfd", "mr", "rrm", "sre", "wvd",
                    "vsA", "mh", "gh", "my", "ye", "accv", "dr", "dssq",
                    "dsrs", "dvvar", "dlnv", "ds3", "zf"])
    wle = pool.gsem("wl")
    xfst = pool.gsem("xfst")
    accst = pool.gsem("accst")
    xfld_g = pool.gsem("xfld")
    XIN, XOUT = cfg["XIN"], cfg["XOUT"]
    RW, RB = cfg["RW"], cfg.get("RB")
    XF = nc.dram_tensor(cfg["name"] + "_xf", [D, T], BF16, kind="Internal").ap()
    ACC = nc.dram_tensor(cfg["name"] + "_acc", [T, D], F32, kind="Internal").ap()
    XIN_T = XIN.rearrange("(t p) d -> t p d", p=128)
    XOUT_T = XOUT.rearrange("(t p) d -> t p d", p=128)
    ACC_T = ACC.rearrange("(t p) d -> t p d", p=128)
    XFr = XF.rearrange("(c p) t -> p c t", p=128)

    _scA = nc.enter_named_scope(cfg["name"] + "_A", False)[0]
    esP = ExitStack()
    wsb = esP.enter_context(nc.sbuf_tensor(nm("f_wsb"), [128, 32 * 8], F32))
    esA = ExitStack()
    xta = esA.enter_context(nc.sbuf_tensor(nm("f_xta"), [128, 2 * 1024], F32))
    hbf2 = esA.enter_context(nc.sbuf_tensor(nm("f_hbf"), [128, 2 * 1024], BF16))
    xfb = esA.enter_context(nc.sbuf_tensor(nm("f_xfb"), [128, 2 * 1024], BF16))
    rw_sb = esA.enter_context(nc.sbuf_tensor(nm("f_rw"), [128, 8 * 8], BF16))
    rb_sb = esA.enter_context(nc.sbuf_tensor(nm("f_rb"), [1, 8], F32))
    prexp = esA.enter_context(nc.sbuf_tensor(nm("f_prexp"), [128, 2 * 8], F32))
    prb = esA.enter_context(nc.sbuf_tensor(nm("f_prb"), [128, 2 * 8], F32))
    msk = esA.enter_context(nc.sbuf_tensor(nm("f_msk"), [128, 2 * 8], F32))
    rstat = esA.enter_context(nc.sbuf_tensor(nm("f_rstat"), [128, 16], F32))
    wl0 = wle.n
    xfst0 = xfst.n
    with (
        esA,
        nc.psum_tensor(nm("f_tp0"), [128, 1024], BF16) as tpf0,
        nc.psum_tensor(nm("f_tp1"), [128, 1024], BF16) as tpf1,
        nc.psum_tensor(nm("f_pr0"), [128, 8], F32) as pr0,
        nc.psum_tensor(nm("f_pr1"), [128, 8], F32) as pr1,
        nc.Block() as blk,
    ):
        tpf = (tpf0, tpf1)
        pr = (pr0, pr1)

        @blk.gpsimd
        def _(gp):
            gp.dma_start(out=rw_sb.rearrange("p (c e) -> p c e", c=8),
                         in_=RW.rearrange("(c p) e -> p c e", p=128)).then_inc(wle.h, 16)
            gp.wait_ge(wle.h, wl0 + 16)
        wle.n += 16

        @blk.sync
        def _(sy):
            if RB is not None:
                sy.dma_start(out=rb_sb[:, :], in_=RB).then_inc(sm["ldx"].h, 16)
            for tt in range(32):
                p2 = tt % 2
                if tt >= 2:
                    sy.wait_ge(sm["cbd"].h, sm["cbd"].v(tt - 1))
                sy.dma_start(out=cv(xta, 1024, p2), in_=XIN_T[tt]).then_inc(sm["ldx"].h, 16)
                sy.wait_ge(sm["xfd"].h, sm["xfd"].v(tt + 1))
                sy.dma_start(out=XFr[:, :, tt * 128:(tt + 1) * 128],
                             in_=xfb.rearrange("p (q c f) -> p q c f", q=2, c=8)[:, p2],
                             ).then_inc(xfst.h, 16)
            sy.wait_ge(xfst.h, xfst0 + 16 * 32)
        xfst.n += 16 * 32

        @blk.vector
        def _(v):
            rbase = 16 if RB is not None else 0
            for tt in range(32):
                p2 = tt % 2
                v.wait_ge(sm["ldx"].h, sm["ldx"].v(rbase + 16 * (tt + 1)))
                if tt >= 2:
                    v.wait_ge(sm["tpd"].h, sm["tpd"].v(tt - 1))
                v.tensor_copy(cv(hbf2, 1024, p2), cv(xta, 1024, p2)).then_inc(sm["cbd"].h, 1)
                v.wait_ge(sm["tpd"].h, sm["tpd"].v(tt + 1))
                if tt >= 2:
                    v.wait_ge(xfst.h, xfst0 + 16 * (tt - 1))
                v.tensor_copy(cv(xfb, 1024, p2), tpf[p2][:, :]).then_inc(sm["xfd"].h, 1)
                v.wait_ge(sm["mr"].h, sm["mr"].v(tt + 1))
                v.reduce_max(rstat[:, p2:p2 + 1], pr[p2][:, :], AX.X,
                             negate=True).then_inc(sm["rrm"].h, 1)
                v.wait_ge(sm["sre"].h, sm["sre"].v(tt + 1))
                v.reciprocal(rstat[:, 4 + p2:5 + p2], rstat[:, 2 + p2:3 + p2]).then_inc(sm["vsA"].h, 1)
                v.wait_ge(sm["vsA"].h, sm["vsA"].v(6 * tt + 1))
                v.tensor_scalar(cv(prb, 8, p2), cv(prexp, 8, p2),
                                rstat[:, 4 + p2:5 + p2], None, ALU.mult).then_inc(sm["vsA"].h, 1)
                v.wait_ge(sm["vsA"].h, sm["vsA"].v(6 * tt + 2))
                v.reduce_max(rstat[:, 6 + p2:7 + p2], cv(prb, 8, p2), AX.X).then_inc(sm["vsA"].h, 1)
                v.wait_ge(sm["vsA"].h, sm["vsA"].v(6 * tt + 3))
                v.tensor_scalar(cv(msk, 8, p2), cv(prb, 8, p2),
                                rstat[:, 6 + p2:7 + p2], None, ALU.is_ge).then_inc(sm["vsA"].h, 1)
                v.wait_ge(sm["vsA"].h, sm["vsA"].v(6 * tt + 4))
                v.scalar_tensor_tensor(cv(msk, 8, p2), cv(msk, 8, p2), -2.0,
                                       cv(prb, 8, p2), ALU.mult, ALU.add).then_inc(sm["vsA"].h, 1)
                v.wait_ge(sm["vsA"].h, sm["vsA"].v(6 * tt + 5))
                v.reduce_max(rstat[:, 8 + p2:9 + p2], cv(msk, 8, p2), AX.X).then_inc(sm["vsA"].h, 1)
                v.wait_ge(sm["vsA"].h, sm["vsA"].v(6 * tt + 6))
                v.tensor_scalar(cv(msk, 8, p2), cv(prb, 8, p2),
                                rstat[:, 8 + p2:9 + p2], None, ALU.is_ge).then_inc(sm["wvd"].h, 1)
                v.wait_ge(sm["wvd"].h, sm["wvd"].v(2 * tt + 1))
                v.tensor_mul(cv(wsb, 8, tt), cv(prb, 8, p2),
                             cv(msk, 8, p2)).then_inc(sm["wvd"].h, 1)

        @blk.scalar
        def _(s_):
            for tt in range(32):
                p2 = tt % 2
                s_.wait_ge(sm["rrm"].h, sm["rrm"].v(tt + 1))
                if tt >= 2:
                    s_.wait_ge(sm["vsA"].h, sm["vsA"].v(6 * (tt - 1)))
                s_.activation(cv(prexp, 8, p2), pr[p2][:, :], AF.Exp,
                              bias=rstat[:, p2:p2 + 1],
                              accum_out=rstat[:, 2 + p2:3 + p2]).then_inc(sm["sre"].h, 1)

        @blk.tensor
        def _(t):
            t.wait_ge(wle.h, wl0 + 16)
            if RB is not None:
                t.wait_ge(sm["ldx"].h, sm["ldx"].v(16))
            for tt in range(32):
                p2 = tt % 2
                t.wait_ge(sm["cbd"].h, sm["cbd"].v(tt + 1))
                if tt >= 2:
                    t.wait_ge(sm["xfd"].h, sm["xfd"].v(tt - 1))
                for c in range(8):
                    ins = t.transpose(cv(tpf[p2], 128, c),
                                      hbf2[:, p2 * 1024 + c * 128:p2 * 1024 + (c + 1) * 128],
                                      g["idn"][:, :])
                ins.then_inc(sm["tpd"].h, 1)
                t.wait_ge(sm["xfd"].h, sm["xfd"].v(tt + 1))
                if tt >= 2:
                    t.wait_ge(sm["sre"].h, sm["sre"].v(tt - 1))
                for c in range(8):
                    ins = t.matmul(pr[p2][:, :], xfb[:, p2 * 1024 + c * 128:p2 * 1024 + (c + 1) * 128],
                                   rw_sb[:, c * 8:(c + 1) * 8],
                                   start=(c == 0), stop=(RB is None and c == 7))
                if RB is not None:
                    ins = t.matmul(pr[p2][:, :], g["ones"][0:1, 0:128], rb_sb[0:1, :],
                                   start=False, stop=True)
                ins.then_inc(sm["mr"].h, 1)
    nc.leave_named_scope(cfg["name"] + "_A", _scA, False)
    totA = {"ldx": (16 if RB is not None else 0) + 16 * 32, "cbd": 32,
            "tpd": 32, "xfd": 32, "mr": 32, "rrm": 32, "sre": 32,
            "wvd": 2 * 32, "vsA": 6 * 32}

    if shared:
        esB = ExitStack()
        sw1_sb = esB.enter_context(nc.sbuf_tensor(nm("f_sw1"), [128, 8 * 2048], BF16))
        sw2_sb = esB.enter_context(nc.sbuf_tensor(nm("f_sw2"), [128, 16 * 1024], BF16))
        sb1_sb = esB.enter_context(nc.sbuf_tensor(nm("f_sb1"), [128, 16], F32))
        sb2_sb = esB.enter_context(nc.sbuf_tensor(nm("f_sb2"), [1, 1024], F32))
        h1b = esB.enter_context(nc.sbuf_tensor(nm("f_h1b"), [128, 2 * 16 * 512], BF16))
        xfs = esB.enter_context(nc.sbuf_tensor(nm("f_xfs"), [128, 2 * 8 * 512], BF16))
        yev = esB.enter_context(nc.sbuf_tensor(nm("f_yev"), [128, 2 * 1024], F32))
        wl1 = wle.n
        xfld0 = xfld_g.n
        accst0 = accst.n
        with (
            esB,
            nc.psum_tensor(nm("f_ph0"), [128, 512], F32) as ph0,
            nc.psum_tensor(nm("f_ph1"), [128, 512], F32) as ph1,
            nc.psum_tensor(nm("f_py0"), [128, 1024], F32) as py0,
            nc.psum_tensor(nm("f_py1"), [128, 1024], F32) as py1,
            nc.Block() as blk,
        ):
            ph = (ph0, ph1)
            py = (py0, py1)

            @blk.gpsimd
            def _(gp):
                gp.dma_start(out=sw1_sb.rearrange("p (c f) -> p c f", c=8),
                             in_=cfg["SW1"].rearrange("(c p) f -> p c f", p=128)).then_inc(wle.h, 16)
                gp.dma_start(out=sw2_sb.rearrange("p (c f) -> p c f", c=16),
                             in_=cfg["SW2"].rearrange("(c p) f -> p c f", p=128)).then_inc(wle.h, 16)
                gp.wait_ge(wle.h, wl1 + 32)
            wle.n += 32

            @blk.sync
            def _(sy):
                sy.dma_start(out=sb1_sb[:, :], in_=cfg["SB1"]).then_inc(sm["ldx"].h, 16)
                sy.dma_start(out=sb2_sb[:, :], in_=cfg["SB2"]).then_inc(sm["ldx"].h, 16)
                with sy.Fori(0, 8) as tg:
                    sy.wait_ge(sm["mh"].h, sm["mh"].v(16 * tg - 16))
                    sy.dma_start(out=xfs.rearrange("p (q c f) -> p q c f", q=2, c=8)[:, tg % 2],
                                 in_=XFr.rearrange("p c (gg f) -> p gg c f", gg=8)[:, tg],
                                 ).then_inc(xfld_g.h, 16)
                    for tt in range(4):
                        j = 4 * tg + tt
                        sy.wait_ge(sm["ye"].h, sm["ye"].v(j + 1))
                        sy.dma_start(out=ACC_T[j], in_=cv(yev, 1024, tt % 2)).then_inc(accst.h, 16)
            xfld_g.n += 16 * 8
            accst.n += 16 * 32

            @blk.scalar
            def _(s_):
                with s_.Fori(0, 8) as tg:
                    for hc in range(16):
                        u = 16 * tg + hc
                        s_.wait_ge(sm["mh"].h, sm["mh"].v(u + 1))
                        if hc == 0:
                            s_.wait_ge(sm["my"].h, sm["my"].v(4 * tg - 4))
                        s_.activation(
                            h1b.rearrange("p (q c f) -> p q c f", q=2, c=16)[:, tg % 2, hc, :],
                            ph[hc % 2][:, :], AF.Gelu,
                            bias=sb1_sb[:, hc:hc + 1]).then_inc(sm["gh"].h, 1)

            @blk.vector
            def _(v):
                with v.Fori(0, 8) as tg:
                    for tt in range(4):
                        j = 4 * tg + tt
                        v.wait_ge(sm["my"].h, sm["my"].v(j + 1))
                        v.wait_ge(accst.h, accst0 + 16 * (j - 1))
                        v.tensor_copy(cv(yev, 1024, tt % 2), py[tt % 2][:, :]).then_inc(sm["ye"].h, 1)

            @blk.tensor
            def _(t):
                t.wait_ge(wle.h, wl1 + 32)
                t.wait_ge(sm["ldx"].h, sm["ldx"].v(32))
                with t.Fori(0, 4) as tg2:
                    for par in range(2):
                        tg = 2 * tg2 + par
                        t.wait_ge(xfld_g.h, xfld0 + 16 * (tg + 1))
                        for hc in range(16):
                            u = 16 * tg + hc
                            t.wait_ge(sm["gh"].h, sm["gh"].v(u - 1))
                            for kc in range(8):
                                ins = t.matmul(
                                    ph[hc % 2][:, :],
                                    sw1_sb[:, kc * 2048 + hc * 128:kc * 2048 + (hc + 1) * 128],
                                    xfs.rearrange("p (q c f) -> p q c f", q=2, c=8)[:, tg % 2, kc, :],
                                    start=(kc == 0), stop=(kc == 7))
                            ins.then_inc(sm["mh"].h, 1)
                        t.wait_ge(sm["gh"].h, sm["gh"].v(16 * (tg + 1)))
                        for tt in range(4):
                            j = 4 * tg + tt
                            t.wait_ge(sm["ye"].h, sm["ye"].v(j - 1))
                            for nd in range(2):
                                for kc in range(16):
                                    t.matmul(
                                        py[tt % 2][:, nd * 512:(nd + 1) * 512],
                                        h1b[:, par * 8192 + kc * 512 + tt * 128:
                                            par * 8192 + kc * 512 + (tt + 1) * 128],
                                        sw2_sb[:, kc * 1024 + nd * 512:kc * 1024 + (nd + 1) * 512],
                                        start=(kc == 0), stop=False)
                                ins = t.matmul(py[tt % 2][:, nd * 512:(nd + 1) * 512],
                                               g["ones"][0:1, 0:128],
                                               sb2_sb[0:1, nd * 512:(nd + 1) * 512],
                                               start=False, stop=True)
                            ins.then_inc(sm["my"].h, 1)
        totB = {"ldx": 32, "mh": 16 * 8, "gh": 16 * 8, "my": 32, "ye": 32}
    else:
        accst0 = accst.n
        totB = {}

    _scC = nc.enter_named_scope(cfg["name"] + "_C", False)[0]
    H_SL = 1024
    NSLAB = H // H_SL
    NPASS_h = 8 * NSLAB
    HCQ = H_SL // 128
    TGW = 512
    NTG_h = 4
    has_b = cfg.get("EB1") is not None
    KGP = 2 if has_b else 1
    esC = ExitStack()
    ew1_sb = esC.enter_context(nc.sbuf_tensor(nm("f_ew1"), [128, 2 * 8 * H_SL], BF16))
    ew2_sb = esC.enter_context(nc.sbuf_tensor(nm("f_ew2"), [128, 2 * HCQ * 1024], BF16))
    h1c = esC.enter_context(nc.sbuf_tensor(nm("f_h1c"), [128, 2 * HCQ * TGW], BF16))
    xfh = esC.enter_context(nc.sbuf_tensor(nm("f_xfh"), [128, 8 * 2048], BF16))
    accb = esC.enter_context(nc.sbuf_tensor(nm("f_accb"), [128, 16 * 1024], F32))
    xrd = esC.enter_context(nc.sbuf_tensor(nm("f_xrd"), [128, 2 * 1024], F32))
    junkd = esC.enter_context(nc.sbuf_tensor(nm("f_junkd"), [128, 2 * 1024], BF16))
    dstat = esC.enter_context(nc.sbuf_tensor(nm("f_dstat"), [128, 16], F32))
    eb1_sb = esC.enter_context(nc.sbuf_tensor(nm("f_eb1"), [128, 2 * 8], F32))
    eb2_sb = esC.enter_context(nc.sbuf_tensor(nm("f_eb2"), [1, 1024], F32))
    ew2l = pool.gsem("wl2")
    aldx_g = pool.gsem("aldx")
    xst_g = pool.gsem("xst2")
    wl2 = wle.n
    ew2l0 = ew2l.n
    xfld1 = xfld_g.n
    aldx0 = aldx_g.n
    xst0 = xst_g.n
    mh0 = totB.get("mh", 0)
    gh0 = totB.get("gh", 0)
    my0 = totB.get("my", 0)
    accv0 = 0
    ALD_H = 16 * 16 * (2 if has_b else 1)
    SY_H = NPASS_h + (8 if has_b else 0)
    with (
        esC,
        nc.psum_tensor(nm("c_ph0"), [128, TGW], F32) as cph0,
        nc.psum_tensor(nm("c_ph1"), [128, TGW], F32) as cph1,
        nc.psum_tensor(nm("c_py0"), [128, 1024], F32) as cpy0,
        nc.psum_tensor(nm("c_py1"), [128, 1024], F32) as cpy1,
        nc.Block() as blk,
    ):
        cph = (cph0, cph1)
        cpy = (cpy0, cpy1)
        EW1v = cfg["EW1"].rearrange("ee (c p) (s f) -> ee s p c f", p=128, s=NSLAB)
        EW2v = cfg["EW2"].rearrange("ee (s c p) f -> ee s p c f", p=128, s=NSLAB)
        ew1v = ew1_sb.rearrange("p (q c f) -> p q c f", q=2, c=8)
        ew2v = ew2_sb.rearrange("p (q c f) -> p q c f", q=2, c=HCQ)
        h1cv = h1c.rearrange("p (q c f) -> p q c f", q=2, c=HCQ)
        xfhv = xfh.rearrange("p (c g f) -> p c g f", c=8, g=NTG_h)
        wsb3 = wsb.rearrange("p (je u) -> p je u", u=1)

        @blk.gpsimd
        def _(gp):
            for th in range(2):
                with gp.Fori(0, 8) as e:
                    for sl in range(NSLAB):
                        P = th * NPASS_h + NSLAB * e + sl
                        buf = sl % 2
                        gp.wait_ge(sm["my"].h, sm["my"].v(my0 + 16 * (P - 1)))
                        gp.dma_start(out=ew1v[:, buf], in_=EW1v[e, sl]).then_inc(wle.h, 16)
                        if has_b:
                            gp.dma_start(
                                out=eb1_sb[:, buf * 8:buf * 8 + 8],
                                in_=cfg["EB1"].rearrange("ee pp (s c) -> ee s pp c",
                                                         s=NSLAB)[e, sl],
                            ).then_inc(wle.h, 16)
                        gp.dma_start(out=ew2v[:, buf], in_=EW2v[e, sl]).then_inc(ew2l.h, 16)
                        if has_b and sl == NSLAB - 1:
                            gp.dma_start(out=eb2_sb[:, :], in_=cfg["EB2"][e]).then_inc(ew2l.h, 16)
            gp.wait_ge(wle.h, wl2 + 16 * KGP * 2 * NPASS_h)
            gp.wait_ge(xst_g.h, xst0 + 16 * 32)
        wle.n += 16 * KGP * 2 * NPASS_h

        @blk.sync
        def _(sy):
            for th in range(2):
                sy.wait_ge(sm["mh"].h, sm["mh"].v(mh0 + 32 * NPASS_h * th))
                sy.dma_start(out=xfh.rearrange("p (c f) -> p c f", c=8),
                             in_=XFr[:, :, th * 2048:(th + 1) * 2048]).then_inc(xfld_g.h, 16)
                if has_b:
                    for j in range(16):
                        sy.wait_ge(accst.h, accst0 + 16 * (th * 16 + j + 1))
                        if th == 1:
                            sy.wait_ge(sm["dlnv"].h, sm["dlnv"].v(j + 1))
                        sy.dma_start(out=cv(accb, 1024, j),
                                     in_=ACC_T[th * 16 + j]).then_inc(aldx_g.h, 16)
                for j in range(16):
                    jg = th * 16 + j
                    if jg >= 2:
                        sy.wait_ge(xst_g.h, xst0 + 16 * (jg - 1))
                    sy.dma_start(out=cv(xrd, 1024, j % 2), in_=XIN_T[jg]).then_inc(aldx_g.h, 16)
        xfld_g.n += 32
        aldx_g.n += 2 * ALD_H
        ew2l.n += 16 * 2 * SY_H

        @blk.scalar
        def _(s_):
            for th in range(2):
                with s_.Fori(0, 8) as e:
                    for sl in range(NSLAB):
                        P = th * NPASS_h + NSLAB * e + sl
                        buf = sl % 2
                        with s_.Fori(0, NTG_h) as tg:
                            for hc in range(HCQ):
                                u = 32 * P + 8 * tg + hc
                                s_.wait_ge(sm["mh"].h, sm["mh"].v(mh0 + u + 1))
                                if hc == 0:
                                    s_.wait_ge(sm["my"].h,
                                               sm["my"].v(my0 + 16 * P + 4 * tg - 4))
                                if has_b:
                                    s_.activation(
                                        h1cv[:, tg % 2, hc, :], cph[hc % 2][:, :], AF.Gelu,
                                        bias=eb1_sb[:, buf * 8 + hc:buf * 8 + hc + 1],
                                    ).then_inc(sm["gh"].h, 1)
                                else:
                                    s_.activation(h1cv[:, tg % 2, hc, :], cph[hc % 2][:, :],
                                                  AF.Gelu).then_inc(sm["gh"].h, 1)
                for j in range(16):
                    jg = th * 16 + j
                    p2 = jg % 2
                    accj = cv(accb, 1024, j)
                    s_.wait_ge(sm["dr"].h, sm["dr"].v(jg + 1))
                    if jg >= 2:
                        s_.wait_ge(sm["dssq"].h, sm["dssq"].v(jg - 1))
                        s_.wait_ge(sm["dsrs"].h, sm["dsrs"].v(2 * jg - 2))
                        s_.wait_ge(sm["dvvar"].h, sm["dvvar"].v(jg - 1))
                    s_.activation(cv(junkd, 1024, p2), accj, AF.Square,
                                  accum_out=dstat[:, 12 + p2:13 + p2]).then_inc(sm["dssq"].h, 1)
                    s_.wait_ge(sm["dvvar"].h, sm["dvvar"].v(jg + 1))
                    s_.activation(dstat[:, 8 + p2:9 + p2], dstat[:, 6 + p2:7 + p2],
                                  AF.Ln, bias=g["epsl"]).then_inc(sm["dsrs"].h, 1)
                    s_.wait_ge(sm["dsrs"].h, sm["dsrs"].v(2 * jg + 1))
                    s_.activation(dstat[:, 10 + p2:11 + p2], dstat[:, 8 + p2:9 + p2],
                                  AF.Exp, scale=-0.5).then_inc(sm["dsrs"].h, 1)
                    s_.wait_ge(sm["dlnv"].h, sm["dlnv"].v(jg + 1))
                    s_.dma_start(out=XOUT_T[jg], in_=cv(xrd, 1024, p2)).then_inc(xst_g.h, 16)

        @blk.vector
        def _(v):
            for th in range(2):
                if has_b:
                    v.wait_ge(aldx_g.h, aldx0 + th * ALD_H + 16 * 16)
                else:
                    for q in range(4):
                        v.memset(cv(accb, 4096, q), 0.0)
                with v.Fori(0, 8) as e:
                    for sl in range(NSLAB):
                        P = th * NPASS_h + NSLAB * e + sl
                        with v.Fori(0, NTG_h) as tg:
                            for tt in range(4):
                                un = 16 * P + 4 * tg + tt
                                v.wait_ge(sm["my"].h, sm["my"].v(my0 + un + 1))
                                v.scalar_tensor_tensor(
                                    cv(accb, 1024, 4 * tg + tt), cpy[tt % 2][:, :],
                                    wsb3[:, (th * 16 + 4 * tg + tt) * 8 + e, :],
                                    cv(accb, 1024, 4 * tg + tt),
                                    ALU.mult, ALU.add).then_inc(sm["accv"].h, 1)
                for j in range(16):
                    jg = th * 16 + j
                    p2 = jg % 2
                    accj = cv(accb, 1024, j)
                    v.wait_ge(aldx_g.h, aldx0 + th * ALD_H
                              + (256 if has_b else 0) + 16 * (j + 1))
                    if jg >= 2:
                        v.wait_ge(xst_g.h, xst0 + 16 * (jg - 1))
                    v.tensor_add(accj, accj, cv(xrd, 1024, p2)).then_inc(sm["dr"].h, 1)
                    v.reduce_sum(dstat[:, p2:p2 + 1], accj, AX.X).then_inc(sm["ds3"].h, 1)
                    v.wait_ge(sm["ds3"].h, sm["ds3"].v(3 * jg + 1))
                    v.tensor_scalar(dstat[:, 2 + p2:3 + p2], dstat[:, p2:p2 + 1],
                                    -1.0 / D, None, ALU.mult).then_inc(sm["ds3"].h, 1)
                    v.wait_ge(sm["ds3"].h, sm["ds3"].v(3 * jg + 2))
                    v.tensor_mul(dstat[:, 4 + p2:5 + p2], dstat[:, 2 + p2:3 + p2],
                                 dstat[:, 2 + p2:3 + p2]).then_inc(sm["ds3"].h, 1)
                    v.wait_ge(sm["dssq"].h, sm["dssq"].v(jg + 1))
                    v.wait_ge(sm["ds3"].h, sm["ds3"].v(3 * jg + 3))
                    v.tensor_scalar(dstat[:, 6 + p2:7 + p2], dstat[:, 12 + p2:13 + p2],
                                    1.0 / D, dstat[:, 4 + p2:5 + p2],
                                    ALU.mult, ALU.subtract).then_inc(sm["dvvar"].h, 1)
                    v.wait_ge(sm["dsrs"].h, sm["dsrs"].v(2 * jg + 2))
                    v.tensor_scalar(cv(xrd, 1024, p2), accj,
                                    dstat[:, 2 + p2:3 + p2], dstat[:, 10 + p2:11 + p2],
                                    ALU.add, ALU.mult).then_inc(sm["dlnv"].h, 1)
            v.wait_ge(xst_g.h, xst0 + 16 * 32)
        xst_g.n += 16 * 32

        @blk.tensor
        def _(t):
            for th in range(2):
                t.wait_ge(xfld_g.h, xfld1 + 16 * (th + 1))
                with t.Fori(0, 8) as e:
                    for sl in range(NSLAB):
                        P = th * NPASS_h + NSLAB * e + sl
                        buf = sl % 2
                        t.wait_ge(wle.h, wl2 + 16 * KGP * (th * NPASS_h + NSLAB * e + sl + 1))
                        if has_b:
                            t.wait_ge(ew2l.h, ew2l0 + 16 * (th * SY_H + 3 * e + sl + 1
                                                            + (1 if sl == NSLAB - 1 else 0)))
                        else:
                            t.wait_ge(ew2l.h, ew2l0 + 16 * (th * SY_H + NSLAB * e + sl + 1))
                        with t.Fori(0, NTG_h // 2) as tg2:
                            for par in range(2):
                                tg = 2 * tg2 + par
                                for hc in range(HCQ):
                                    u = 32 * P + 8 * tg + hc
                                    t.wait_ge(sm["gh"].h, sm["gh"].v(gh0 + u - 1))
                                    for kc in range(8):
                                        ins = t.matmul(
                                            cph[hc % 2][:, :],
                                            ew1v[:, buf, kc, hc * 128:(hc + 1) * 128],
                                            xfhv[:, kc, tg, :],
                                            start=(kc == 0), stop=(kc == 7))
                                    ins.then_inc(sm["mh"].h, 1)
                                t.wait_ge(sm["gh"].h, sm["gh"].v(gh0 + 32 * P + 8 * tg + 8))
                                for tt in range(4):
                                    un = 16 * P + 4 * tg + tt
                                    t.wait_ge(sm["accv"].h, sm["accv"].v(accv0 + un - 1))
                                    for nd in range(2):
                                        for kc in range(HCQ):
                                            ins = t.matmul(
                                                cpy[tt % 2][:, nd * 512:(nd + 1) * 512],
                                                h1cv[:, par, kc, tt * 128:(tt + 1) * 128],
                                                ew2v[:, buf, kc, nd * 512:(nd + 1) * 512],
                                                start=(kc == 0),
                                                stop=(not (has_b and sl == NSLAB - 1)
                                                      and kc == HCQ - 1))
                                        if has_b and sl == NSLAB - 1:
                                            ins = t.matmul(
                                                cpy[tt % 2][:, nd * 512:(nd + 1) * 512],
                                                g["ones"][0:1, 0:128],
                                                eb2_sb[0:1, nd * 512:(nd + 1) * 512],
                                                start=False, stop=True)
                                    ins.then_inc(sm["my"].h, 1)
    nc.leave_named_scope(cfg["name"] + "_C", _scC, False)
    totC = {"mh": 64 * NPASS_h, "gh": 64 * NPASS_h, "my": 32 * NPASS_h,
            "accv": 32 * NPASS_h}
    esP.close()
    totD = {"dr": 32, "ds3": 96, "dssq": 32, "dvvar": 32, "dsrs": 64, "dlnv": 32}

    tot = {}
    for d in (totA, totB, totC, totD):
        for k, v_ in d.items():
            tot[k] = tot.get(k, 0) + v_
    pool.finish(tot)


def emit_attn_l1(nc, pool, g, cfg):
    sm = pool.take(["ql", "ttk", "etk", "ms", "rv", "se", "vm", "vsL", "pn",
                    "tp2", "ep", "mo", "oe", "radd", "mpj", "xld", "xst",
                    "ssq2", "srs", "vvar", "lnv", "vs3", "aotd", "aofd",
                    "aostp", "aoldp", "xrlp"])
    wl = pool.gsem("wl")
    aost = sm["aostp"]
    aold = sm["aoldp"]
    QS, KS, VS, XRES = cfg["QS"], cfg["KS"], cfg["VS"], cfg["XRES"]
    WO, BO, XOUT = cfg["WO"], cfg["BO"], cfg["XOUT"]
    AO = nc.dram_tensor(cfg["name"] + "_ao", [T, D], BF16, kind="Internal").ap()
    if cfg.get("dbg_ao"):
        AODBG = nc.dram_tensor(cfg["name"] + "_aodbg", [T, D], F32, kind="ExternalOutput").ap()
    NG = cfg.get("ng", T * 16 // 128)
    NGQ = NG // 4
    wl0, aost0, aold0 = wl.n, 0, 0
    Q2 = QS.rearrange("t (h d) -> (t h) d", d=64).rearrange("(i j p) d -> i p j d", j=4, p=128)
    K2 = KS.rearrange("t (h d) -> (t h) d", d=64).rearrange("(i j p) d -> i p j d", j=4, p=128)
    V2 = VS.rearrange("t (h d) -> (t h) d", d=64).rearrange("(i j p) d -> i p j d", j=4, p=128)
    AO2 = AO.rearrange("t (h d) -> (t h) d", d=64).rearrange("(i j p) d -> i p j d", j=4, p=128)
    AOT = AO.rearrange("(t p) d -> t p d", p=128)
    XRT = XRES.rearrange("(t p) d -> t p d", p=128)
    XOT = XOUT.rearrange("(t p) d -> t p d", p=128)

    es = ExitStack()
    qg = es.enter_context(nc.sbuf_tensor(nm("m_qg"), [128, 8 * 256], BF16))
    kg = es.enter_context(nc.sbuf_tensor(nm("m_kg"), [128, 8 * 256], BF16))
    vg = es.enter_context(nc.sbuf_tensor(nm("m_vg"), [128, 8 * 256], BF16))
    tkq = es.enter_context(nc.sbuf_tensor(nm("m_tkq"), [64, 8 * 1024], BF16))
    Ef = es.enter_context(nc.sbuf_tensor(nm("m_Ef"), [128, 8 * 512], F32))
    Em = es.enter_context(nc.sbuf_tensor(nm("m_Em"), [128, 8 * 512], BF16))
    PnT = es.enter_context(nc.sbuf_tensor(nm("m_PnT"), [128, 8 * 512], BF16))
    aosb = es.enter_context(nc.sbuf_tensor(nm("m_ao"), [128, 8 * 256], BF16))
    msksb = es.enter_context(nc.sbuf_tensor(nm("m_msk"), [128, 512], F32))
    gstat = es.enter_context(nc.sbuf_tensor(nm("m_gs"), [128, 128], F32))
    wo_sb = es.enter_context(nc.sbuf_tensor(nm("m_wo"), [128, 8 * 1024], BF16))
    bo_sb = es.enter_context(nc.sbuf_tensor(nm("m_bo"), [1, 1024], F32))
    aot = es.enter_context(nc.sbuf_tensor(nm("m_aot"), [128, 2 * 1024], BF16))
    afm = es.enter_context(nc.sbuf_tensor(nm("m_afm"), [128, 2 * 1024], BF16))
    xrt = es.enter_context(nc.sbuf_tensor(nm("m_xrt"), [128, 2 * 1024], F32))
    rt = es.enter_context(nc.sbuf_tensor(nm("m_rt"), [128, 2 * 1024], F32))
    ob = es.enter_context(nc.sbuf_tensor(nm("m_ob"), [128, 2 * 1024], F32))
    junkP = es.enter_context(nc.sbuf_tensor(nm("m_junk"), [128, 2 * 1024], BF16))
    pstat = es.enter_context(nc.sbuf_tensor(nm("m_ps"), [128, 16], F32))
    with (
        es,
        nc.psum_tensor(nm("m_pS0"), [128, 512], F32) as pS0,
        nc.psum_tensor(nm("m_pS1"), [128, 512], F32) as pS1,
        nc.psum_tensor(nm("m_pS2"), [128, 512], F32) as pS2,
        nc.psum_tensor(nm("m_pS3"), [128, 512], F32) as pS3,
        nc.psum_tensor(nm("m_pTK0"), [64, 1024], BF16) as pTK0,
        nc.psum_tensor(nm("m_pTK1"), [64, 1024], BF16) as pTK1,
        nc.psum_tensor(nm("m_pPO0"), [128, 512], F32) as pPO0,
        nc.psum_tensor(nm("m_pPO1"), [128, 512], F32) as pPO1,
        nc.Block() as blk,
    ):
        pS = (pS0, pS1, pS2, pS3)
        pTK = (pTK0, pTK1)
        pP = (pPO0[:, 0:256].bitcast(BF16), pPO1[:, 0:256].bitcast(BF16))
        pO = (pPO0[:, 256:512], pPO1[:, 256:512])

        @blk.gpsimd
        def _(gp):
            gp.dma_start(out=wo_sb.rearrange("p (c f) -> p c f", c=8),
                         in_=WO.rearrange("(c p) f -> p c f", p=128)).then_inc(wl.h, 16)
            with gp.Fori(0, NGQ // 8) as i4:
              for par in range(8):
                i = 8 * i4 + par
                gp.wait_ge(sm["oe"].h, sm["oe"].v(i + 1))
                gp.dma_start(out=AO2[i],
                             in_=cv(aosb, 256, par).rearrange("p (j d) -> p j d", j=4),
                             ).then_inc(aost.h, 16)
            gp.wait_ge(aost.h, aost.v(aost0 + 16 * NGQ))
            gp.wait_ge(wl.h, wl0 + 16)
        wl.n += 16

        @blk.sync
        def _(sy):
            sy.dma_start(out=bo_sb[:, :], in_=BO).then_inc(sm["xld"].h, 16)
            sy.dma_start(out=msksb[:, :], in_=cfg["MASK"]).then_inc(sm["xld"].h, 16)
            with sy.Fori(0, NGQ // 8) as i4:
              for par in range(8):
                i = 8 * i4 + par
                sy.wait_ge(sm["mo"].h, sm["mo"].v(i - 7))
                sy.dma_start(out=cv(qg, 256, par).rearrange("p (j d) -> p j d", j=4),
                             in_=Q2[i]).then_inc(sm["ql"].h, 16)
                sy.dma_start(out=cv(kg, 256, par).rearrange("p (j d) -> p j d", j=4),
                             in_=K2[i]).then_inc(sm["ql"].h, 16)
                sy.dma_start(out=cv(vg, 256, par).rearrange("p (j d) -> p j d", j=4),
                             in_=V2[i]).then_inc(sm["ql"].h, 16)


        @blk.scalar
        def _(s_):
            with s_.Fori(0, NGQ // 8) as i4:
              for par in range(8):
                i = 8 * i4 + par
                gb = par * 16
                s_.wait_ge(sm["rv"].h, sm["rv"].v(i + 1))
                s_.wait_ge(sm["vm"].h, sm["vm"].v(i - 7))
                s_.activation(cv(Ef, 512, par), pS[par % 4][:, :], AF.Exp,
                              bias=gstat[:, gb:gb + 1]).then_inc(sm["se"].h, 1)

        @blk.vector
        def _(v):
            with v.Fori(0, NGQ // 8) as i4:
              for par in range(8):
                i = 8 * i4 + par
                gb = par * 16
                v.wait_ge(sm["ttk"].h, sm["ttk"].v(i + 1))
                v.wait_ge(sm["ms"].h, sm["ms"].v(4 * (i - 7)))
                v.tensor_copy(cv(tkq, 1024, par), pTK[par % 2][:, :]).then_inc(sm["etk"].h, 1)
                v.wait_ge(sm["ms"].h, sm["ms"].v(4 * (i + 1)))
                v.wait_ge(sm["se"].h, sm["se"].v(i - 7))
                v.wait_ge(sm["oe"].h, sm["oe"].v(i - 7))
                v.reduce_max(gstat[:, gb:gb + 1], pS[par % 4][:, :], AX.X,
                             negate=True).then_inc(sm["rv"].h, 1)
                v.wait_ge(sm["se"].h, sm["se"].v(i + 1))
                v.wait_ge(sm["tp2"].h, sm["tp2"].v(i - 7))
                v.scalar_tensor_tensor(cv(Em, 512, par), cv(Ef, 512, par), 1.0,
                                       msksb[:, :], ALU.mult, ALU.mult).then_inc(sm["vm"].h, 1)
                for j in range(4):
                    ins = v.reduce_sum(gstat[:, gb + 1 + j:gb + 2 + j],
                                       Em[:, par * 512 + j * 128:par * 512 + (j + 1) * 128],
                                       AX.X)
                ins.then_inc(sm["vsL"].h, 1)
                v.wait_ge(sm["vsL"].h, sm["vsL"].v(2 * i + 1))
                v.reciprocal(gstat[:, gb + 5:gb + 9],
                             gstat[:, gb + 1:gb + 5]).then_inc(sm["vsL"].h, 1)
                v.wait_ge(sm["tp2"].h, sm["tp2"].v(i + 1))
                v.wait_ge(sm["mo"].h, sm["mo"].v(i - 7))
                v.tensor_copy(cv(PnT, 512, par), pP[par % 2][:, :]).then_inc(sm["ep"].h, 1)
                v.wait_ge(sm["vsL"].h, sm["vsL"].v(2 * i + 2))
                v.wait_ge(sm["mo"].h, sm["mo"].v(i + 1))
                v.wait_ge(aost.h, aost.v(aost0 + 16 * (i - 7)))
                for j in range(4):
                    ins = v.tensor_scalar(
                        aosb[:, par * 256 + j * 64:par * 256 + (j + 1) * 64],
                        pO[par % 2][:, j * 64:(j + 1) * 64],
                        gstat[:, gb + 5 + j:gb + 6 + j], None, ALU.mult)
                ins.then_inc(sm["oe"].h, 1)

        @blk.tensor
        def _(t):
            t.wait_ge(sm["xld"].h, sm["xld"].v(32))
            t.wait_ge(wl.h, wl0 + 16)
            with t.Fori(0, NGQ // 8) as i4:
              for par in range(8):
                i = 8 * i4 + par
                t.wait_ge(sm["ql"].h, sm["ql"].v(48 * (i + 1)))
                t.wait_ge(sm["etk"].h, sm["etk"].v(i - 1))
                for j in range(4):
                    t.transpose(pTK[par % 2][0:64, j * 128:(j + 1) * 128],
                                qg[:, par * 256 + j * 64:par * 256 + (j + 1) * 64],
                                g["idn"][:, :])
                for j in range(4):
                    ins = t.transpose(pTK[par % 2][0:64, 512 + j * 128:512 + (j + 1) * 128],
                                      kg[:, par * 256 + j * 64:par * 256 + (j + 1) * 64],
                                      g["idn"][:, :])
                ins.then_inc(sm["ttk"].h, 1)
                t.wait_ge(sm["etk"].h, sm["etk"].v(i + 1))
                t.wait_ge(sm["se"].h, sm["se"].v(i - 3))
                for j in range(4):
                    t.matmul(pS[par % 4][:, j * 128:(j + 1) * 128],
                             tkq[0:64, par * 1024 + j * 128:par * 1024 + (j + 1) * 128],
                             tkq[0:64, par * 1024 + 512 + j * 128:par * 1024 + 512 + (j + 1) * 128],
                             start=True, stop=True).then_inc(sm["ms"].h, 1)
                t.wait_ge(sm["vm"].h, sm["vm"].v(i + 1))
                t.wait_ge(sm["ep"].h, sm["ep"].v(i - 1))
                for j in range(4):
                    ins = t.transpose(pP[par % 2][:, j * 128:(j + 1) * 128],
                                      Em[:, par * 512 + j * 128:par * 512 + (j + 1) * 128],
                                      g["idn"][:, :])
                ins.then_inc(sm["tp2"].h, 1)
                t.wait_ge(sm["ep"].h, sm["ep"].v(i + 1))
                t.wait_ge(sm["oe"].h, sm["oe"].v(i - 1))
                for j in range(4):
                    ins = t.matmul(pO[par % 2][:, j * 64:(j + 1) * 64],
                                   PnT[:, par * 512 + j * 128:par * 512 + (j + 1) * 128],
                                   vg[:, par * 256 + j * 64:par * 256 + (j + 1) * 64],
                                   start=True, stop=True)
                ins.then_inc(sm["mo"].h, 1)
    totg = {"ql": 48 * NGQ, "ttk": NGQ, "etk": NGQ, "ms": 4 * NGQ,
            "rv": NGQ, "se": NGQ, "vm": NGQ, "vsL": 2 * NGQ,
            "tp2": NGQ, "ep": NGQ, "mo": NGQ, "oe": NGQ,
            "aostp": 16 * NGQ, "xld": 32}
    if cfg.get("skip_proj"):
        gdmp = pool.gsem("dump")
        with nc.Block() as blkd:
            @blkd.gpsimd
            def _(gp):
                gp.dma_start(out=AODBG, in_=AO).then_inc(gdmp.h, 16)
                gdmp.n += 16
                gp.wait_ge(gdmp.h, gdmp.n)
        pool.finish(totg)
        return
    if cfg.get("dbg_ao"):
        gdmp = pool.gsem("dump")
        with nc.Block() as blkd:
            @blkd.gpsimd
            def _(gp):
                gp.dma_start(out=AODBG, in_=AO).then_inc(gdmp.h, 16)
                gdmp.n += 16
                gp.wait_ge(gdmp.h, gdmp.n)
    with (
        nc.psum_tensor(nm("m_po0"), [128, 1024], F32) as po0,
        nc.psum_tensor(nm("m_po1"), [128, 1024], F32) as po1,
        nc.psum_tensor(nm("m_pt0"), [128, 1024], BF16) as pt0,
        nc.psum_tensor(nm("m_pt1"), [128, 1024], BF16) as pt1,
        nc.Block() as blk2,
    ):
        po = (po0, po1)
        pt2 = (pt0, pt1)

        @blk2.sync
        def _(sy):
            for tt in range(32):
                p2 = tt % 2
                if tt >= 2:
                    sy.wait_ge(sm["aotd"].h, sm["aotd"].v(tt - 1))
                sy.dma_start(out=cv(aot, 1024, p2), in_=AOT[tt]).then_inc(aold.h, 16)
                if tt >= 2:
                    sy.wait_ge(sm["radd"].h, sm["radd"].v(tt - 1))
                sy.dma_start(out=cv(xrt, 1024, p2), in_=XRT[tt]).then_inc(sm["xrlp"].h, 16)
                sy.wait_ge(sm["lnv"].h, sm["lnv"].v(tt + 1))
                sy.dma_start(out=XOT[tt], in_=cv(ob, 1024, p2)).then_inc(sm["xst"].h, 16)
            sy.wait_ge(sm["xst"].h, sm["xst"].v(16 * 32))

        @blk2.scalar
        def _(s_):
            for tt in range(32):
                p2 = tt % 2
                s_.wait_ge(sm["radd"].h, sm["radd"].v(tt + 1))
                if tt >= 2:
                    s_.wait_ge(sm["ssq2"].h, sm["ssq2"].v(tt - 1))
                    s_.wait_ge(sm["srs"].h, sm["srs"].v(2 * tt - 2))
                    s_.wait_ge(sm["vvar"].h, sm["vvar"].v(tt - 1))
                s_.activation(cv(junkP, 1024, p2), cv(rt, 1024, p2), AF.Square,
                              accum_out=pstat[:, 12 + p2:13 + p2]).then_inc(sm["ssq2"].h, 1)
                s_.wait_ge(sm["vvar"].h, sm["vvar"].v(tt + 1))
                s_.activation(pstat[:, 8 + p2:9 + p2], pstat[:, 6 + p2:7 + p2],
                              AF.Ln, bias=g["epsl"]).then_inc(sm["srs"].h, 1)
                s_.wait_ge(sm["srs"].h, sm["srs"].v(2 * tt + 1))
                s_.activation(pstat[:, 10 + p2:11 + p2], pstat[:, 8 + p2:9 + p2],
                              AF.Exp, scale=-0.5).then_inc(sm["srs"].h, 1)

        @blk2.vector
        def _(v):
            for tt in range(32):
                p2 = tt % 2
                v.wait_ge(sm["aotd"].h, sm["aotd"].v(tt + 1))
                if tt >= 2:
                    v.wait_ge(sm["mpj"].h, sm["mpj"].v(2 * tt - 2))
                v.tensor_copy(cv(afm, 1024, p2), pt2[p2][:, :]).then_inc(sm["aofd"].h, 1)
                v.wait_ge(sm["mpj"].h, sm["mpj"].v(2 * tt + 2))
                v.wait_ge(aold.h, aold.v(aold0 + 16 * (tt + 1)))
                v.wait_ge(sm["xrlp"].h, sm["xrlp"].v(16 * (tt + 1)))
                if tt >= 2:
                    v.wait_ge(sm["xst"].h, sm["xst"].v(16 * (tt - 1)))
                v.tensor_add(cv(rt, 1024, p2), po[p2][:, :],
                             cv(xrt, 1024, p2)).then_inc(sm["radd"].h, 1)
                v.wait_ge(sm["radd"].h, sm["radd"].v(tt + 1))
                v.reduce_sum(pstat[:, p2:p2 + 1], cv(rt, 1024, p2), AX.X).then_inc(sm["vs3"].h, 1)
                v.wait_ge(sm["vs3"].h, sm["vs3"].v(3 * tt + 1))
                v.tensor_scalar(pstat[:, 2 + p2:3 + p2], pstat[:, p2:p2 + 1],
                                -1.0 / D, None, ALU.mult).then_inc(sm["vs3"].h, 1)
                v.wait_ge(sm["vs3"].h, sm["vs3"].v(3 * tt + 2))
                v.tensor_mul(pstat[:, 4 + p2:5 + p2], pstat[:, 2 + p2:3 + p2],
                             pstat[:, 2 + p2:3 + p2]).then_inc(sm["vs3"].h, 1)
                v.wait_ge(sm["ssq2"].h, sm["ssq2"].v(tt + 1))
                v.wait_ge(sm["vs3"].h, sm["vs3"].v(3 * tt + 3))
                v.tensor_scalar(pstat[:, 6 + p2:7 + p2], pstat[:, 12 + p2:13 + p2],
                                1.0 / D, pstat[:, 4 + p2:5 + p2],
                                ALU.mult, ALU.subtract).then_inc(sm["vvar"].h, 1)
                v.wait_ge(sm["srs"].h, sm["srs"].v(2 * tt + 2))
                v.tensor_scalar(cv(ob, 1024, p2), cv(rt, 1024, p2),
                                pstat[:, 2 + p2:3 + p2], pstat[:, 10 + p2:11 + p2],
                                ALU.add, ALU.mult).then_inc(sm["lnv"].h, 1)

        @blk2.tensor
        def _(t):
            for tt in range(32):
                p2 = tt % 2
                t.wait_ge(aold.h, aold.v(aold0 + 16 * (tt + 1)))
                if tt >= 2:
                    t.wait_ge(sm["aofd"].h, sm["aofd"].v(tt - 1))
                for c in range(8):
                    ins = t.transpose(cv(pt2[p2], 128, c),
                                      aot[:, p2 * 1024 + c * 128:p2 * 1024 + (c + 1) * 128],
                                      g["idn"][:, :])
                ins.then_inc(sm["aotd"].h, 1)
                t.wait_ge(sm["aofd"].h, sm["aofd"].v(tt + 1))
                if tt >= 2:
                    t.wait_ge(sm["radd"].h, sm["radd"].v(tt - 1))
                for nd in range(2):
                    for kc in range(8):
                        t.matmul(po[p2][:, nd * 512:(nd + 1) * 512],
                                 afm[:, p2 * 1024 + kc * 128:p2 * 1024 + (kc + 1) * 128],
                                 wo_sb[:, kc * 1024 + nd * 512:kc * 1024 + (nd + 1) * 512],
                                 start=(kc == 0), stop=False)
                    t.matmul(po[p2][:, nd * 512:(nd + 1) * 512],
                             g["ones"][0:1, 0:128], bo_sb[0:1, nd * 512:(nd + 1) * 512],
                             start=False, stop=True).then_inc(sm["mpj"].h, 1)
    tot2 = dict(totg)
    tot2.update({"radd": 32, "mpj": 64, "xst": 16 * 32,
                 "ssq2": 32, "srs": 64, "vvar": 32, "lnv": 32, "vs3": 96,
                 "aotd": 32, "aofd": 32, "aoldp": 16 * 32, "xrlp": 16 * 32})
    pool.finish(tot2)


from contextlib import ExitStack


def _new_prog():
    nc = bacc.Bacc("TRN2", target_bir_lowering=False)
    ctx = ExitStack()
    pool = Pool(nc, ctx)
    return nc, ctx, pool


def _global_prologue(nc, ctx, pool):
    IDENT = nc.dram_tensor("c_ident", [128, 128], F32, kind="ExternalInput").ap()
    pool.seed_src = IDENT[0:1, 0:16]
    ONES = nc.dram_tensor("c_ones1", [1, 512], F32, kind="ExternalInput").ap()
    idn = ctx.enter_context(nc.sbuf_tensor(nm("g_idn"), [128, 128], BF16))
    ones = ctx.enter_context(nc.sbuf_tensor(nm("g_ones"), [1, 512], F32))
    eps = ctx.enter_context(nc.sbuf_tensor(nm("g_eps"), [128, 2], F32))
    sp = pool.take(["pro"])["pro"]
    gw = pool.gsem("wl")
    gnames = ["wl", "qst", "vst", "xfst", "accst", "xfld", "aldx", "xst2", "dump"]
    with nc.Block() as blk:
        @blk.gpsimd
        def _(gp):
            gp.dma_start(out=idn[:, :], in_=IDENT).then_inc(gw.h, 16)
            gw.n += 16
            for nm in gnames:
                gs = pool.gsem(nm)
                pool.seed_dma(gp, gs, 4)
            for nm in gnames:
                gs = pool.gsem(nm)
                gp.wait_ge(gs.h, gs.n)
            gp.wait_ge(gw.h, gw.n)
        @blk.sync
        def _(sy):
            sy.dma_start(out=ones[:, :], in_=ONES).then_inc(sp.h, 16)
            sy.wait_ge(sp.h, sp.v(16))
            for s_ in pool.sems:
                sy.sem_inc(s_, 64)
        @blk.vector
        def _(v):
            v.memset(eps[:, 0:1], EPS_RMS)
            v.memset(eps[:, 1:2], EPS_LN)
    pool.counts = [c + 64 for c in pool.counts]
    pool.finish({"pro": 16})
    return {"idn": idn, "ones": ones, "epsr": eps[:, 0:1], "epsl": eps[:, 1:2]}


def _qkv_cfg(nc, layer, xs_ap=None):
    l = f"l{layer}"
    kind = "Internal"
    tm = (layer == 1)
    cfg = dict(
        rms=(layer == 0),
        tm_qk=tm,
        qscale=0.125,
        XS=xs_ap if xs_ap is not None else
            nc.dram_tensor(f"{l}_xs_in", [T, D], F32, kind="ExternalInput").ap(),
        WQ=nc.dram_tensor(f"{l}_wq", [D, D], F32, kind="ExternalInput").ap(),
        WK=nc.dram_tensor(f"{l}_wk", [D, D], F32, kind="ExternalInput").ap(),
        WV=nc.dram_tensor(f"{l}_wv", [D, D], F32, kind="ExternalInput").ap(),
        GAM=nc.dram_tensor(f"{l}_gam", [128, 8], F32, kind="ExternalInput").ap(),
        QS=nc.dram_tensor(f"{l}_qsc", [T, D] if tm else [D, T], BF16, kind=kind).ap(),
        KS=nc.dram_tensor(f"{l}_ksc", [T, D] if tm else [D, T], BF16, kind=kind).ap(),
        VS=nc.dram_tensor(f"{l}_vsc", [T, D], BF16, kind=kind).ap(),
    )
    return cfg


def _dump_f32(nc, pool, pairs):
    gd = pool.gsem("dump")
    with nc.Block() as blk:
        @blk.gpsimd
        def _(gp):
            for name, src, shape in pairs:
                dst = nc.dram_tensor(name, list(shape), F32, kind="ExternalOutput").ap()
                gp.dma_start(out=dst, in_=src).then_inc(gd.h, 16)
                gd.n += 16
            gp.wait_ge(gd.h, gd.n)


def _attn_cfg(nc, layer, qkv_cfg, xout_kind="Internal"):
    l = f"l{layer}"
    cfg = dict(
        name=f"{l}_attn",
        QS=qkv_cfg["QS"], KS=qkv_cfg["KS"], VS=qkv_cfg["VS"],
        XRES=qkv_cfg["XS"],
        WO=nc.dram_tensor(f"{l}_wo", [D, D], F32, kind="ExternalInput").ap(),
        BO=nc.dram_tensor(f"{l}_bo", [1, D], F32, kind="ExternalInput").ap(),
        XOUT=nc.dram_tensor(f"{l}_xattn_out", [T, D], F32, kind=xout_kind).ap(),
    )
    if layer == 1:
        cfg["MASK"] = nc.dram_tensor("c_mask", [128, 512], F32, kind="ExternalInput").ap()
    return cfg


def build_attn_test(layer=0):
    nc, ctx, pool = _new_prog()
    g = _global_prologue(nc, ctx, pool)
    qcfg = _qkv_cfg(nc, layer)
    emit_qkv(nc, pool, g, qcfg)
    acfg = _attn_cfg(nc, layer, qcfg, xout_kind="ExternalOutput")
    if layer == 0:
        emit_attn(nc, pool, g, acfg)
    else:
        emit_attn_l1(nc, pool, g, acfg)
    nc.finalize()
    return nc, ctx


def _ffn_cfg(nc, layer, xin_ap, xout_kind="Internal"):
    l = f"l{layer}"
    if layer == 0:
        return dict(
            layer=0, H=2048, shared=True, name=f"{l}_ffn",
            XIN=xin_ap,
            XOUT=nc.dram_tensor(f"{l}_xffn_out", [T, D], F32, kind=xout_kind).ap(),
            RW=nc.dram_tensor(f"{l}_rw", [D, 8], F32, kind="ExternalInput").ap(),
            RB=nc.dram_tensor(f"{l}_rb", [1, 8], F32, kind="ExternalInput").ap(),
            SW1=nc.dram_tensor(f"{l}_sw1", [D, 2048], F32, kind="ExternalInput").ap(),
            SB1=nc.dram_tensor(f"{l}_sb1", [128, 16], F32, kind="ExternalInput").ap(),
            SW2=nc.dram_tensor(f"{l}_sw2", [2048, D], F32, kind="ExternalInput").ap(),
            SB2=nc.dram_tensor(f"{l}_sb2", [1, D], F32, kind="ExternalInput").ap(),
            EW1=nc.dram_tensor(f"{l}_ew1", [8, D, 2048], F32, kind="ExternalInput").ap(),
            EB1=nc.dram_tensor(f"{l}_eb1", [8, 128, 16], F32, kind="ExternalInput").ap(),
            EW2=nc.dram_tensor(f"{l}_ew2", [8, 2048, D], F32, kind="ExternalInput").ap(),
            EB2=nc.dram_tensor(f"{l}_eb2", [8, 1, D], F32, kind="ExternalInput").ap(),
        )
    return dict(
        layer=1, H=4096, shared=False, name=f"{l}_ffn",
        XIN=xin_ap,
        XOUT=nc.dram_tensor(f"{l}_xffn_out", [T, D], F32, kind=xout_kind).ap(),
        RW=nc.dram_tensor(f"{l}_rw", [D, 8], F32, kind="ExternalInput").ap(),
        RB=None,
        EW1=nc.dram_tensor(f"{l}_ew1", [8, D, 4096], F32, kind="ExternalInput").ap(),
        EB1=None,
        EW2=nc.dram_tensor(f"{l}_ew2", [8, 4096, D], F32, kind="ExternalInput").ap(),
        EB2=None,
    )


def build_ffn_test(layer=0, debug=False):
    nc, ctx, pool = _new_prog()
    g = _global_prologue(nc, ctx, pool)
    xin = nc.dram_tensor("ffn_xin", [T, D], F32, kind="ExternalInput").ap()
    cfg = _ffn_cfg(nc, layer, xin, xout_kind="ExternalOutput")
    cfg["debug"] = debug
    emit_ffn(nc, pool, g, cfg)
    nc.finalize()
    return nc, ctx


def build_qkv_test(layer=0, ntg=8):
    nc, ctx, pool = _new_prog()
    g = _global_prologue(nc, ctx, pool)
    cfg = _qkv_cfg(nc, layer)
    cfg["ntg"] = ntg
    emit_qkv(nc, pool, g, cfg)
    l = f"l{layer}"
    _dump_f32(nc, pool, [(f"{l}_qs_o", cfg["QS"], (D, T)),
                         (f"{l}_ks_o", cfg["KS"], (D, T)),
                         (f"{l}_vs_o", cfg["VS"], (T, D))])
    nc.finalize()
    return nc, ctx


def build_full(nstages=4):
    nc, ctx, pool = _new_prog()
    g = _global_prologue(nc, ctx, pool)
    x0 = nc.dram_tensor("x0", [T, D], F32, kind="ExternalInput").ap()
    q0 = _qkv_cfg(nc, 0, xs_ap=x0)
    with nc.named_scope("qkv0"):
        emit_qkv(nc, pool, g, q0)
    a0 = _attn_cfg(nc, 0, q0, xout_kind="ExternalOutput" if nstages == 1 else "Internal")
    with nc.named_scope("attn0"):
        emit_attn(nc, pool, g, a0)
    out = "l0_xattn_out"
    if nstages >= 2:
        f0 = _ffn_cfg(nc, 0, a0["XOUT"], xout_kind="ExternalOutput" if nstages == 2 else "Internal")
        with nc.named_scope("ffn0"):
            emit_ffn(nc, pool, g, f0)
        out = "l0_xffn_out"
    if nstages >= 3:
        q1 = _qkv_cfg(nc, 1, xs_ap=f0["XOUT"])
        with nc.named_scope("qkv1"):
            emit_qkv(nc, pool, g, q1)
        a1 = _attn_cfg(nc, 1, q1, xout_kind="ExternalOutput" if nstages == 3 else "Internal")
        with nc.named_scope("attn1"):
            emit_attn_l1(nc, pool, g, a1)
        out = "l1_xattn_out"
    if nstages >= 4:
        f1 = _ffn_cfg(nc, 1, a1["XOUT"], xout_kind="ExternalOutput")
        with nc.named_scope("ffn1"):
            emit_ffn(nc, pool, g, f1)
        out = "l1_xffn_out"
    nc.finalize()
    return nc, ctx, out


def _in_maps(inputs):
    w = _prep_weights(inputs)
    xs = _split_x(inputs["L0_attn_x"])
    base = {
        "c_ident": w["ident"], "c_ones1": w["ones1"],
        "l0_wq": w["l0_wq"], "l0_wk": w["l0_wk"], "l0_wv": w["l0_wv"],
        "l0_gam": w["l0_gam"], "l0_wo": w["l0_wo"], "l0_bo": w["l0_bo"],
        "l0_rw": w["l0_rw"], "l0_rb": w["l0_rb"],
        "l0_sw1": w["l0_sw1"], "l0_sb1": w["l0_sb1"],
        "l0_sw2": w["l0_sw2"], "l0_sb2": w["l0_sb2"],
        "l0_ew1": w["l0_ew1"], "l0_eb1": w["l0_eb1"],
        "l0_ew2": w["l0_ew2"], "l0_eb2": w["l0_eb2"],
        "l1_wq": w["l1_wq"], "l1_wk": w["l1_wk"], "l1_wv": w["l1_wv"],
        "l1_gam": w["l1_gam"], "l1_wo": w["l1_wo"], "l1_bo": w["l1_bo"],
        "l1_rw": w["l1_rw"], "l1_ew1": w["l1_ew1"], "l1_ew2": w["l1_ew2"],
        "c_mask": w["mask"],
    }
    return [dict(base, x0=xs[c]) for c in range(NCORES)]


_CACHE = {}


def _split_x(x):
    x = np.ascontiguousarray(np.asarray(x, np.float32))
    return [np.ascontiguousarray(x[c * SPC:(c + 1) * SPC].reshape(T, D)) for c in range(NCORES)]


def kernel(**inputs):
    if "nc" not in _CACHE:
        _CACHE["nc"], _CACHE["ctx"], _CACHE["out"] = build_full()
    nc = _CACHE["nc"]
    in_maps = _in_maps(inputs)
    res = run_bass_kernel_spmd(nc, in_maps, core_ids=list(range(NCORES)))
    out = np.concatenate([r[_CACHE["out"]].reshape(SPC, S, D) for r in res.results], axis=0)
    return out
